# version 06
# baseline (speedup 1.0000x reference)
import contextlib
import numpy as np
import ml_dtypes
import concourse.bass as bass
import concourse.mybir as mybir
from concourse.alu_op_type import AluOpType as OP
from concourse.bass_utils import run_bass_kernel_spmd

F32 = mybir.dt.float32
BF16 = mybir.dt.bfloat16
I32 = mybir.dt.int32
U8 = mybir.dt.uint8
U32 = mybir.dt.uint32
AF = mybir.ActivationFunctionType
AX = mybir.AxisListType
f32 = np.float32
NC = 8
T = 2048
D = 1024
NT = T // 128
NCH = D // 128

ENGS = ("pe", "act", "dve", "pool", "sp")


class Prog:

    def __init__(self, name="prog"):
        self.nc = bass.Bass("TRN2", target_bir_lowering=False, debug=False)
        self.name = name
        self.stack = contextlib.ExitStack()
        self.ops = []
        self.last_w = {}
        self.readers = {}
        self._eng_cnt = {e: 0 for e in ENGS}
        self._sems = {}
        self._dma_cnt = {}
        self.t = {}
        self._done = False

    def inp(self, name, shape, dtype=F32):
        ap = self.nc.dram_tensor(name, list(shape), dtype, kind="ExternalInput").ap()
        self.t[name] = ap
        return ap

    def outp(self, name, shape, dtype=F32):
        ap = self.nc.dram_tensor(name, list(shape), dtype, kind="ExternalOutput").ap()
        self.t[name] = ap
        return ap

    def dscratch(self, name, shape, dtype=F32):
        ap = self.nc.dram_tensor(name, list(shape), dtype).ap()
        self.t[name] = ap
        return ap

    def sb(self, name, shape, dtype=F32):
        h = self.stack.enter_context(self.nc.sbuf_tensor(name, list(shape), dtype))
        self.t[name] = h
        return h

    def ps(self, name, shape, dtype=F32):
        h = self.stack.enter_context(self.nc.psum_tensor(name, list(shape), dtype))
        self.t[name] = h
        return h

    def _record(self, eng, fn, r, w, dma, tag):
        i = len(self.ops)
        deps = set()
        for k in r:
            j = self.last_w.get(k)
            if j is not None:
                deps.add(j)
        for k in w:
            j = self.last_w.get(k)
            if j is not None:
                deps.add(j)
            deps.update(self.readers.get(k, ()))
        deps.discard(i)
        for k in r:
            self.readers.setdefault(k, []).append(i)
        for k in w:
            self.last_w[k] = i
            self.readers[k] = []
        if dma:
            skey = "d:" + eng + ":" + "|".join(sorted(w))
            n = self._dma_cnt.get(skey, 0)
            self._dma_cnt[skey] = n + 1
            val = 16 * (n + 1)
        else:
            skey = "c:" + eng
            n = self._eng_cnt[eng]
            self._eng_cnt[eng] = n + 1
            val = n + 1
        self.ops.append(dict(i=i, eng=eng, fn=fn, deps=deps, sem=skey, val=val,
                             dma=dma, tag=tag))

    def op(self, eng, fn, r=(), w=(), tag=""):
        self._record(eng, fn, tuple(r), tuple(w), False, tag)

    def dma(self, eng, fn, r=(), w=(), tag=""):
        self._record(eng, fn, tuple(r), tuple(w), True, tag)

    def _sem(self, key):
        if key not in self._sems:
            self._sems[key] = self.stack.enter_context(
                self.nc.semaphore(f"s{len(self._sems)}"))
        return self._sems[key]

    def finalize(self):
        assert not self._done
        self._done = True
        for rec in self.ops:
            self._sem(rec["sem"])
        finals = {}
        for rec in self.ops:
            finals[rec["sem"]] = max(finals.get(rec["sem"], 0), rec["val"])
        block = self.stack.enter_context(self.nc.Block())

        def mk(engname):
            def section(e):
                wm = {}
                for rec in self.ops:
                    if rec["eng"] != engname:
                        continue
                    needed = {}
                    for j in rec["deps"]:
                        d = self.ops[j]
                        if (engname == "pe" and not d["dma"] and d["eng"] == "pe"):
                            continue
                        k = d["sem"]
                        if needed.get(k, 0) < d["val"]:
                            needed[k] = d["val"]
                    for k in sorted(needed):
                        v = needed[k]
                        if wm.get(k, 0) >= v:
                            continue
                        e.wait_ge(self._sem(k), v)
                        wm[k] = v
                    inst = rec["fn"](e)
                    if inst is None:
                        inst = e.drain()
                    inst.then_inc(self._sem(rec["sem"]), 16 if rec["dma"] else 1)
                if engname == "sp":
                    for k in sorted(finals):
                        if wm.get(k, 0) < finals[k]:
                            e.wait_ge(self._sem(k), finals[k])
            return section

        block.sync(mk("sp"))
        block.scalar(mk("act"))
        block.vector(mk("dve"))
        block.gpsimd(mk("pool"))
        block.tensor(mk("pe"))
        self.stack.close()
        return self.nc

    def run(self, in_maps, **kw):
        nc = self.finalize()
        return run_bass_kernel_spmd(nc, in_maps, core_ids=list(range(len(in_maps))), **kw)



def ld(p, eng, dst_ap, src_ap, w, r=(), tag="ld"):
    p.dma(eng, lambda e, d=dst_ap, s=src_ap: e.dma_start(out=d, in_=s), r=r, w=w, tag=tag)


def st(p, eng, dst_ap, src_ap, r, w, tag="st"):
    p.dma(eng, lambda e, d=dst_ap, s=src_ap: e.dma_start(out=d, in_=s), r=r, w=w, tag=tag)


def mm(p, mms, r, w, tag="mm"):
    def fn(e, mms=mms):
        inst = None
        for (o, lh, rh, s0, s1) in mms:
            inst = e.matmul(o, lh, rh, start=s0, stop=s1)
        return inst
    p.op("pe", fn, r=r, w=w, tag=tag)


def trans(p, pairs, ident, r, w, tag="tr"):
    idap = ident[:, :] if hasattr(ident, "ap") else ident
    def fn(e, pairs=pairs, idap=idap):
        inst = None
        for (o, i_) in pairs:
            inst = e.transpose(o, i_, idap)
        return inst
    p.op("pe", fn, r=tuple(r) + ("ident",), w=w, tag=tag)


def cp(p, eng, dst, src, r, w, tag="cp"):
    if eng == "act":
        p.op("act", lambda e, d=dst, s=src: e.activation(d, s, AF.Identity, bias=f32(0.0)),
             r=r, w=w, tag=tag)
    else:
        p.op(eng, lambda e, d=dst, s=src: e.tensor_copy(d, s), r=r, w=w, tag=tag)


def _co(v):
    return float(v) if isinstance(v, (np.floating, int)) else v


def ts(p, dst, src, s1, op1, s2=None, op2=None, r=(), w=(), eng="dve", tag="ts", accum=None):
    op2 = OP.bypass if op2 is None else op2
    s1 = _co(s1); s2 = _co(s2)
    def fn(e, d=dst, s=src, a=s1, b=s2, o1=op1, o2=op2, acc=accum):
        return e.tensor_scalar(d, s, a, b, o1, o2, accum_out=acc)
    p.op(eng, fn, r=r, w=w, tag=tag)


def tt(p, dst, a, b, alu, r=(), w=(), eng="dve", tag="tt"):
    p.op(eng, lambda e, d=dst, x=a, y=b, o=alu: e.tensor_tensor(d, x, y, o), r=r, w=w, tag=tag)


def stt(p, dst, in0, scalar, in1, op0, op1, r=(), w=(), tag="stt"):
    scalar = _co(scalar)
    p.op("dve", lambda e, d=dst, a=in0, s=scalar, b=in1, o0=op0, o1=op1:
         e.scalar_tensor_tensor(d, a, s, b, o0, o1), r=r, w=w, tag=tag)


def act(p, dst, src, func, bias=f32(0.0), scale=f32(1.0), accum=None, r=(), w=(), tag="act"):
    def fn(e, d=dst, s=src, fu=func, b=bias, sc=scale, acc=accum):
        return e.activation(d, s, fu, bias=b, scale=sc, accum_out=acc)
    p.op("act", fn, r=r, w=w, tag=tag)


def red(p, dst, src, alu, absval=None, negate=None, r=(), w=(), tag="red"):
    p.op("dve", lambda e, d=dst, s=src, o=alu, a=absval, n=negate:
         e.tensor_reduce(d, s, AX.X, o, apply_absolute_value=a, negate=n),
         r=r, w=w, tag=tag)


def memset(p, ap_, val, w, tag="memset"):
    p.op("pool", lambda e, a=ap_, v=val: e.memset(a, v), r=(), w=w, tag=tag)



def emit_ln_from_hbm(p, x_hbm, pfx, xn, xnT, pt_bf, with_absmax=None):
    stg = p.t[pfx + "stg"]
    bn = p.t[pfx + "bn"]
    mv = p.t[pfx + "mv"]
    sd = p.t[pfx + "sd"]
    rstd = p.t[pfx + "rstd"]
    ngm = p.t[pfx + "ngm"]
    xv = x_hbm.rearrange("(tt q) c -> tt q c", q=128)
    for t_ in range(NT):
        sl = t_ % 2
        kstg = f"{pfx}stg{sl}"
        ld(p, "sp", stg[:, sl, :], xv[t_], w=(kstg,))
        p.op("dve", lambda e, s=stg[:, sl, 0:512], d=bn[:, 0:6]: e.bn_stats(d, s),
             r=(kstg,), w=(pfx + "bn",))
        p.op("dve", lambda e, s=stg[:, sl, 512:1024], d=bn[:, 6:12]: e.bn_stats(d, s),
             r=(kstg,), w=(pfx + "bn",))
        p.op("dve", lambda e, s=bn[:, 0:12], d=mv[:, 0:2]: e.bn_aggr(d, s),
             r=(pfx + "bn",), w=(pfx + "mv",))
        act(p, sd[:, 0:1], mv[:, 1:2], AF.Sqrt, bias=f32(1e-5),
            r=(pfx + "mv",), w=(pfx + "sd",))
        p.op("dve", lambda e, d=rstd[:, t_:t_ + 1], s=sd[:, 0:1]: e.reciprocal(d, s),
             r=(pfx + "sd",), w=(pfx + "rstd",))
        stt(p, ngm[:, t_:t_ + 1], mv[:, 0:1], f32(-1.0), rstd[:, t_:t_ + 1],
            OP.mult, OP.mult, r=(pfx + "mv", pfx + "rstd"), w=(pfx + "ngm",))
        act(p, xn[:, t_, :], stg[:, sl, :], AF.Identity,
            bias=ngm[:, t_:t_ + 1], scale=rstd[:, t_:t_ + 1],
            r=(kstg, pfx + "rstd", pfx + "ngm"), w=(f"xn{t_}",))
        if with_absmax is not None:
            red(p, with_absmax[:, t_:t_ + 1], xn[:, t_, :], OP.max, absval=True,
                r=(f"xn{t_}",), w=(pfx + "amx",))
        trans(p, [(pt_bf[:, c * 128:(c + 1) * 128], xn[:, t_, c * 128:(c + 1) * 128])
                  for c in range(NCH)], p.t["ident"],
              r=(f"xn{t_}",), w=("ptb",))
        cp(p, "dve", xnT[:, :, t_ * 128:(t_ + 1) * 128], pt_bf[:, 0:1024].rearrange("p (c q) -> p c q", q=128),
           r=("ptb",), w=(f"xnT{t_}",))


def alloc_ln(p, pfx, ptkey="pq"):
    p.t[pfx + "ptkey"] = ptkey
    p.sb(pfx + "xnf", [128, 2, 1024], F32)
    p.sb(pfx + "stg", [128, 2, 1024], F32)
    p.sb(pfx + "bn", [128, 12], F32)
    p.sb(pfx + "mv", [128, 2], F32)
    p.sb(pfx + "sd", [128, 1], F32)
    p.sb(pfx + "rstd", [128, 16], F32)
    p.sb(pfx + "ngm", [128, 16], F32)


XN_KEYS = tuple(f"xn{t}" for t in range(NT))
XNT_KEYS = tuple(f"xnT{t}" for t in range(NT))



def emit_causal_attn_head(p, h, qh, kh, vtok, yT, masktri, smax_chunks=4):
    sf = p.t["sf"]; abf = p.t["abf"]; at = p.t["at"]
    mx = p.t["mx4"]; lp = p.t["lp4"]; negm = p.t["negm"]; rec = p.t["rec"]
    ps = [p.t["ps0"], p.t["ps1"]]
    pt = p.t["ptb"]; py = p.t["pyb"]
    hp = (h % 2) * 64
    for qi in range(NT):
        kmax = (qi + 1) * 128
        nkc = (kmax + 511) // 512
        for kc in range(nkc):
            klo = kc * 512
            kw = min(512, kmax - klo)
            b = kc % 2
            pk = f"ps{b}"
            mm(p, [(ps[b][0:128, 0:kw], qh[0:64, qi * 128:(qi + 1) * 128],
                    kh[0:64, klo:klo + kw], True, True)],
               r=("qh", "kh"), w=(pk,), tag=f"s{h}.{qi}.{kc}")
            if klo <= qi * 128 < klo + kw:
                off = qi * 128 - klo
                tt(p, ps[b][0:128, off:off + 128], ps[b][0:128, off:off + 128],
                   masktri[:, 0:128], OP.add, r=(pk, "masktri"), w=(pk,))
            red(p, mx[:, kc:kc + 1], ps[b][0:128, 0:kw], OP.max, r=(pk,), w=("mx4",))
            cp(p, "dve", sf[:, klo:klo + kw], ps[b][0:128, 0:kw], r=(pk,), w=("sf",))
        red(p, negm[:, 0:1], mx[:, 0:nkc], OP.max, negate=True, r=("mx4",), w=("negm",))
        for kc in range(nkc):
            klo = kc * 512
            kw = min(512, kmax - klo)
            act(p, abf[:, klo:klo + kw], sf[:, klo:klo + kw], AF.Exp,
                bias=negm[:, 0:1], accum=lp[:, kc:kc + 1],
                r=("sf", "negm"), w=("abf", "lp4"))
        red(p, rec[:, 0:1], lp[:, 0:nkc], OP.add, r=("lp4",), w=("rec",))
        p.op("dve", lambda e, d=rec[:, 0:1], s=rec[:, 0:1]: e.reciprocal(d, s),
             r=("rec",), w=("rec",))
        ts(p, abf[:, 0:kmax], abf[:, 0:kmax], rec[:, 0:1], OP.mult,
           r=("abf", "rec"), w=("abf",))
        nkb = qi + 1
        for g in range(0, nkb, 8):
            ge = min(g + 8, nkb)
            trans(p, [(pt[:, (kb - g) * 128:(kb - g + 1) * 128],
                       abf[:, kb * 128:(kb + 1) * 128]) for kb in range(g, ge)],
                  p.t["ident"], r=("abf",), w=("ptb",))
            cp(p, "dve", at[:, g:ge, :],
               pt[:, 0:(ge - g) * 128].rearrange("p (c q) -> p c q", q=128),
               r=("ptb",), w=("at",))
        mm(p, [(py[hp:hp + 64, 0:128], vtok[:, kb, h * 64:(h + 1) * 64],
                at[:, kb, :], kb == 0, kb == nkb - 1) for kb in range(nkb)],
           r=("vtok", "at"), w=("pyb",), tag=f"av{h}.{qi}")
        cp(p, "dve", yT[hp:hp + 64, h // 2, qi * 128:(qi + 1) * 128],
           py[hp:hp + 64, 0:128], r=("pyb",), w=("yT",))



def build_la():
    p = Prog("la")
    x = p.inp("x", (T, D))
    wqT = p.inp("wqT", (D, D)); wkT = p.inp("wkT", (D, D))
    wvT = p.inp("wvT", (D, D)); woT = p.inp("woT", (D, D))
    bq_s = p.inp("bq_s", (128, 8)); bk_s = p.inp("bk_s", (128, 8))
    bv_bc = p.inp("bv_bc", (128, D)); bo_bc = p.inp("bo_bc", (128, D))
    identb = p.inp("identb", (128, 128), BF16)
    identf = p.inp("identf", (128, 128), F32)
    masktri = p.inp("masktri", (128, 128), F32)

    a0_o = p.outp("a0", (T, D), F32)
    xnT_o = p.outp("xnT_o", (NCH, 128, T), BF16)
    mx_o = p.outp("mx", (1, 1), F32)

    alloc_ln(p, "l.")
    xnT = p.sb("xnT", [128, NCH, T], BF16)
    amx = p.sb("amx", [128, 16], F32)
    wqb = p.sb("wqb", [128, 8, 1024], BF16)
    wkb = p.sb("wkb", [128, 8, 1024], BF16)
    wvb = p.sb("wvb", [128, 8, 1024], BF16)
    bqs = p.sb("bqs", [128, 8], F32); bks = p.sb("bks", [128, 8], F32)
    bvb = p.sb("bvb", [128, 1024], F32); bob = p.sb("bob", [128, 1024], F32)
    idb = p.sb("ident", [128, 128], BF16)
    idf = p.sb("identf_s", [128, 128], F32)
    mtri = p.sb("mtri", [128, 128], F32)
    vtok = p.sb("vtok", [128, NT, 1024], BF16)
    yT = p.sb("yT", [128, NCH, T], BF16)
    qh2 = p.sb("qh2", [64, T], BF16)
    kh2 = p.sb("kh2", [64, T], BF16)
    sf = p.sb("sf", [128, T], F32)
    abf = p.sb("abf", [128, T], BF16)
    at = p.sb("at", [128, 16, 128], BF16)
    mx4 = p.sb("mx4", [128, 4], F32)
    lp4 = p.sb("lp4", [128, 4], F32)
    negm = p.sb("negm", [128, 1], F32)
    rec = p.sb("rec", [128, 1], F32)
    a0stg = p.sb("a0stg", [128, 2, 1024], F32)
    mxc = p.sb("mxc", [128, 1], F32)
    mxr = p.sb("mxr", [1, 1], F32)
    ps0 = p.ps("ps0", [128, 512], F32)
    ps1 = p.ps("ps1", [128, 512], F32)
    ptb = p.ps("ptb", [128, 1024], BF16)
    pyb = p.ps("pyb", [128, 512], F32)
    pq = p.ps("pq", [128, 512], F32)
    pv = p.ps("pv", [128, 512], F32)
    pmx = p.ps("pmx", [128, 512], F32)

    ld(p, "sp", idb[:], identb[:], w=("ident",))
    ld(p, "sp", idf[:], identf[:], w=("identf",))
    ld(p, "sp", mtri[:], masktri[:], w=("masktri",))
    ld(p, "sp", bqs[:], bq_s[:], w=("bq",))
    ld(p, "sp", bks[:], bk_s[:], w=("bk",))
    ld(p, "sp", bvb[:], bv_bc[:], w=("bv",))
    ld(p, "sp", bob[:], bo_bc[:], w=("bo",))
    for wb, wd, key in ((wqb, wqT, "wq"), (wkb, wkT, "wk"), (wvb, wvT, "wv")):
        ld(p, "pool", wb[:, :, :], wd.rearrange("(c p) o -> p c o", p=128), w=(key,))
    ts(p, bqs[:], bqs[:], f32(0.125), OP.mult, r=("bq",), w=("bq",))

    emit_ln_from_hbm_rot(p, x, "l.", xnT, pq, amx)

    for c in range(NCH):
        st(p, "act", xnT_o[c], xnT[:, c, :], r=XNT_KEYS, w=("xnT_o",))

    red(p, mxc[:, 0:1], amx[:, 0:16], OP.max, r=("l.amx",), w=("mxc",))
    trans(p, [(pmx[0:1, 0:128], mxc[:, 0:1])], idf,
          r=("mxc", "identf"), w=("pmx",))
    red(p, mxr[0:1, 0:1], pmx[0:1, 0:128], OP.max, r=("pmx",), w=("mxr",))
    st(p, "act", mx_o[:], mxr[0:1, 0:1], r=("mxr",), w=("mx_o",))

    for t_ in range(NT):
        for oc in range(2):
            mm(p, [(pv[:, 0:512], xnT[:, ci, t_ * 128:(t_ + 1) * 128],
                    wvb[:, ci, oc * 512:(oc + 1) * 512], ci == 0, ci == 7)
                   for ci in range(NCH)],
               r=XNT_KEYS + ("wv",), w=("pv",), tag=f"v{t_}.{oc}")
            tt(p, vtok[:, t_, oc * 512:(oc + 1) * 512], pv[:, 0:512],
               bvb[:, oc * 512:(oc + 1) * 512], OP.add, r=("pv", "bv"), w=("vtok",))

    for h in range(16):
        for tc in range(4):
            mm(p, [(pq[0:64, 0:512], wqb[:, ci, h * 64:(h + 1) * 64],
                    xnT[:, ci, tc * 512:(tc + 1) * 512], ci == 0, ci == 7)
                   for ci in range(NCH)],
               r=XNT_KEYS + ("wq",), w=("pq",), tag=f"q{h}.{tc}")
            act(p, qh2[0:64, tc * 512:(tc + 1) * 512], pq[0:64, 0:512], AF.Identity,
                bias=bqs[(h % 2) * 64:(h % 2) * 64 + 64, h // 2:h // 2 + 1],
                scale=f32(0.125), r=("pq", "bq"), w=("qh",))
            mm(p, [(pq[0:64, 0:512], wkb[:, ci, h * 64:(h + 1) * 64],
                    xnT[:, ci, tc * 512:(tc + 1) * 512], ci == 0, ci == 7)
                   for ci in range(NCH)],
               r=XNT_KEYS + ("wk",), w=("pq",), tag=f"k{h}.{tc}")
            act(p, kh2[0:64, tc * 512:(tc + 1) * 512], pq[0:64, 0:512], AF.Identity,
                bias=bks[(h % 2) * 64:(h % 2) * 64 + 64, h // 2:h // 2 + 1],
                r=("pq", "bk"), w=("kh",))
        emit_causal_attn_head(p, h, qh2, kh2, vtok, yT, mtri)

    ld(p, "pool", wqb[:, :, :], woT.rearrange("(c p) o -> p c o", p=128), w=("wq",))
    for t_ in range(NT):
        sl = t_ % 2
        for oc in range(2):
            mm(p, [(pv[:, 0:512], yT[:, ic, t_ * 128:(t_ + 1) * 128],
                    wqb[:, ic, oc * 512:(oc + 1) * 512], ic == 0, ic == 7)
                   for ic in range(NCH)],
               r=("yT", "wq"), w=("pv",), tag=f"o{t_}.{oc}")
            tt(p, a0stg[:, sl, oc * 512:(oc + 1) * 512], pv[:, 0:512],
               bob[:, oc * 512:(oc + 1) * 512], OP.add,
               r=("pv", "bo"), w=(f"a0s{sl}",))
        st(p, "act", a0_o.rearrange("(tt q) c -> tt q c", q=128)[t_],
           a0stg[:, sl, :], r=(f"a0s{sl}",), w=("a0_o",))
    return p


def emit_ln_from_hbm_rot(p, x_hbm, pfx, xnT, ptf32, amx, xn_store_f32=None):
    stg = p.t[pfx + "stg"]
    bn = p.t[pfx + "bn"]
    mv = p.t[pfx + "mv"]
    sd = p.t[pfx + "sd"]
    rstd = p.t[pfx + "rstd"]
    ngm = p.t[pfx + "ngm"]
    xnf = p.t[pfx + "xnf"]
    ptk = p.t[pfx + "ptkey"]
    xv = x_hbm.rearrange("(tt q) c -> tt q c", q=128)
    for t_ in range(NT):
        sl = t_ % 2
        kstg = f"{pfx}stg{sl}"
        kxn = f"{pfx}xnf{sl}"
        ld(p, "sp", stg[:, sl, :], xv[t_], w=(kstg,))
        p.op("dve", lambda e, s=stg[:, sl, 0:512], d=bn[:, 0:6]: e.bn_stats(d, s),
             r=(kstg,), w=(pfx + "bn",))
        p.op("dve", lambda e, s=stg[:, sl, 512:1024], d=bn[:, 6:12]: e.bn_stats(d, s),
             r=(kstg,), w=(pfx + "bn",))
        p.op("dve", lambda e, s=bn[:, 0:12], d=mv[:, 0:2]: e.bn_aggr(d, s),
             r=(pfx + "bn",), w=(pfx + "mv",))
        act(p, sd[:, 0:1], mv[:, 1:2], AF.Sqrt, bias=f32(1e-5),
            r=(pfx + "mv",), w=(pfx + "sd",))
        p.op("dve", lambda e, d=rstd[:, t_:t_ + 1], s=sd[:, 0:1]: e.reciprocal(d, s),
             r=(pfx + "sd",), w=(pfx + "rstd",))
        stt(p, ngm[:, t_:t_ + 1], mv[:, 0:1], f32(-1.0), rstd[:, t_:t_ + 1],
            OP.mult, OP.mult, r=(pfx + "mv", pfx + "rstd"), w=(pfx + "ngm",))
        act(p, xnf[:, sl, :], stg[:, sl, :], AF.Identity,
            bias=ngm[:, t_:t_ + 1], scale=rstd[:, t_:t_ + 1],
            r=(kstg, pfx + "rstd", pfx + "ngm"), w=(kxn,))
        if amx is not None:
            red(p, amx[:, t_:t_ + 1], xnf[:, sl, :], OP.max, absval=True,
                r=(kxn,), w=(pfx + "amx",))
        if xn_store_f32 is not None:
            st(p, "act", xn_store_f32[t_], xnf[:, sl, :], r=(kxn,), w=("xnf_store",))
        for g in range(2):
            trans(p, [(ptf32[:, j * 128:(j + 1) * 128],
                       xnf[:, sl, (g * 4 + j) * 128:(g * 4 + j + 1) * 128])
                      for j in range(4)], p.t["identf_s"],
                  r=(kxn, "identf"), w=(ptk,))
            cp(p, "dve", xnT[:, g * 4:(g + 1) * 4, t_ * 128:(t_ + 1) * 128],
               ptf32[:, 0:512].rearrange("p (c q) -> p c q", q=128),
               r=(ptk,), w=(f"xnT{t_}",))


def _bcol(v):
    return np.ascontiguousarray(v.reshape(8, 128).T, dtype=np.float32)


def _tri_mask():
    m = np.zeros((128, 128), np.float32)
    m[np.triu_indices(128, 1)] = -1e30
    return m


def stage_la(xb, L0):
    return {
        "x": np.ascontiguousarray(xb, np.float32),
        "wqT": np.ascontiguousarray(L0["W_q"].T, np.float32),
        "wkT": np.ascontiguousarray(L0["W_k"].T, np.float32),
        "wvT": np.ascontiguousarray(L0["W_v"].T, np.float32),
        "woT": np.ascontiguousarray(L0["W_o"].T, np.float32),
        "bq_s": _bcol(L0["b_q"]), "bk_s": _bcol(L0["b_k"]),
        "bv_bc": np.ascontiguousarray(np.broadcast_to(L0["b_v"], (128, D)), np.float32),
        "bo_bc": np.ascontiguousarray(np.broadcast_to(L0["b_o"], (128, D)), np.float32),
        "identb": np.eye(128, dtype=ml_dtypes.bfloat16),
        "identf": np.eye(128, dtype=np.float32),
        "masktri": _tri_mask(),
    }



def emit_scale_from_max8(p, m8_ap, qmax, svec, rvec, s11, r11, pfx):
    m8s = p.t[pfx + "m8s"]
    ones = p.t[pfx + "ones"]
    pbc = p.t[pfx + "pbc"]
    memset(p, ones[0:1, 0:128], 1.0, w=(pfx + "ones",))
    ld(p, "sp", m8s[0:1, 0:8], m8_ap, w=(pfx + "m8",))
    red(p, s11[0:1, 0:1], m8s[0:1, 0:8], OP.max, r=(pfx + "m8",), w=(pfx + "s11",))
    ts(p, s11[0:1, 0:1], s11[0:1, 0:1], f32(1.0 / qmax), OP.mult, f32(1e-12), OP.max,
       r=(pfx + "s11",), w=(pfx + "s11",))
    p.op("dve", lambda e, d=s11[0:1, 1:2], s=s11[0:1, 0:1]: e.reciprocal(d, s),
         r=(pfx + "s11",), w=(pfx + "r11",))
    mm(p, [(pbc[0:128, 0:2], ones[0:1, 0:128], s11[0:1, 0:2], True, True)],
       r=(pfx + "s11", pfx + "r11", pfx + "ones"), w=(pfx + "pbc",))
    cp(p, "dve", svec[:, 0:1], pbc[0:128, 0:1], r=(pfx + "pbc",), w=(pfx + "sv",))
    cp(p, "dve", rvec[:, 0:1], pbc[0:128, 1:2], r=(pfx + "pbc",), w=(pfx + "rv",))


def alloc_scale8(p, pfx):
    p.sb(pfx + "m8s", [1, 8], F32)
    p.sb(pfx + "s11", [1, 2], F32)
    p.sb(pfx + "ones", [1, 128], F32)
    p.ps(pfx + "pbc", [128, 512], F32)
    sv = p.sb(pfx + "svec", [128, 1], F32)
    rv = p.sb(pfx + "rvec", [128, 1], F32)
    return sv, rv


def emit_qdq_i16(p, dst, src, rvec, svec, lo, hi, i32stg, rk, wk, pfx, extra_r=()):
    act(p, i32stg, src, AF.Identity, scale=rvec[:, 0:1],
        r=tuple(extra_r) + (rk, pfx + "rv"), w=("qi32",))
    ts(p, i32stg, i32stg, f32(hi), OP.min, f32(lo), OP.max, r=("qi32",), w=("qi32",))
    ts(p, dst, i32stg, svec[:, 0:1], OP.mult, r=("qi32", pfx + "sv"), w=(wk,))


def emit_lpbq_chunk(p, wstg_ap, nI, t1, ti, sc, rb, bq_out, kstg, kout):
    nb = nI // 32
    red(p, sc[:, 0:nb].rearrange("p (b o) -> p b o", o=1),
        wstg_ap.rearrange("p (b k) -> p b k", k=32), OP.max,
        absval=True, r=(kstg,), w=("lpb.sc",))
    ts(p, sc[:, 0:nb], sc[:, 0:nb], f32(1.0 / 7.0), OP.mult, f32(1e-12), OP.max,
       r=("lpb.sc",), w=("lpb.sc",))
    p.op("dve", lambda e, d=rb[:, 0:nb], s=sc[:, 0:nb]: e.reciprocal(d, s),
         r=("lpb.sc",), w=("lpb.rb",))
    tt(p, t1[:, 0:nI].rearrange("p (b k) -> p b k", k=32),
       wstg_ap.rearrange("p (b k) -> p b k", k=32),
       rb[:, 0:nb].rearrange("p (b k) -> p b k", k=1).broadcast_to((128, nb, 32)),
       OP.mult, r=(kstg, "lpb.rb"), w=("lpb.t1",))
    act(p, ti[:, 0:nI], t1[:, 0:nI], AF.Identity, r=("lpb.t1",), w=("lpb.ti",))
    ts(p, ti[:, 0:nI], ti[:, 0:nI], f32(7.0), OP.min, f32(-8.0), OP.max,
       r=("lpb.ti",), w=("lpb.ti",))
    tt(p, bq_out[:, 0:nI].rearrange("p (b k) -> p b k", k=32),
       ti[:, 0:nI].rearrange("p (b k) -> p b k", k=32),
       sc[:, 0:nb].rearrange("p (b k) -> p b k", k=1).broadcast_to((128, nb, 32)),
       OP.mult, r=("lpb.ti", "lpb.sc"), w=(kout,))


IB0 = 24


def build_lb():
    p = Prog("lb")
    xnT_i = p.inp("xnT_i", (NCH, 128, T), BF16)
    m8 = p.inp("m8", (1, 8), F32)
    wup = p.inp("wupN", (3072, D), F32)
    wgt = p.inp("wgtN", (3072, D), F32)
    identb = p.inp("identb", (128, 128), BF16)
    up_o = p.outp("up_o", (IB0, 128, T), F32)
    gt_o = p.outp("gt_o", (IB0, 128, T), F32)
    mup_o = p.outp("mup", (1, 1), F32)
    mgt_o = p.outp("mgt", (1, 1), F32)

    xnT = p.sb("xnT", [128, NCH, T], BF16)
    qx = p.sb("qx", [128, NCH, T], BF16)
    qi = p.sb("qi", [128, T], I32)
    sv, rv = alloc_scale8(p, "sx.")
    wstg = p.sb("wstg", [128, 2, D], F32)
    t1 = p.sb("t1", [128, D], F32)
    ti = p.sb("ti", [128, D], I32)
    sc = p.sb("sc", [128, 32], F32)
    rb = p.sb("rb", [128, 32], F32)
    wqn = p.sb("wqn", [128, D], BF16)
    wqT = p.sb("wqT", [128, 2, NCH, 128], BF16)
    idb = p.sb("ident", [128, 128], BF16)
    ustg = p.sb("ustg", [128, 2, T], F32)
    amx = p.sb("amx", [128, 2 * IB0], F32)
    amxg = p.sb("amxg", [128, 2 * IB0], F32)
    mc = p.sb("mc", [128, 1], F32)
    mr = p.sb("mr", [1, 1], F32)
    idf = p.sb("identf_s", [128, 128], F32)
    identf = p.inp("identf", (128, 128), F32)
    pup = p.ps("pup", [128, 512], F32)
    ptb = p.ps("ptb", [128, 1024], BF16)
    pmx = p.ps("pmx", [128, 512], F32)

    ld(p, "sp", idb[:], identb[:], w=("ident",))
    ld(p, "sp", idf[:], identf[:], w=("identf",))
    ld(p, "pool", xnT[:, :, :], xnT_i.rearrange("c p t -> p c t"), w=("xnTl",))
    emit_scale_from_max8(p, m8[:, :], 32767.0, sv, rv, p.t["sx.s11"], None, "sx.")
    for c in range(NCH):
        emit_qdq_i16(p, qx[:, c, :], xnT[:, c, :], rv, sv, -32768.0, 32767.0,
                     qi[:, :], "xnTl", f"qx{c}", "sx.")
    QXK = tuple(f"qx{c}" for c in range(NCH))

    def one_weight(w_hbm, out_hbm, amx_t, col0, tag):
        wv = w_hbm.rearrange("(ib q) c -> ib q c", q=128)
        ov = out_hbm
        for ib in range(IB0):
            sl = ib % 2
            kstg = f"wstg{sl}"
            ld(p, "sp", wstg[:, sl, :], wv[ib], w=(kstg,))
            emit_lpbq_chunk(p, wstg[:, sl, :], D, t1, ti, sc, rb, wqn, kstg, "wqn")
            trans(p, [(ptb[:, c * 128:(c + 1) * 128], wqn[:, c * 128:(c + 1) * 128])
                      for c in range(NCH)], idb, r=("wqn",), w=("ptb",))
            cp(p, "dve", wqT[:, sl, :, :],
               ptb[:, 0:1024].rearrange("p (c q) -> p c q", q=128),
               r=("ptb",), w=(f"wqT{sl}",))
            usl = ib % 2
            for tc in range(4):
                mm(p, [(pup[:, 0:512], wqT[:, sl, c, :], qx[:, c, tc * 512:(tc + 1) * 512],
                        c == 0, c == 7) for c in range(NCH)],
                   r=(f"wqT{sl}",) + QXK, w=("pup",), tag=f"{tag}{ib}.{tc}")
                cp(p, "dve", ustg[:, usl, tc * 512:(tc + 1) * 512], pup[:, 0:512],
                   r=("pup",), w=(f"ustg{usl}",))
            red(p, amx_t[:, col0 + ib:col0 + ib + 1], ustg[:, usl, :], OP.max,
                absval=True, r=(f"ustg{usl}",), w=("amxk" + tag,))
            st(p, "act", ov[ib], ustg[:, usl, :], r=(f"ustg{usl}",), w=("out" + tag,))

    one_weight(wup, up_o, amx, 0, "u")
    one_weight(wgt, gt_o, amxg, 0, "g")
    for amx_t, key, out in ((amx, "amxku", mup_o), (amxg, "amxkg", mgt_o)):
        red(p, mc[:, 0:1], amx_t[:, 0:IB0], OP.max, r=(key,), w=("mc",))
        trans(p, [(pmx[0:1, 0:128], mc[:, 0:1])], idf, r=("mc", "identf"), w=("pmx",))
        red(p, mr[0:1, 0:1], pmx[0:1, 0:128], OP.max, r=("pmx",), w=("mr",))
        st(p, "act", out[:], mr[0:1, 0:1], r=("mr",), w=("m_o" + key,))
    return p


def stage_lb(xnT, m8, L0f):
    return {
        "xnT_i": xnT, "m8": m8.reshape(1, 8).astype(np.float32),
        "wupN": np.ascontiguousarray(L0f["w_up"], np.float32),
        "wgtN": np.ascontiguousarray(L0f["w_gate"], np.float32),
        "identb": np.eye(128, dtype=ml_dtypes.bfloat16),
        "identf": np.eye(128, dtype=np.float32),
    }


def build_lc():
    p = Prog("lc")
    up_i = p.inp("up_i", (IB0, 128, T), F32)
    gt_i = p.inp("gt_i", (IB0, 128, T), F32)
    mup8 = p.inp("mup8", (1, 8), F32)
    mgt8 = p.inp("mgt8", (1, 8), F32)
    upq_o = p.outp("upq_o", (IB0, 128, T), BF16)
    g2_o = p.outp("g2_o", (IB0, 128, T), F32)
    mg2_o = p.outp("mg2", (1, 1), F32)
    identf = p.inp("identf", (128, 128), F32)

    stg = p.sb("stg", [128, 2, T], F32)
    qi = p.sb("qi", [128, T], I32)
    gq = p.sb("gq", [128, T], BF16)
    sg = p.sb("sg", [128, T], F32)
    sgi = p.sb("sgi", [128, T], I32)
    g2 = p.sb("g2", [128, 2, T], F32)
    uq = p.sb("uq", [128, 2, T], BF16)
    amx = p.sb("amx", [128, IB0], F32)
    mc = p.sb("mc", [128, 1], F32)
    mr = p.sb("mr", [1, 1], F32)
    idf = p.sb("identf_s", [128, 128], F32)
    svu, rvu = alloc_scale8(p, "su.")
    svg, rvg = alloc_scale8(p, "sg.")
    pmx = p.ps("pmx", [128, 512], F32)

    ld(p, "sp", idf[:], identf[:], w=("identf",))
    emit_scale_from_max8(p, mup8[:, :], 32767.0, svu, rvu, p.t["su.s11"], None, "su.")
    emit_scale_from_max8(p, mgt8[:, :], 32767.0, svg, rvg, p.t["sg.s11"], None, "sg.")
    for ib in range(IB0):
        sl = ib % 2
        ld(p, "sp", stg[:, sl, :], up_i[ib], w=(f"stg{sl}",))
        emit_qdq_i16(p, uq[:, sl, :], stg[:, sl, :], rvu, svu, -32768.0, 32767.0,
                     qi[:, :], f"stg{sl}", f"uq{sl}", "su.")
        st(p, "act", upq_o[ib], uq[:, sl, :], r=(f"uq{sl}",), w=("upq_o",))
        ld(p, "sp", stg[:, sl, :], gt_i[ib], w=(f"stg{sl}",))
        emit_qdq_i16(p, gq[:, :], stg[:, sl, :], rvg, svg, -32768.0, 32767.0,
                     qi[:, :], f"stg{sl}", "gq", "sg.")
        act(p, sg[:, :], gq[:, :], AF.Sigmoid, r=("gq",), w=("sg",))
        act(p, sgi[:, :], sg[:, :], AF.Identity, scale=f32(65536.0), r=("sg",), w=("sgi",))
        ts(p, sgi[:, :], sgi[:, :], f32(65535.0), OP.min, f32(0.0), OP.max,
           r=("sgi",), w=("sgi",))
        stt(p, g2[:, sl, :], sgi[:, :], f32(1.0 / 65536.0), gq[:, :], OP.mult, OP.mult,
            r=("sgi", "gq"), w=(f"g2{sl}",))
        red(p, amx[:, ib:ib + 1], g2[:, sl, :], OP.max, absval=True,
            r=(f"g2{sl}",), w=("amxk",))
        st(p, "act", g2_o[ib], g2[:, sl, :], r=(f"g2{sl}",), w=("g2_o",))
    red(p, mc[:, 0:1], amx[:, 0:IB0], OP.max, r=("amxk",), w=("mc",))
    trans(p, [(pmx[0:1, 0:128], mc[:, 0:1])], idf, r=("mc", "identf"), w=("pmx",))
    red(p, mr[0:1, 0:1], pmx[0:1, 0:128], OP.max, r=("pmx",), w=("mr",))
    st(p, "act", mg2_o[:], mr[0:1, 0:1], r=("mr",), w=("mg2_o",))
    return p


def build_ld():
    p = Prog("ld")
    g2_i = p.inp("g2_i", (IB0, 128, T), F32)
    upq_i = p.inp("upq_i", (IB0, 128, T), BF16)
    mg28 = p.inp("mg28", (1, 8), F32)
    or_o = p.outp("or_o", (IB0, 128, T), F32)
    mo_o = p.outp("mo", (1, 1), F32)
    identf = p.inp("identf", (128, 128), F32)

    stg = p.sb("stg", [128, 2, T], F32)
    ustg = p.sb("ustg", [128, 2, T], BF16)
    qi = p.sb("qi", [128, T], I32)
    g2q = p.sb("g2q", [128, T], BF16)
    orr = p.sb("orr", [128, 2, T], F32)
    amx = p.sb("amx", [128, IB0], F32)
    mc = p.sb("mc", [128, 1], F32)
    mr = p.sb("mr", [1, 1], F32)
    idf = p.sb("identf_s", [128, 128], F32)
    sv, rv = alloc_scale8(p, "sg2.")
    pmx = p.ps("pmx", [128, 512], F32)

    ld(p, "sp", idf[:], identf[:], w=("identf",))
    emit_scale_from_max8(p, mg28[:, :], 32767.0, sv, rv, p.t["sg2.s11"], None, "sg2.")
    for ib in range(IB0):
        sl = ib % 2
        ld(p, "sp", stg[:, sl, :], g2_i[ib], w=(f"stg{sl}",))
        ld(p, "sp", ustg[:, sl, :], upq_i[ib], w=(f"ustg{sl}",))
        emit_qdq_i16(p, g2q[:, :], stg[:, sl, :], rv, sv, -32768.0, 32767.0,
                     qi[:, :], f"stg{sl}", "g2q", "sg2.")
        tt(p, orr[:, sl, :], g2q[:, :], ustg[:, sl, :], OP.mult,
           r=("g2q", f"ustg{sl}"), w=(f"or{sl}",))
        red(p, amx[:, ib:ib + 1], orr[:, sl, :], OP.max, absval=True,
            r=(f"or{sl}",), w=("amxk",))
        st(p, "act", or_o[ib], orr[:, sl, :], r=(f"or{sl}",), w=("or_o",))
    red(p, mc[:, 0:1], amx[:, 0:IB0], OP.max, r=("amxk",), w=("mc",))
    trans(p, [(pmx[0:1, 0:128], mc[:, 0:1])], idf, r=("mc", "identf"), w=("pmx",))
    red(p, mr[0:1, 0:1], pmx[0:1, 0:128], OP.max, r=("pmx",), w=("mr",))
    st(p, "act", mo_o[:], mr[0:1, 0:1], r=("mr",), w=("mo_o",))
    return p


def build_le():
    p = Prog("le")
    or_i = p.inp("or_i", (IB0, 128, T), F32)
    mo8 = p.inp("mo8", (1, 8), F32)
    wdn = p.inp("wdnN", (D, 3072), F32)
    x_i = p.inp("x", (T, D), F32)
    a0_i = p.inp("a0", (T, D), F32)
    identb = p.inp("identb", (128, 128), BF16)
    x1_o = p.outp("x1", (T, D), F32)
    qo_d = p.dscratch("qo_d", (IB0, 128, T), BF16)

    stg = p.sb("stg", [128, 2, T], F32)
    qi = p.sb("qi", [128, T], I32)
    qo = p.sb("qo", [128, 2, T], BF16)
    wstg = p.sb("wstg", [128, 2, 3072], F32)
    t1 = p.sb("t1", [128, 3072], F32)
    ti = p.sb("ti", [128, 3072], I32)
    sc = p.sb("sc", [128, 96], F32)
    rb = p.sb("rb", [128, 96], F32)
    wqn = p.sb("wqn", [128, 3072], BF16)
    wdT = p.sb("wdT", [128, IB0, 1024], BF16)
    idb = p.sb("ident", [128, 128], BF16)
    qos = p.sb("qos", [128, IB0, 128], BF16)
    xst = p.sb("xst", [128, 2, D], F32)
    ast = p.sb("ast", [128, 2, D], F32)
    x1s = p.sb("x1s", [128, 2, D], F32)
    sv, rv = alloc_scale8(p, "so.")
    pdn = p.ps("pdn", [128, 512], F32)
    ptb = p.ps("ptb", [128, 1024], BF16)

    ld(p, "sp", idb[:], identb[:], w=("ident",))
    emit_scale_from_max8(p, mo8[:, :], 32767.0, sv, rv, p.t["so.s11"], None, "so.")
    for ib in range(IB0):
        sl = ib % 2
        ld(p, "sp", stg[:, sl, :], or_i[ib], w=(f"stg{sl}",))
        emit_qdq_i16(p, qo[:, sl, :], stg[:, sl, :], rv, sv, -32768.0, 32767.0,
                     qi[:, :], f"stg{sl}", f"qo{sl}", "so.")
        st(p, "act", qo_d[ib], qo[:, sl, :], r=(f"qo{sl}",), w=("qo_d",))
    wv = wdn.rearrange("(oc q) i -> oc q i", q=128)
    for oc in range(NCH):
        sl = oc % 2
        ld(p, "sp", wstg[:, sl, :], wv[oc], w=(f"wstg{sl}",))
        emit_lpbq_chunk(p, wstg[:, sl, :], 3072, t1, ti, sc, rb, wqn, f"wstg{sl}", "wqn")
        for g in range(3):
            trans(p, [(ptb[:, j * 128:(j + 1) * 128],
                       wqn[:, (g * 8 + j) * 128:(g * 8 + j + 1) * 128]) for j in range(8)],
                  idb, r=("wqn",), w=("ptb",))
            cp(p, "dve", wdT[:, g * 8:(g + 1) * 8, oc * 128:(oc + 1) * 128],
               ptb[:, 0:1024].rearrange("p (c q) -> p c q", q=128),
               r=("ptb",), w=("wdT",))
    qview = qo_d.rearrange("ib p t -> p ib t")
    xv = x_i.rearrange("(tt q) c -> tt q c", q=128)
    av = a0_i.rearrange("(tt q) c -> tt q c", q=128)
    x1v = x1_o.rearrange("(tt q) c -> tt q c", q=128)
    for t_ in range(NT):
        sl = t_ % 2
        ld(p, "sp", qos[:, :, :], qview[:, :, t_ * 128:(t_ + 1) * 128],
           w=("qos",), r=("qo_d",))
        ld(p, "sp", xst[:, sl, :], xv[t_], w=(f"xst{sl}",))
        ld(p, "sp", ast[:, sl, :], av[t_], w=(f"ast{sl}",))
        for oc in range(2):
            mm(p, [(pdn[:, 0:512], qos[:, ib, :], wdT[:, ib, oc * 512:(oc + 1) * 512],
                    ib == 0, ib == IB0 - 1) for ib in range(IB0)],
               r=("qos", "wdT"), w=("pdn",), tag=f"dn{t_}.{oc}")
            tt(p, x1s[:, sl, oc * 512:(oc + 1) * 512], pdn[:, 0:512],
               xst[:, sl, oc * 512:(oc + 1) * 512], OP.add,
               r=("pdn", f"xst{sl}"), w=(f"x1s{sl}",))
        tt(p, x1s[:, sl, :], x1s[:, sl, :], ast[:, sl, :], OP.add,
           r=(f"x1s{sl}", f"ast{sl}"), w=(f"x1s{sl}",))
        st(p, "act", x1v[t_], x1s[:, sl, :], r=(f"x1s{sl}",), w=("x1_o",))
    return p


def stage_lc(up, gt, mup8, mgt8):
    return {"up_i": up, "gt_i": gt, "mup8": mup8.reshape(1, 8).astype(np.float32),
            "mgt8": mgt8.reshape(1, 8).astype(np.float32),
            "identf": np.eye(128, dtype=np.float32)}


def stage_ld(g2, upq, mg28):
    return {"g2_i": g2, "upq_i": upq, "mg28": mg28.reshape(1, 8).astype(np.float32),
            "identf": np.eye(128, dtype=np.float32)}


def stage_le(orr, mo8, L0f, xb, a0):
    return {"or_i": orr, "mo8": mo8.reshape(1, 8).astype(np.float32),
            "wdnN": np.ascontiguousarray(L0f["w_down"], np.float32),
            "x": xb, "a0": a0,
            "identb": np.eye(128, dtype=ml_dtypes.bfloat16)}



FP4_MID = (0.25, 0.75, 1.25, 1.75, 2.5, 3.5, 5.0)
FP4_STEP = (0.5, 0.5, 0.5, 0.5, 1.0, 1.0, 2.0)


def emit_fp4_qdq(p, dst, src, F, sfx, rk, wk, out_extra=None):
    nb = F // 16
    sc = p.t["f4.sc"]; rb = p.t["f4.rb"]; u = p.t["f4.u"]; au = p.t["f4.au"]
    a1 = p.t["f4.acc"]; a2 = p.t["f4.acc2"]; sg = p.t["f4.sg"]
    K = lambda n: "f4." + n
    red(p, sc[:, 0:nb].rearrange("p (b o) -> p b o", o=1),
        src.rearrange("p (b k) -> p b k", k=16), OP.max, absval=True,
        r=(rk,), w=(K("sc"),))
    ts(p, sc[:, 0:nb], sc[:, 0:nb], f32(1.0 / 6.0), OP.mult, f32(1e-12), OP.max,
       r=(K("sc"),), w=(K("sc"),))
    p.op("dve", lambda e, d=rb[:, 0:nb], s=sc[:, 0:nb]: e.reciprocal(d, s),
         r=(K("sc"),), w=(K("rb"),))
    tt(p, u[:, 0:F].rearrange("p (b k) -> p b k", k=16),
       src.rearrange("p (b k) -> p b k", k=16),
       rb[:, 0:nb].rearrange("p (b k) -> p b k", k=1).broadcast_to((128, nb, 16)),
       OP.mult, r=(rk, K("rb")), w=(K("u"),))
    act(p, au[:, 0:F], u[:, 0:F], AF.Abs, r=(K("u"),), w=(K("au"),))
    accs = (a1, a2)
    for j in range(7):
        act(p, sg[:, 0:F], au[:, 0:F], AF.Sign, bias=f32(-FP4_MID[j]),
            r=(K("au"),), w=(K("sg"),))
        if j == 0:
            ts(p, a1[:, 0:F], sg[:, 0:F], float(FP4_STEP[0]), OP.mult,
               r=(K("sg"),), w=(K("acc"),))
        else:
            stt(p, accs[j % 2][:, 0:F], sg[:, 0:F], float(FP4_STEP[j]),
                accs[(j + 1) % 2][:, 0:F], OP.mult, OP.add,
                r=(K("sg"), K("acc") if j % 2 == 1 else K("acc2")),
                w=(K("acc2") if j % 2 == 1 else K("acc"),))
    fin = accs[6 % 2]
    ts(p, a1[:, 0:F], fin[:, 0:F], f32(0.5), OP.mult, f32(3.0), OP.add,
       r=(K("acc"),), w=(K("acc"),))
    act(p, sg[:, 0:F], u[:, 0:F], AF.Sign, r=(K("u"),), w=(K("sg"),))
    tt(p, a2[:, 0:F].rearrange("p (b k) -> p b k", k=16),
       a1[:, 0:F].rearrange("p (b k) -> p b k", k=16),
       sc[:, 0:nb].rearrange("p (b k) -> p b k", k=1).broadcast_to((128, nb, 16)),
       OP.mult, r=(K("acc"), K("sc")), w=(K("acc2"),))
    tt(p, dst, a2[:, 0:F], sg[:, 0:F], OP.mult, r=(K("acc2"), K("sg")), w=(wk,))


def alloc_fp4(p, F):
    p.sb("f4.sc", [128, F // 16], F32)
    p.sb("f4.rb", [128, F // 16], F32)
    p.sb("f4.u", [128, F], F32)
    p.sb("f4.au", [128, F], F32)
    p.sb("f4.acc", [128, F], BF16)
    p.sb("f4.acc2", [128, F], BF16)
    p.sb("f4.sg", [128, F], BF16)


def rope_tables(t, dim):
    inv = 1.0 / (10000.0 ** (np.arange(0, dim, 2, dtype=np.float64) / dim))
    freqs = np.arange(t, dtype=np.float64)[:, None] * inv[None, :]
    emb = np.concatenate([freqs, freqs], axis=-1)
    return np.cos(emb).astype(np.float32), np.sin(emb).astype(np.float32)


def _tile_tok(a):
    return np.ascontiguousarray(a.reshape(16, 128, -1).transpose(1, 0, 2))


def build_l1a(upto=99):
    p = Prog("l1a")
    x_i = p.inp("x", (T, D))
    wqd = p.inp("wqd", (D, D)); wkd = p.inp("wkd", (D, D)); wvd = p.inp("wvd", (D, D))
    wc_i = p.inp("wc", (D, D))
    wupq0 = p.inp("wupq0", (128, 64))
    wupq1 = p.inp("wupq1", (128, 64))
    wupk0 = p.inp("wupk0", (128, 64))
    wupk1 = p.inp("wupk1", (128, 64))
    wupv0 = p.inp("wupv0", (128, 64))
    wupv1 = p.inp("wupv1", (128, 64))
    cs_ev = p.inp("cs_ev", (128, 16, 16)); cs_od = p.inp("cs_od", (128, 16, 16))
    sn_ev = p.inp("sn_ev", (128, 16, 16)); sn_od = p.inp("sn_od", (128, 16, 16))
    identb = p.inp("identb", (128, 128), BF16)
    masktri = p.inp("masktri", (128, 128), F32)

    a1_o = p.outp("a1", (T, D), F32)
    xn_o = p.outp("xn_o", (NT, 128, D), F32)
    xnT_o = p.outp("xnT_o", (NCH, 128, T), BF16)

    alloc_ln(p, "l.")
    xnT = p.sb("xnT", [128, NCH, T], BF16)
    wqdb = p.sb("wqdb", [128, 8, 1024], BF16)
    wkdb = p.sb("wkdb", [128, 8, 1024], BF16)
    wvdb = p.sb("wvdb", [128, 8, 1024], BF16)
    wupqb = [p.sb(f"wupqb{i}", [128, 64], BF16) for i in range(2)]
    wupkb = [p.sb(f"wupkb{i}", [128, 64], BF16) for i in range(2)]
    wupvb = [p.sb(f"wupvb{i}", [128, 64], BF16) for i in range(2)]
    csod = {n: p.sb(n + "_s", [128, 16, 16], BF16) for n in ("cs_ev", "cs_od", "sn_ev", "sn_od")}
    idb = p.sb("ident", [128, 128], BF16)
    idf = p.sb("identf_s", [128, 128], F32)
    identf = p.inp("identf", (128, 128), F32)
    mtri = p.sb("mtri", [128, 128], F32)
    latq = p.sb("latq", [128, T], BF16)
    latk = p.sb("latk", [128, T], BF16)
    latv = p.sb("latv", [128, T], BF16)
    qtok = p.sb("qtok", [128, 16, 64], BF16)
    ktok = p.sb("ktok", [128, 16, 64], BF16)
    rt_ev = p.sb("rt_ev", [128, 16, 16], BF16)
    rt_od = p.sb("rt_od", [128, 16, 16], BF16)
    vtok = p.sb("vtok", [128, NT, 1024], BF16)
    yT = p.sb("yT", [128, NCH, T], BF16)
    qh2 = p.sb("qh2", [64, T], BF16)
    kh2 = p.sb("kh2", [64, T], BF16)
    sf = p.sb("sf", [128, T], F32)
    abf = p.sb("abf", [128, T], BF16)
    at = p.sb("at", [128, 16, 128], BF16)
    mx4 = p.sb("mx4", [128, 4], F32)
    lp4 = p.sb("lp4", [128, 4], F32)
    negm = p.sb("negm", [128, 1], F32)
    rec = p.sb("rec", [128, 1], F32)
    ps0 = p.ps("ps0", [128, 512], F32)
    ps1 = p.ps("ps1", [128, 512], F32)
    ptb = p.ps("ptb", [128, 1024], BF16)
    pyb = p.ps("pyb", [128, 512], F32)
    pq = p.ps("pq", [128, 512], F32)
    pv = p.ps("pv", [128, 512], F32)

    ld(p, "sp", idb[:], identb[:], w=("ident",))
    ld(p, "sp", idf[:], identf[:], w=("identf",))
    ld(p, "sp", mtri[:], masktri[:], w=("masktri",))
    for n in ("cs_ev", "cs_od", "sn_ev", "sn_od"):
        ld(p, "pool", csod[n][:, :, :], p.t[n][:, :, :], w=(n + "s",))
    for wb, wd, key in ((wqdb, wqd, "wqd"), (wkdb, wkd, "wkd"), (wvdb, wvd, "wvd")):
        ld(p, "pool", wb[:, :, :], wd.rearrange("(c p) o -> p c o", p=128), w=(key,))
    for i in range(2):
        for wb, wd, key in ((wupqb[i], p.t[f"wupq{i}"], "wupq"),
                            (wupkb[i], p.t[f"wupk{i}"], "wupk"),
                            (wupvb[i], p.t[f"wupv{i}"], "wupv")):
            ld(p, "pool", wb[:, :], wd[:, :], w=(key,))

    emit_ln_from_hbm_rot(p, x_i, "l.", xnT, pq, None, xn_store_f32=xn_o)
    for c in range(NCH):
        st(p, "act", xnT_o[c], xnT[:, c, :], r=XNT_KEYS, w=("xnT_o",))

    ROPE_T = tuple(n + "s" for n in ("cs_ev", "cs_od", "sn_ev", "sn_od"))

    def rope_inplace(p, tok, key):
        qe = tok[:, :, 32:64].rearrange("p t (d e) -> p t d e", e=2)
        ev = qe[:, :, :, 0:1]
        od = qe[:, :, :, 1:2]
        c_ev = csod["cs_ev"][:, :, :].rearrange("p t (d o) -> p t d o", o=1)
        c_od = csod["cs_od"][:, :, :].rearrange("p t (d o) -> p t d o", o=1)
        s_ev = csod["sn_ev"][:, :, :].rearrange("p t (d o) -> p t d o", o=1)
        s_od = csod["sn_od"][:, :, :].rearrange("p t (d o) -> p t d o", o=1)
        t_ev = rt_ev[:, :, :].rearrange("p t (d o) -> p t d o", o=1)
        t_od = rt_od[:, :, :].rearrange("p t (d o) -> p t d o", o=1)
        u_ev = p.t["f4u_ev"][:, :, :].rearrange("p t (d o) -> p t d o", o=1)
        u_od = p.t["f4u_od"][:, :, :].rearrange("p t (d o) -> p t d o", o=1)
        tt(p, t_ev, ev, c_ev, OP.mult, r=(key,) + ROPE_T, w=("rt_ev",))
        tt(p, u_ev, od, s_ev, OP.mult, r=(key,) + ROPE_T, w=("u_ev",))
        tt(p, t_ev, t_ev, u_ev, OP.subtract, r=("rt_ev", "u_ev"), w=("rt_ev",))
        tt(p, t_od, od, c_od, OP.mult, r=(key,) + ROPE_T, w=("rt_od",))
        tt(p, u_od, ev, s_od, OP.mult, r=(key,) + ROPE_T, w=("u_od",))
        tt(p, t_od, t_od, u_od, OP.add, r=("rt_od", "u_od"), w=("rt_od",))
        cp(p, "dve", ev, t_ev, r=("rt_ev",), w=(key,))
        cp(p, "dve", od, t_od, r=("rt_od",), w=(key,))

    p.sb("f4u_ev", [128, 16, 16], BF16)
    p.sb("f4u_od", [128, 16, 16], BF16)

    for hp in range(8 if upto >= 4 else (1 if upto >= 1 else 0)):
        if upto < 1:
            break
        for lat, wdk, key in ((latq, "wqd", "latq"), (latk, "wkd", "latk"), (latv, "wvd", "latv")):
            wb = p.t[wdk + "b"]
            for tc in range(4):
                mm(p, [(pq[:, 0:512], wb[:, c, hp * 128:(hp + 1) * 128],
                        xnT[:, c, tc * 512:(tc + 1) * 512], c == 0, c == 7)
                       for c in range(NCH)],
                   r=XNT_KEYS + (wdk,), w=("pq",), tag=f"lat{hp}")
                cp(p, "dve", lat[:, tc * 512:(tc + 1) * 512], pq[:, 0:512],
                   r=("pq",), w=(key,))
        for hi in range(2 if upto >= 2 else 0):
            h = hp * 2 + hi
            b0 = hi * 64
            for t_ in range(NT):
                mm(p, [(pv[:, 0:64], latq[:, t_ * 128:(t_ + 1) * 128],
                        wupqb[hi][:, 0:64], True, True),
                       (pv[:, 64:128], latk[:, t_ * 128:(t_ + 1) * 128],
                        wupkb[hi][:, 0:64], True, True),
                       (pv[:, 128:192], latv[:, t_ * 128:(t_ + 1) * 128],
                        wupvb[hi][:, 0:64], True, True)],
                   r=("latq", "latk", "latv", "wupq", "wupk", "wupv"),
                   w=("pv",), tag=f"up{h}.{t_}")
                act(p, qtok[:, t_, :], pv[:, 0:64], AF.Identity, scale=f32(0.125),
                    r=("pv",), w=("qtok", "pv"))
                cp(p, "dve", ktok[:, t_, :], pv[:, 64:128], r=("pv",), w=("ktok", "pv"))
                cp(p, "dve", vtok[:, t_, h * 64:(h + 1) * 64], pv[:, 128:192],
                   r=("pv",), w=("vtok", "pv"))
            if upto >= 3.5:
                rope_inplace(p, qtok, "qtok")
                rope_inplace(p, ktok, "ktok")
            for g in range(2 if upto >= 3 else 0):
                trans(p, [(ptb[0:64, j * 128:(j + 1) * 128], qtok[:, g * 8 + j, :])
                          for j in range(8)], idb, r=("qtok",), w=("ptb",))
                cp(p, "dve", qh2[0:64, g * 1024:(g + 1) * 1024], ptb[0:64, 0:1024],
                   r=("ptb",), w=("qh",))
                trans(p, [(ptb[0:64, j * 128:(j + 1) * 128], ktok[:, g * 8 + j, :])
                          for j in range(8)], idb, r=("ktok",), w=("ptb",))
                cp(p, "dve", kh2[0:64, g * 1024:(g + 1) * 1024], ptb[0:64, 0:1024],
                   r=("ptb",), w=("kh",))
            if upto >= 4:
                emit_causal_attn_head(p, h, qh2, kh2, vtok, yT, mtri)

    ld(p, "pool", wqdb[:, :, :], wc_i.rearrange("(c p) o -> p c o", p=128), w=("wqd",))
    for t_ in range(NT if upto >= 4 else 0):
        sl = t_ % 2
        for oc in range(2):
            mm(p, [(pv[:, 0:512], yT[:, ic, t_ * 128:(t_ + 1) * 128],
                    wqdb[:, ic, oc * 512:(oc + 1) * 512], ic == 0, ic == 7)
                   for ic in range(NCH)],
               r=("yT", "wqd"), w=("pv",), tag=f"wc{t_}.{oc}")
            cp(p, "dve", p.t["l.stg"][:, sl, oc * 512:(oc + 1) * 512], pv[:, 0:512],
               r=("pv",), w=(f"l.stg{sl}",))
        st(p, "act", a1_o.rearrange("(tt q) c -> tt q c", q=128)[t_],
           p.t["l.stg"][:, sl, :], r=(f"l.stg{sl}",), w=("a1_o",))
    if upto < 4:
        memset(p, p.t["l.stg"][:, 0, :], 0.0, w=("l.stg0",))
        for t_ in range(NT):
            st(p, "act", a1_o.rearrange("(tt q) c -> tt q c", q=128)[t_],
               p.t["l.stg"][:, 0, :], r=("l.stg0",), w=("a1_o",))
    return p


def stage_l1a(xb, L1):
    cos, sin = rope_tables(T, 32)
    csT = _tile_tok(cos)
    snT = _tile_tok(sin)
    def dup(a, half):
        z = np.zeros_like(a)
        parts = [a, z] if half == 0 else [z, a]
        return np.ascontiguousarray(np.concatenate(parts, 0), np.float32)
    return {
        "x": np.ascontiguousarray(xb, np.float32),
        "wqd": np.ascontiguousarray(L1["Wq_down"], np.float32),
        "wkd": np.ascontiguousarray(L1["Wk_down"], np.float32),
        "wvd": np.ascontiguousarray(L1["Wv_down"], np.float32),
        "wc": np.ascontiguousarray(L1["Wc"], np.float32),
        "wupq0": dup(np.concatenate([L1["Wq_up_c"], L1["Wq_up_e"]], 1), 0),
        "wupq1": dup(np.concatenate([L1["Wq_up_c"], L1["Wq_up_e"]], 1), 1),
        "wupk0": dup(np.concatenate([L1["Wk_up_c"], L1["Wk_up_e"]], 1), 0),
        "wupk1": dup(np.concatenate([L1["Wk_up_c"], L1["Wk_up_e"]], 1), 1),
        "wupv0": dup(L1["Wv_up"], 0),
        "wupv1": dup(L1["Wv_up"], 1),
        "cs_ev": np.ascontiguousarray(csT[:, :, 0::2]),
        "cs_od": np.ascontiguousarray(csT[:, :, 1::2]),
        "sn_ev": np.ascontiguousarray(snT[:, :, 0::2]),
        "sn_od": np.ascontiguousarray(snT[:, :, 1::2]),
        "identb": np.eye(128, dtype=ml_dtypes.bfloat16),
        "identf": np.eye(128, dtype=np.float32),
        "masktri": _tri_mask(),
    }


IB1 = 22


def build_l1f():
    p = Prog("l1f")
    x_i = p.inp("x", (T, D))
    a1_i = p.inp("a1", (T, D))
    xn_i = p.inp("xn_i", (NT, 128, D))
    wg_i = p.inp("wgN", (2816, D))
    wu_i = p.inp("wuN", (2816, D))
    wd_i = p.inp("wdN", (D, 2816))
    identb = p.inp("identb", (128, 128), BF16)
    x2_o = p.outp("x2", (T, D), F32)
    h_d = p.dscratch("h_d", (NT, 128, 2816), F32)

    alloc_fp4(p, 2816)
    xn = p.sb("xn", [128, 2, D], F32)
    qxn = p.sb("qxn", [128, 2, D], BF16)
    qxnT = p.sb("qxnT", [128, NCH, T], BF16)
    idb = p.sb("ident", [128, 128], BF16)
    wstg = p.sb("wstg", [128, 2, D], F32)
    wq4 = p.sb("wq4", [128, D], BF16)
    wqT = p.sb("wqT", [128, 2, NCH, 128], BF16)
    wu4T = p.sb("wu4T", [128, 2, NCH, 128], BF16)
    gs = p.sb("gs", [128, 128], F32)
    us = p.sb("us", [128, 128], F32)
    hs = p.sb("hs", [128, 128], F32)
    hstg = p.sb("hstg", [128, 2816], F32)
    qh = p.sb("qh", [128, 2816], BF16)
    qhT = p.sb("qhT", [128, IB1, 128], BF16)
    wdstg = p.sb("wdstg", [128, 2, 2816], F32)
    wd4 = p.sb("wd4", [128, 2816], BF16)
    wdT = p.sb("wdT", [128, IB1, 1024], BF16)
    pg = p.ps("pg", [128, 512], F32)
    pu = p.ps("pu", [128, 512], F32)
    ptb = p.ps("ptb", [128, 1024], BF16)
    pdn = p.ps("pdn", [128, 512], F32)

    ld(p, "sp", idb[:], identb[:], w=("ident",))
    for t_ in range(NT):
        sl = t_ % 2
        ld(p, "sp", xn[:, sl, :], xn_i[t_], w=(f"xn{sl}",))
        emit_fp4_qdq(p, qxn[:, sl, :], xn[:, sl, :], D, f"x{t_}", f"xn{sl}", f"qxn{sl}")
        trans(p, [(ptb[:, c * 128:(c + 1) * 128], qxn[:, sl, c * 128:(c + 1) * 128])
                  for c in range(NCH)], idb, r=(f"qxn{sl}",), w=("ptb",))
        cp(p, "dve", qxnT[:, :, t_ * 128:(t_ + 1) * 128],
           ptb[:, 0:1024].rearrange("p (c q) -> p c q", q=128),
           r=("ptb",), w=("qxnT",))
    wdv = wd_i.rearrange("(oc q) i -> oc q i", q=128)
    for oc in range(NCH):
        sl = oc % 2
        ld(p, "sp", wdstg[:, sl, :], wdv[oc], w=(f"wdstg{sl}",))
        emit_fp4_qdq(p, wd4[:, :], wdstg[:, sl, :], 2816, f"wd{oc}", f"wdstg{sl}", "wd4")
        for g in range(3):
            n = min(8, IB1 - g * 8)
            trans(p, [(ptb[:, j * 128:(j + 1) * 128],
                       wd4[:, (g * 8 + j) * 128:(g * 8 + j + 1) * 128]) for j in range(n)],
                  idb, r=("wd4",), w=("ptb",))
            cp(p, "dve", wdT[:, g * 8:g * 8 + n, oc * 128:(oc + 1) * 128],
               ptb[:, 0:n * 128].rearrange("p (c q) -> p c q", q=128),
               r=("ptb",), w=("wdT",))
    wgv = wg_i.rearrange("(ib q) c -> ib q c", q=128)
    wuv = wu_i.rearrange("(ib q) c -> ib q c", q=128)
    for ib in range(IB1):
        sl = ib % 2
        ld(p, "sp", wstg[:, sl, :], wgv[ib], w=(f"wstg{sl}",))
        emit_fp4_qdq(p, wq4[:, :], wstg[:, sl, :], D, f"wg{ib}", f"wstg{sl}", "wq4")
        trans(p, [(ptb[:, c * 128:(c + 1) * 128], wq4[:, c * 128:(c + 1) * 128])
                  for c in range(NCH)], idb, r=("wq4",), w=("ptb",))
        cp(p, "dve", wqT[:, sl, :, :], ptb[:, 0:1024].rearrange("p (c q) -> p c q", q=128),
           r=("ptb",), w=(f"wqT{sl}",))
        ld(p, "sp", wstg[:, sl, :], wuv[ib], w=(f"wstg{sl}",))
        emit_fp4_qdq(p, wq4[:, :], wstg[:, sl, :], D, f"wu{ib}", f"wstg{sl}", "wq4")
        trans(p, [(ptb[:, c * 128:(c + 1) * 128], wq4[:, c * 128:(c + 1) * 128])
                  for c in range(NCH)], idb, r=("wq4",), w=("ptb",))
        cp(p, "dve", wu4T[:, sl, :, :], ptb[:, 0:1024].rearrange("p (c q) -> p c q", q=128),
           r=("ptb",), w=(f"wu4T{sl}",))
        for t_ in range(NT):
            mm(p, [(pg[:, 0:128], qxnT[:, c, t_ * 128:(t_ + 1) * 128], wqT[:, sl, c, :],
                    c == 0, c == 7) for c in range(NCH)],
               r=(f"wqT{sl}", "qxnT"), w=("pg",), tag=f"g{ib}.{t_}")
            mm(p, [(pu[:, 0:128], qxnT[:, c, t_ * 128:(t_ + 1) * 128], wu4T[:, sl, c, :],
                    c == 0, c == 7) for c in range(NCH)],
               r=(f"wu4T{sl}", "qxnT"), w=("pu",), tag=f"u{ib}.{t_}")
            act(p, gs[:, :], pg[:, 0:128], AF.Silu, r=("pg",), w=("gs", "pg"))
            cp(p, "dve", us[:, :], pu[:, 0:128], r=("pu",), w=("us", "pu"))
            tt(p, hs[:, :], gs[:, :], us[:, :], OP.mult, r=("gs", "us"), w=("hs",))
            st(p, "act", h_d[t_, :, ib * 128:(ib + 1) * 128],
               hs[:, :], r=("hs",), w=("h_d",))
    hv = h_d
    xv = x_i.rearrange("(tt q) c -> tt q c", q=128)
    av = a1_i.rearrange("(tt q) c -> tt q c", q=128)
    x2v = x2_o.rearrange("(tt q) c -> tt q c", q=128)
    for t_ in range(NT):
        sl = t_ % 2
        ld(p, "sp", hstg[:, :], hv[t_], w=("hstg",), r=("h_d",))
        emit_fp4_qdq(p, qh[:, :], hstg[:, :], 2816, f"h{t_}", "hstg", "qh")
        for g in range(3):
            n = min(8, IB1 - g * 8)
            trans(p, [(ptb[:, j * 128:(j + 1) * 128],
                       qh[:, (g * 8 + j) * 128:(g * 8 + j + 1) * 128]) for j in range(n)],
                  idb, r=("qh",), w=("ptb",))
            cp(p, "dve", qhT[:, g * 8:g * 8 + n, :],
               ptb[:, 0:n * 128].rearrange("p (c q) -> p c q", q=128),
               r=("ptb",), w=("qhT",))
        xst = wdstg[:, sl, 0:1024]
        ast = wdstg[:, sl, 1024:2048]
        kst = f"wdstg{sl}"
        ld(p, "sp", xst, xv[t_], w=(kst,))
        ld(p, "sp", ast, av[t_], w=(kst,))
        for oc in range(2):
            mm(p, [(pdn[:, 0:512], qhT[:, ib, :], wdT[:, ib, oc * 512:(oc + 1) * 512],
                    ib == 0, ib == IB1 - 1) for ib in range(IB1)],
               r=("qhT", "wdT"), w=("pdn",), tag=f"dn{t_}.{oc}")
            tt(p, xst[:, oc * 512:(oc + 1) * 512], pdn[:, 0:512],
               xst[:, oc * 512:(oc + 1) * 512], OP.add,
               r=("pdn", kst), w=(kst,))
        tt(p, xst, xst, ast, OP.add, r=(kst,), w=(kst,))
        st(p, "act", x2v[t_], xst, r=(kst,), w=("x2_o",))
    return p


def stage_l1f(xb, a1, xn, L1f):
    return {"x": xb, "a1": a1, "xn_i": xn,
            "wgN": np.ascontiguousarray(L1f["w_gate"], np.float32),
            "wuN": np.ascontiguousarray(L1f["w_up"], np.float32),
            "wdN": np.ascontiguousarray(L1f["w_down"], np.float32),
            "identb": np.eye(128, dtype=ml_dtypes.bfloat16)}



def build_lf1():
    p = Prog("lf1")
    hT_i = p.inp("hT_i", (NCH, 128, T), BF16)
    h_i = p.inp("h_i", (T, D))
    wqkvT = p.inp("wqkvT", (D, 3088))
    woT = p.inp("woT", (D, D))
    lng_bc = p.inp("lng_bc", (128, D))
    lnb_bc = p.inp("lnb_bc", (128, D))
    at_o = p.outp("at2", (T, D), F32)

    xnT = p.sb("xnT", [128, NCH, T], BF16)
    wqh = p.sb("wqh", [128, 8, 208], BF16)
    qkb = p.sb("qkb", [128, 16, 208], BF16)
    x2b = p.sb("x2b", [128, 16, 128], F32)
    hq3 = p.sb("hq3", [128, 16, 256], F32)
    hk3 = p.sb("hk3", [128, 16, 256], F32)
    dacc = p.sb("dacc", [128, 16, 256], F32)
    hv = p.sb("hv", [128, 16, 64], BF16)
    hb = p.sb("hb", [128, 16, 1], F32)
    kd = p.sb("kd", [128, 16, 1], F32)
    den = p.sb("den", [128, 16, 1], F32)
    rtmp = p.sb("rtmp", [128, 16, 1], F32)
    amat = p.sb("amat", [128, 2, 8, 8], F32)
    cmat = p.sb("cmat", [128, 2, 8, 8], F32)
    nv = p.sb("nv", [128, 2, 8, 64], F32)
    ttp = p.sb("ttp", [128, 256], F32)
    acc1 = p.sb("acc1", [128, 64], F32)
    acc2 = p.sb("acc2", [128, 64], F32)
    lo = p.sb("lo", [128, 16, 1024], BF16)
    loT = p.sb("loT", [128, NCH, T], BF16)
    wob = p.sb("wob", [128, 8, 1024], BF16)
    lngb = p.sb("lngb", [128, D], F32)
    lnbb = p.sb("lnbb", [128, D], F32)
    hstg = p.sb("hstg", [128, 2, D], F32)
    idb = p.sb("ident", [128, 128], BF16)
    identb = p.inp("identb", (128, 128), BF16)
    for nm, shp in (("l.bn", [128, 12]), ("l.mv", [128, 2]), ("l.sd", [128, 1]),
                    ("l.rstd", [128, 1]), ("l.ngm", [128, 1])):
        p.sb(nm, shp, F32)
    pq = p.ps("pq", [128, 512], F32)
    pv = p.ps("pv", [128, 512], F32)
    ptb = p.ps("ptb", [128, 1024], BF16)

    ld(p, "sp", idb[:], identb[:], w=("ident",))
    ld(p, "pool", xnT[:, :, :], hT_i.rearrange("c p t -> p c t"), w=("xnTl",))
    ld(p, "pool", wob[:, :, :], woT.rearrange("(c p) o -> p c o", p=128), w=("wo",))
    ld(p, "sp", lngb[:], lng_bc[:], w=("lng",))
    ld(p, "sp", lnbb[:], lnb_bc[:], w=("lnb",))

    EPS = 1e-5

    def ttr(dst, in0, in1, rr, ww):
        p.op("dve", lambda e, d=dst, a=in0, b=in1:
             e.tensor_tensor_reduce(d, a, b, 1.0, 0.0, OP.mult, OP.add, accum_out=None)
             if False else e.tensor_tensor_scan(d, a, b, 0.0, OP.bypass, OP.bypass),
             r=rr, w=ww)

    for h in range(16):
        ld(p, "pool", wqh[:, :, 0:193],
           wqkvT.rearrange("(c p) o -> p c o", p=128)[:, :, h * 193:(h + 1) * 193],
           w=("wqh",))
        for t_ in range(NT):
            mm(p, [(pq[:, 0:193], xnT[:, c, t_ * 128:(t_ + 1) * 128], wqh[:, c, 0:193],
                    c == 0, c == 7) for c in range(NCH)],
               r=("wqh", "xnTl"), w=("pq",), tag=f"qkb{h}.{t_}")
            cp(p, "dve", qkb[:, t_, 0:193], pq[:, 0:193], r=("pq",), w=("qkb", "pq"))
        act(p, hb[:, :, :], qkb[:, :, 192:193], AF.Sigmoid, r=("qkb",), w=("hb",))
        for (dst, src0) in ((hq3, 0), (hk3, 64)):
            k = "hq3" if src0 == 0 else "hk3"
            act(p, x2b[:, :, 0:64], qkb[:, :, src0:src0 + 64], AF.Relu,
                r=("qkb",), w=("x2b",))
            act(p, x2b[:, :, 64:128], qkb[:, :, src0:src0 + 64], AF.Relu,
                scale=f32(-1.0), r=("qkb",), w=("x2b",))
            for i_ in (1, 2):
                o0 = 128 * (i_ - 1)
                tt(p, dst[:, :, o0 + i_:o0 + 128], x2b[:, :, i_:128],
                   x2b[:, :, 0:128 - i_], OP.mult, r=("x2b",), w=(k,))
                tt(p, dst[:, :, o0:o0 + i_], x2b[:, :, 0:i_],
                   x2b[:, :, 128 - i_:128], OP.mult, r=("x2b",), w=(k,))
            red(p, rtmp[:, :, :], dst[:, :, :], OP.add, r=(k,), w=("rtmp",))
            p.op("dve", lambda e, d=rtmp[:, :, :], s=rtmp[:, :, :]: e.reciprocal(d, s),
                 r=("rtmp",), w=("rtmp",))
            tt(p, dst[:, :, :], dst[:, :, :],
               rtmp[:, :, :].broadcast_to((128, 16, 256)), OP.mult,
               r=(k, "rtmp"), w=(k,))
        cp(p, "dve", hv[:, :, :], qkb[:, :, 128:192], r=("qkb",), w=("hv",))
        for bh in range(2):
            cp(p, "dve", dacc[:, bh, :].rearrange("p (a b) -> p a b", b=256),
               hk3[:, bh, :].rearrange("p (a b) -> p a b", b=256), r=("hk3",), w=("dacc",))
        for s in range(1, 8):
            for bh in range(2):
                t_ = s * 2 + bh
                tt(p, dacc[:, t_, :], dacc[:, t_ - 2, :], hk3[:, t_, :], OP.add,
                   r=("dacc", "hk3"), w=("dacc",))
        memset(p, kd[:, 0:2, :], 1.0, w=("kd",))
        for s in range(1, 8):
            for bh in range(2):
                t_ = s * 2 + bh
                tt(p, ttp[:, 0:256], dacc[:, t_ - 2, :], hk3[:, t_, :], OP.mult,
                   r=("dacc", "hk3"), w=("ttp",))
                red(p, kd[:, t_, :], ttp[:, 0:256], OP.add, r=("ttp",), w=("kd",))
        tt(p, hb[:, :, :], hb[:, :, :], kd[:, :, :], OP.mult, r=("hb", "kd"), w=("hb",))
        ts(p, rtmp[:, :, :], kd[:, :, :], f32(EPS), OP.add, r=("kd",), w=("rtmp",))
        p.op("dve", lambda e, d=rtmp[:, :, :], s=rtmp[:, :, :]: e.reciprocal(d, s),
             r=("rtmp",), w=("rtmp",))
        tt(p, hk3[:, :, :], hk3[:, :, :],
           rtmp[:, :, :].broadcast_to((128, 16, 256)), OP.mult,
           r=("hk3", "rtmp"), w=("hk3",))
        for t_ in range(NT):
            tt(p, ttp[:, 0:256], dacc[:, t_, :], hq3[:, t_, :], OP.mult,
               r=("dacc", "hq3"), w=("ttp",))
            red(p, den[:, t_, :], ttp[:, 0:256], OP.add, r=("ttp",), w=("den",))
        for bh in range(2):
            for s in range(8):
                for sp_ in range(s + 1):
                    if sp_ < s:
                        tt(p, ttp[:, 0:256], hk3[:, s * 2 + bh, :],
                           hk3[:, sp_ * 2 + bh, :], OP.mult, r=("hk3",), w=("ttp",))
                        red(p, amat[:, bh, s, sp_:sp_ + 1], ttp[:, 0:256], OP.add,
                            r=("ttp",), w=("amat",))
                    tt(p, ttp[:, 0:256], hq3[:, s * 2 + bh, :],
                       hk3[:, sp_ * 2 + bh, :], OP.mult, r=("hk3", "hq3"), w=("ttp",))
                    red(p, cmat[:, bh, s, sp_:sp_ + 1], ttp[:, 0:256], OP.add,
                        r=("ttp",), w=("cmat",))
        for bh in range(2):
            for s in range(8):
                t_ = s * 2 + bh
                if s == 0:
                    tt(p, nv[:, bh, 0, :], hv[:, t_, :],
                       hb[:, t_, :].broadcast_to((128, 64)), OP.mult,
                       r=("hv", "hb"), w=("nv",))
                    continue
                ts(p, acc1[:, :], nv[:, bh, 0, :], amat[:, bh, s, 0:1], OP.mult,
                   r=("nv", "amat"), w=("acc1",))
                cur = 0
                for sp_ in range(1, s):
                    src = (acc1, acc2)[cur]
                    dst = (acc1, acc2)[1 - cur]
                    stt(p, dst[:, :], nv[:, bh, sp_, :], amat[:, bh, s, sp_:sp_ + 1],
                        src[:, :], OP.mult, OP.add,
                        r=("nv", "amat", ("acc1", "acc2")[cur]),
                        w=(("acc1", "acc2")[1 - cur],))
                    cur = 1 - cur
                fin = ("acc1", "acc2")[cur]
                tt(p, (acc2, acc1)[cur][:, :], hv[:, t_, :], (acc1, acc2)[cur][:, :],
                   OP.subtract, r=("hv", fin), w=(("acc2", "acc1")[cur],))
                ts(p, nv[:, bh, s, :], (acc2, acc1)[cur][:, :], hb[:, t_, :], OP.mult,
                   r=(("acc2", "acc1")[cur], "hb"), w=("nv",))
        for bh in range(2):
            for s in range(8):
                t_ = s * 2 + bh
                ts(p, acc1[:, :], nv[:, bh, 0, :], cmat[:, bh, s, 0:1], OP.mult,
                   r=("nv", "cmat"), w=("acc1",))
                cur = 0
                for sp_ in range(1, s + 1):
                    src = (acc1, acc2)[cur]
                    dst = (acc1, acc2)[1 - cur]
                    stt(p, dst[:, :], nv[:, bh, sp_, :], cmat[:, bh, s, sp_:sp_ + 1],
                        src[:, :], OP.mult, OP.add,
                        r=("nv", "cmat", ("acc1", "acc2")[cur]),
                        w=(("acc1", "acc2")[1 - cur],))
                    cur = 1 - cur
                fin = ("acc1", "acc2")[cur]
                ts(p, rtmp[:, t_, :], den[:, t_, :], f32(EPS), OP.add,
                   r=("den",), w=("rtmp",))
                p.op("dve", lambda e, d=rtmp[:, t_, :], s2=rtmp[:, t_, :]:
                     e.reciprocal(d, s2), r=("rtmp",), w=("rtmp",))
                ts(p, lo[:, t_, h * 64:(h + 1) * 64], (acc1, acc2)[cur][:, :],
                   rtmp[:, t_, :], OP.mult, f32(0.125), OP.mult,
                   r=(fin, "rtmp"), w=("lo",))
    for t_ in range(NT):
        trans(p, [(ptb[:, c * 128:(c + 1) * 128], lo[:, t_, c * 128:(c + 1) * 128])
                  for c in range(NCH)], idb, r=("lo",), w=("ptb",))
        cp(p, "dve", loT[:, :, t_ * 128:(t_ + 1) * 128],
           ptb[:, 0:1024].rearrange("p (c q) -> p c q", q=128),
           r=("ptb",), w=("loT",))
    hview = h_i.rearrange("(tt q) c -> tt q c", q=128)
    oview = at_o.rearrange("(tt q) c -> tt q c", q=128)
    bnl = p.t["l.bn"]; mvl = p.t["l.mv"]; sdl = p.t["l.sd"]
    rstdl = p.t["l.rstd"]; ngml = p.t["l.ngm"]
    for t_ in range(NT):
        sl = t_ % 2
        ld(p, "sp", hstg[:, sl, :], hview[t_], w=(f"hstg{sl}",))
        for oc in range(2):
            mm(p, [(pv[:, 0:512], loT[:, ic, t_ * 128:(t_ + 1) * 128],
                    wob[:, ic, oc * 512:(oc + 1) * 512], ic == 0, ic == 7)
                   for ic in range(NCH)],
               r=("loT", "wo"), w=("pv",), tag=f"wo{t_}.{oc}")
            tt(p, hstg[:, sl, oc * 512:(oc + 1) * 512], pv[:, 0:512],
               hstg[:, sl, oc * 512:(oc + 1) * 512], OP.add,
               r=("pv", f"hstg{sl}"), w=(f"hstg{sl}",))
        p.op("dve", lambda e, s=hstg[:, sl, 0:512], d=bnl[:, 0:6]: e.bn_stats(d, s),
             r=(f"hstg{sl}",), w=("l.bn",))
        p.op("dve", lambda e, s=hstg[:, sl, 512:1024], d=bnl[:, 6:12]: e.bn_stats(d, s),
             r=(f"hstg{sl}",), w=("l.bn",))
        p.op("dve", lambda e, s=bnl[:, 0:12], d=mvl[:, 0:2]: e.bn_aggr(d, s),
             r=("l.bn",), w=("l.mv",))
        act(p, sdl[:, 0:1], mvl[:, 1:2], AF.Sqrt, bias=f32(1e-5),
            r=("l.mv",), w=("l.sd",))
        p.op("dve", lambda e, d=rstdl[:, 0:1], s=sdl[:, 0:1]: e.reciprocal(d, s),
             r=("l.sd",), w=("l.rstd",))
        stt(p, ngml[:, 0:1], mvl[:, 0:1], f32(-1.0), rstdl[:, 0:1],
            OP.mult, OP.mult, r=("l.mv", "l.rstd"), w=("l.ngm",))
        act(p, hstg[:, sl, :], hstg[:, sl, :], AF.Identity,
            bias=ngml[:, 0:1], scale=rstdl[:, 0:1],
            r=(f"hstg{sl}", "l.rstd", "l.ngm"), w=(f"hstg{sl}",))
        tt(p, hstg[:, sl, :], hstg[:, sl, :], lngb[:, 0:D], OP.mult,
           r=(f"hstg{sl}", "lng"), w=(f"hstg{sl}",))
        tt(p, hstg[:, sl, :], hstg[:, sl, :], lnbb[:, 0:D], OP.add,
           r=(f"hstg{sl}", "lnb"), w=(f"hstg{sl}",))
        st(p, "act", oview[t_], hstg[:, sl, :], r=(f"hstg{sl}",), w=("at_o",))
    return p


def stage_lf1(hT_bsz_bf16, h_bsz_f32, L2a):
    return {
        "hT_i": hT_bsz_bf16, "h_i": h_bsz_f32,
        "wqkvT": np.ascontiguousarray(L2a["W_qkvb"].T, np.float32),
        "woT": np.ascontiguousarray(L2a["W_o"].T, np.float32),
        "lng_bc": np.ascontiguousarray(np.broadcast_to(L2a["ln_g"], (128, D)), np.float32),
        "lnb_bc": np.ascontiguousarray(np.broadcast_to(L2a["ln_b"], (128, D)), np.float32),
        "identb": np.eye(128, dtype=ml_dtypes.bfloat16),
    }



HID2 = 2730
NJ2 = 22


def _cj2(j):
    return 128 if j < 21 else 42


def build_lf2():
    p = Prog("lf2")
    xnT_i = p.inp("xnT_i", (NCH, 128, T + 1), BF16)
    xn_i = p.inp("xn_i", (T, D))
    wqkv_i = [p.inp(f"wqkv{l}", (D, 1536)) for l in range(2)]
    wo_i = [p.inp(f"wo{l}", (512, D)) for l in range(2)]
    w1_i = [p.inp(f"w1{l}", (D, 2 * HID2)) for l in range(2)]
    w2_i = [p.inp(f"w2{l}", (HID2, D)) for l in range(2)]
    b1a_s = [p.inp(f"b1a{l}", (128, NJ2)) for l in range(2)]
    b1g_s = [p.inp(f"b1g{l}", (128, NJ2)) for l in range(2)]
    b2_bc = [p.inp(f"b2bc{l}", (128, D)) for l in range(2)]
    lg_s = [p.inp(f"lg{l}", (128, 8)) for l in range(2)]
    lb_s = [p.inp(f"lb{l}", (128, 8)) for l in range(2)]
    qs_bc = [p.inp(f"qsbc{l}", (128, 64)) for l in range(2)]
    ks_bc = [p.inp(f"ksbc{l}", (128, 64)) for l in range(2)]
    csq = p.inp("csq", (128, 16, 64)); snq = p.inp("snq", (128, 16, 64))
    csk = p.inp("csk", (128, 16, 64)); snk = p.inp("snk", (128, 16, 64))
    identb = p.inp("identb", (128, 128), BF16)
    masktri = p.inp("masktri", (128, 128), F32)
    masklo = p.inp("masklo", (128, 128), F32)
    ff2_o = p.outp("ff2", (T, D), F32)
    x_d = p.dscratch("x_d", (T, D), F32)
    h2_d = p.dscratch("h2_d", (NJ2, 128, T), BF16)

    xTz = p.sb("xTz", [128, NCH, T + 1], BF16)
    big = p.sb("big", [128, 16, 1536], BF16)
    lnhT = big.ap().rearrange("p a b -> p (a b)")[:, 0:NCH * T].rearrange(
        "p (c t) -> p c t", t=T)
    w2b = p.sb("w2b", [128, NJ2, D], BF16)
    yT = p.sb("yT", [128, 4, T], BF16)
    wob = p.sb("wob", [128, 4, D], BF16)
    w1a = p.sb("w1a", [128, 2, 8, 128], BF16)
    w1g = p.sb("w1g", [128, 2, 8, 128], BF16)
    idb = p.sb("ident", [128, 128], BF16)
    mtri = p.sb("mtri", [128, 128], F32)
    mlo = p.sb("mlo", [128, 128], F32)
    csqb = p.sb("csqb", [128, 16, 64], BF16); snqb = p.sb("snqb", [128, 16, 64], BF16)
    cskb = p.sb("cskb", [128, 16, 64], BF16); snkb = p.sb("snkb", [128, 16, 64], BF16)
    qsb = p.sb("qsb", [128, 64], BF16); ksb = p.sb("ksb", [128, 64], BF16)
    lgb = p.sb("lgb", [128, 8], F32); lbb = p.sb("lbb", [128, 8], F32)
    b1ab = p.sb("b1ab", [128, NJ2], F32); b1gb = p.sb("b1gb", [128, NJ2], F32)
    b2b = p.sb("b2b", [128, D], F32)
    nrm = p.sb("nrm", [128, 16, 1], F32)
    ttq = p.sb("ttq", [128, 16, 64], BF16)
    rt1 = p.sb("rt1", [128, 16, 32], BF16)
    rt2 = p.sb("rt2", [128, 16, 32], BF16)
    qh2 = p.sb("qh2", [64, T], BF16)
    kh2 = p.sb("kh2", [64, T], BF16)
    sf = p.sb("sf", [128, 1024], F32)
    abf = p.sb("abf", [128, 1024], BF16)
    at = p.sb("at", [128, 8, 128], BF16)
    mx4 = p.sb("mx4", [128, 2], F32)
    lp4 = p.sb("lp4", [128, 2], F32)
    negm = p.sb("negm", [128, 1], F32)
    rec = p.sb("rec", [128, 1], F32)
    xstg = p.sb("xstg", [128, 2, D], F32)
    xbstg = p.sb("xbstg", [128, D], BF16)
    sqst = p.sb("sqst", [128, 512], F32)
    ast2 = p.sb("ast2", [128, 512], F32)
    gst2 = p.sb("gst2", [128, 512], F32)
    h2j = p.sb("h2j", [128, 512], BF16)
    rows = big.ap().rearrange("p a b -> p (a b)").bitcast(F32)[0:1, 8192:10240]
    ones1 = p.sb("ones1", [1, 128], F32)
    cmeanb = p.sb("cmeanb", [128, 1], BF16)
    cmeanf = p.sb("cmeanf", [128, 1], F32)
    h2slc = yT.ap().rearrange("p a t -> p (a t)")[:, 0:NJ2 * 128].rearrange(
        "p (j q) -> p j q", q=128)
    ps0 = p.ps("ps0", [128, 512], F32)
    ps1 = p.ps("ps1", [128, 512], F32)
    ptb = p.ps("ptb", [128, 1024], BF16)
    pyb = p.ps("pyb", [128, 512], F32)
    pq = p.ps("pq", [128, 512], F32)
    pr = p.ps("pr", [128, 512], F32)
    pn = p.ps("pn", [128, 512], F32)
    pg2 = p.ps("pg2", [128, 512], F32)

    ld(p, "sp", idb[:], identb[:], w=("ident",))
    ld(p, "sp", mtri[:], masktri[:], w=("masktri",))
    ld(p, "sp", mlo[:], masklo[:], w=("masklo",))
    for dst, srcn in ((csqb, "csq"), (snqb, "snq"), (cskb, "csk"), (snkb, "snk")):
        ld(p, "pool", dst[:, :, :], p.t[srcn][:, :, :], w=(srcn + "s",))
    memset(p, ones1[0:1, 0:128], 1.0, w=("ones1",))
    memset(p, cmeanb[:, 0:1], 1.0 / 1024.0, w=("cmb",))
    memset(p, cmeanf[:, 0:1], 1.0 / 1024.0, w=("cmf",))
    memset(p, xTz[:, :, 0:1], 0.0, w=("xTz0",))
    memset(p, sqst[:, 0:512], 0.0, w=("sqst",))
    for tc4 in range(4):
        st(p, "pool", h2_d[NJ2 - 1, 42:128, tc4 * 512:(tc4 + 1) * 512],
           sqst[0:86, 0:512], r=("sqst",), w=("h2_d",))
    ld(p, "pool", xTz[:, :, 1:T + 1], xnT_i.rearrange("c p t -> p c t")[:, :, 1:T + 1],
       w=("xTzl",))
    st(p, "sp", x_d[:, :], xn_i[:, :], r=(), w=("x_d",))

    XK = ("xTz0", "xTzl")

    def shifted(c, t0, width=128):
        off = 0 if c < 4 else -1
        return xTz[:, c, 1 + t0 + off: 1 + t0 + off + width]

    for l in range(2):
        wqkvb = w2b.ap().rearrange("p j o -> p (j o)")[:, 0:8 * 1536].rearrange("p (j o) -> p j o", o=1536)
        ld(p, "pool", wqkvb, wqkv_i[l].rearrange("(c p) o -> p c o", p=128), w=("w2b",))
        ld(p, "pool", wob[:, :, :], wo_i[l].rearrange("(c p) o -> p c o", p=128), w=("wo",))
        ld(p, "pool", qsb[:, :], qs_bc[l][:, :], w=("qs",))
        ld(p, "pool", ksb[:, :], ks_bc[l][:, :], w=("ks",))
        ld(p, "sp", lgb[:, :], lg_s[l][:, :], w=("lg",))
        ld(p, "sp", lbb[:, :], lb_s[l][:, :], w=("lb",))
        ld(p, "sp", b1ab[:, :], b1a_s[l][:, :], w=("b1a",))
        ld(p, "sp", b1gb[:, :], b1g_s[l][:, :], w=("b1g",))
        ld(p, "sp", b2b[:, :], b2_bc[l][:, :], w=("b2",))
        for t_ in range(NT):
            for oc in range(3):
                mm(p, [(pq[:, 0:512], shifted(c, t_ * 128),
                        wqkvb[:, c, oc * 512:(oc + 1) * 512], c == 0, c == 7)
                       for c in range(NCH)],
                   r=XK + ("w2b",), w=("pq",), tag=f"qkv{l}.{t_}.{oc}")
                cp(p, "dve", big[:, t_, oc * 512:(oc + 1) * 512], pq[:, 0:512],
                   r=("pq",), w=("big",))
        for h in range(8):
            qsl = big[:, :, h * 64:(h + 1) * 64]
            ksl = big[:, :, 512 + h * 64:512 + (h + 1) * 64]
            for (sl, key, scb, csb, snb) in ((qsl, "q", qsb, csqb, snqb),
                                             (ksl, "k", ksb, cskb, snkb)):
                tt(p, ttq[:, :, :], sl, sl, OP.mult, r=("big",), w=("ttq",))
                red(p, nrm[:, :, :], ttq[:, :, :], OP.add, r=("ttq",), w=("nrm",))
                act(p, nrm[:, :, :], nrm[:, :, :], AF.Sqrt, r=("nrm",), w=("nrm",))
                ts(p, nrm[:, :, :], nrm[:, :, :], f32(1e-12), OP.max, r=("nrm",), w=("nrm",))
                p.op("dve", lambda e, d=nrm[:, :, :], s=nrm[:, :, :]: e.reciprocal(d, s),
                     r=("nrm",), w=("nrm",))
                tt(p, sl, sl, nrm[:, :, :].broadcast_to((128, 16, 64)), OP.mult,
                   r=("big", "nrm"), w=("big",))
                tt(p, sl, sl, scb[:, :].rearrange("p (a d) -> p a d", a=1)
                   .broadcast_to((128, 16, 64)), OP.mult,
                   r=("big", "qs", "ks"), w=("big",))
                x1 = sl[:, :, 0:32]; x2 = sl[:, :, 32:64]
                tt(p, rt1[:, :, :], x1, csb[:, :, 0:32], OP.mult,
                   r=("big", "csqs", "snqs", "csks", "snks"), w=("rt1",))
                tt(p, rt2[:, :, :], x2, snb[:, :, 0:32], OP.mult,
                   r=("big", "csqs", "snqs", "csks", "snks"), w=("rt2",))
                tt(p, rt1[:, :, :], rt1[:, :, :], rt2[:, :, :], OP.subtract,
                   r=("rt1", "rt2"), w=("rt1",))
                tt(p, rt2[:, :, :], x2, csb[:, :, 32:64], OP.mult,
                   r=("big",), w=("rt2",))
                tt(p, ttq[:, :, 0:32], x1, snb[:, :, 32:64], OP.mult,
                   r=("big",), w=("ttq",))
                tt(p, rt2[:, :, :], rt2[:, :, :], ttq[:, :, 0:32], OP.add,
                   r=("rt2", "ttq"), w=("rt2",))
                cp(p, "dve", x1, rt1[:, :, :], r=("rt1",), w=("big",))
                cp(p, "dve", x2, rt2[:, :, :], r=("rt2",), w=("big",))
            for g in range(2):
                trans(p, [(ptb[0:64, j * 128:(j + 1) * 128], qsl[:, g * 8 + j, :])
                          for j in range(8)], idb, r=("big",), w=("ptb",))
                ts(p, qh2[0:64, g * 1024:(g + 1) * 1024], ptb[0:64, 0:1024],
                   f32(8.0), OP.mult, r=("ptb",), w=("qh", "ptb"))
                trans(p, [(ptb[0:64, j * 128:(j + 1) * 128], ksl[:, g * 8 + j, :])
                          for j in range(8)], idb, r=("big", "qh"), w=("ptb",))
                cp(p, "dve", kh2[0:64, g * 1024:(g + 1) * 1024], ptb[0:64, 0:1024],
                   r=("ptb",), w=("kh", "ptb"))
            for wi in range(4):
                for qt in range(4):
                    t_ = wi * 4 + qt
                    ktiles = []
                    if wi > 0:
                        ktiles += [((wi - 1) * 4 + ck, "lo" if ck == qt else None)
                                   for ck in range(qt, 4)]
                    ktiles += [(wi * 4 + ck, "tri" if ck == qt else None)
                               for ck in range(qt + 1)]
                    nk = len(ktiles)
                    for ki, (kt, mk) in enumerate(ktiles):
                        b = ki % 2
                        psb = (ps0, ps1)[b]
                        pk = f"ps{b}"
                        mm(p, [(psb[0:128, 0:128], qh2[0:64, t_ * 128:(t_ + 1) * 128],
                                kh2[0:64, kt * 128:(kt + 1) * 128], True, True)],
                           r=("qh", "kh"), w=(pk,), tag=f"w{l}s{h}.{t_}.{ki}")
                        if mk == "tri":
                            tt(p, psb[0:128, 0:128], psb[0:128, 0:128], mtri[:, :],
                               OP.add, r=(pk, "masktri"), w=(pk,))
                        elif mk == "lo":
                            tt(p, psb[0:128, 0:128], psb[0:128, 0:128], mlo[:, :],
                               OP.add, r=(pk, "masklo"), w=(pk,))
                        red(p, mx4[:, ki % 2:ki % 2 + 1] if False else mx4[:, 0:1] if ki == 0 else mx4[:, 1:2],
                            psb[0:128, 0:128], OP.max, r=(pk,), w=("mx4", pk))
                        if ki > 0:
                            red(p, mx4[:, 0:1], mx4[:, 0:2], OP.max, r=("mx4",), w=("mx4",))
                        cp(p, "dve", sf[:, ki * 128:(ki + 1) * 128], psb[0:128, 0:128],
                           r=(pk,), w=("sf",))
                    ts(p, negm[:, 0:1], mx4[:, 0:1], f32(-1.0), OP.mult,
                       r=("mx4",), w=("negm",))
                    act(p, abf[:, 0:nk * 128], sf[:, 0:nk * 128], AF.Exp,
                        bias=negm[:, 0:1], accum=lp4[:, 0:1],
                        r=("sf", "negm"), w=("abf", "lp4"))
                    p.op("dve", lambda e, d=rec[:, 0:1], s=lp4[:, 0:1]: e.reciprocal(d, s),
                         r=("lp4",), w=("rec",))
                    ts(p, abf[:, 0:nk * 128], abf[:, 0:nk * 128], rec[:, 0:1], OP.mult,
                       r=("abf", "rec"), w=("abf",))
                    trans(p, [(ptb[:, ki * 128:(ki + 1) * 128],
                               abf[:, ki * 128:(ki + 1) * 128]) for ki in range(nk)],
                          idb, r=("abf",), w=("ptb",))
                    cp(p, "dve", at[:, 0:nk, :],
                       ptb[:, 0:nk * 128].rearrange("p (c q) -> p c q", q=128),
                       r=("ptb",), w=("at", "ptb"))
                    mm(p, [(pyb[0:64, 0:128], big[:, kt, 1024 + h * 64:1024 + (h + 1) * 64],
                            at[:, ki, :], ki == 0, ki == nk - 1)
                           for ki, (kt, mk) in enumerate(ktiles)],
                       r=("big", "at"), w=("pyb",), tag=f"av{l}.{h}.{t_}")
                    cp(p, "dve", yT[h % 2 * 64:h % 2 * 64 + 64, h // 2, t_ * 128:(t_ + 1) * 128],
                       pyb[0:64, 0:128], r=("pyb",), w=("yT", "pyb"))
        xv = x_d.rearrange("(tt q) c -> tt q c", q=128)
        for t_ in range(NT):
            sl2 = t_ % 2
            ld(p, "sp", xstg[:, sl2, :], xv[t_], w=(f"xstg{sl2}",), r=("x_d",))
            for oc in range(2):
                mm(p, [(pq[:, 0:512], yT[:, ic, t_ * 128:(t_ + 1) * 128],
                        wob[:, ic, oc * 512:(oc + 1) * 512], ic == 0, ic == 3)
                       for ic in range(4)],
                   r=("yT", "wo"), w=("pq",), tag=f"wo{l}.{t_}.{oc}")
                tt(p, xstg[:, sl2, oc * 512:(oc + 1) * 512], pq[:, 0:512],
                   xstg[:, sl2, oc * 512:(oc + 1) * 512], OP.add,
                   r=("pq", f"xstg{sl2}"), w=(f"xstg{sl2}",))
            st(p, "act", xv[t_], xstg[:, sl2, :], r=(f"xstg{sl2}",), w=("x_d",))
        for t_ in range(NT):
            sl2 = t_ % 2
            ld(p, "sp", xstg[:, sl2, :], xv[t_], w=(f"xstg{sl2}",), r=("x_d",))
            cp(p, "dve", xbstg[:, :], xstg[:, sl2, :], r=(f"xstg{sl2}",), w=("xb",))
            trans(p, [(ptb[:, c * 128:(c + 1) * 128], xbstg[:, c * 128:(c + 1) * 128])
                      for c in range(NCH)], idb, r=("xb",), w=("ptb",))
            cp(p, "dve", xTz[:, :, 1 + t_ * 128:1 + (t_ + 1) * 128],
               ptb[:, 0:1024].rearrange("p (c q) -> p c q", q=128),
               r=("ptb",), w=("xTzl", "ptb"))
        for tc in range(4):
            t0 = tc * 512
            mm(p, [(pr[0:1, 0:512], cmeanb[:, 0:1], shifted(c, t0, 512), c == 0, c == 7)
                   for c in range(NCH)], r=XK + ("cmb",), w=("pr",), tag=f"mu{l}.{tc}")
            cp(p, "dve", rows[:, 0:512], pr[0:1, 0:512], r=("pr",), w=("rows", "big", "pr"))
            for c in range(NCH):
                act(p, sqst[:, 0:512], shifted(c, t0, 512), AF.Square,
                    r=XK, w=("sqst",))
                mm(p, [(pr[0:1, 0:512], cmeanf[:, 0:1], sqst[:, 0:512], c == 0, c == 7)],
                   r=("sqst", "cmf"), w=("pr",), tag=f"msq{l}.{tc}.{c}")
            cp(p, "dve", rows[:, 512:1024], pr[0:1, 0:512],
               r=("pr",), w=("rows", "big", "pr"))
            tt(p, rows[:, 1024:1536], rows[:, 0:512], rows[:, 0:512], OP.mult,
               r=("rows", "big"), w=("rows", "big"))
            tt(p, rows[:, 1024:1536], rows[:, 512:1024], rows[:, 1024:1536],
               OP.subtract, r=("rows", "big"), w=("rows", "big"))
            act(p, rows[:, 1024:1536], rows[:, 1024:1536], AF.Sqrt, bias=f32(1e-5),
                r=("rows", "big"), w=("rows", "big"))
            p.op("dve", lambda e, d=rows[:, 1024:1536], s=rows[:, 1024:1536]:
                 e.reciprocal(d, s), r=("rows", "big"), w=("rows", "big"))
            stt(p, rows[:, 1536:2048], rows[:, 0:512], f32(-1.0),
                rows[:, 1024:1536], OP.mult, OP.mult, r=("rows", "big"), w=("rows", "big"))
            mm(p, [(pr[0:128, 0:512], ones1[0:1, 0:128], rows[:, 1024:1536],
                    True, True)], r=("rows", "ones1"), w=("pr",), tag=f"brs{l}.{tc}")
            mm(p, [(pn[0:128, 0:512], ones1[0:1, 0:128], rows[:, 1536:2048],
                    True, True)], r=("rows", "ones1"), w=("pn",), tag=f"bng{l}.{tc}")
            for c in range(NCH):
                tt(p, sqst[:, 0:512], shifted(c, t0, 512), pr[0:128, 0:512], OP.mult,
                   r=XK + ("pr",), w=("sqst", "pr"))
                tt(p, sqst[:, 0:512], sqst[:, 0:512], pn[0:128, 0:512], OP.add,
                   r=("sqst", "pn"), w=("sqst", "pn"))
                act(p, lnhT[:, c, t0:t0 + 512], sqst[:, 0:512], AF.Identity,
                    scale=lgb[:, c:c + 1], bias=lbb[:, c:c + 1],
                    r=("sqst", "lg", "lb"), w=("big",))
        for j in range(NJ2):
            cj = _cj2(j)
            sj = j % 2
            ld(p, "pool", w1a[:, sj, :, 0:cj],
               w1_i[l].rearrange("(c p) o -> p c o", p=128)[:, :, j * 128:j * 128 + cj],
               w=(f"w1a{sj}",))
            ld(p, "pool", w1g[:, sj, :, 0:cj],
               w1_i[l].rearrange("(c p) o -> p c o", p=128)[:, :, HID2 + j * 128:HID2 + j * 128 + cj],
               w=(f"w1g{sj}",))
            for tc in range(4):
                t0 = tc * 512
                mm(p, [(ps0[0:cj, 0:512], w1a[:, sj, c, 0:cj], lnhT[:, c, t0:t0 + 512],
                        c == 0, c == 7) for c in range(NCH)],
                   r=(f"w1a{sj}", "big"), w=("ps0",), tag=f"a{l}.{j}.{tc}")
                mm(p, [(ps1[0:cj, 0:512], w1g[:, sj, c, 0:cj], lnhT[:, c, t0:t0 + 512],
                        c == 0, c == 7) for c in range(NCH)],
                   r=(f"w1g{sj}", "big"), w=("ps1",), tag=f"g{l}.{j}.{tc}")
                act(p, ast2[0:cj, 0:512], ps0[0:cj, 0:512], AF.Identity,
                    bias=b1ab[0:cj, j:j + 1],
                    r=("ps0", "b1a"), w=("ast2", "ps0"))
                act(p, gst2[0:cj, 0:512], ps1[0:cj, 0:512], AF.Gelu,
                    bias=b1gb[0:cj, j:j + 1], r=("ps1", "b1g"), w=("gst2", "ps1"))
                tt(p, h2j[0:cj, 0:512], ast2[0:cj, 0:512], gst2[0:cj, 0:512], OP.mult,
                   r=("ast2", "gst2"), w=("h2j",))
                st(p, "act", h2_d[j, 0:cj, t0:t0 + 512], h2j[0:cj, 0:512],
                   r=("h2j",), w=("h2_d",))
        ld(p, "pool", w2b[:, 0:21, :],
           w2_i[l][0:2688, :].rearrange("(j p) o -> p j o", p=128), w=("w2b",))
        ld(p, "pool", w2b[0:42, 21, :], w2_i[l][2688:2730, :], w=("w2b",))
        hv2 = h2_d.rearrange("j p t -> p j t")
        for t_ in range(NT):
            sl2 = t_ % 2
            ld(p, "sp", h2slc, hv2[:, :, t_ * 128:(t_ + 1) * 128],
               w=("yT",), r=("h2_d",))
            ld(p, "sp", xstg[:, sl2, :], xv[t_], w=(f"xstg{sl2}",), r=("x_d",))
            for oc in range(2):
                mm(p, [(pg2[:, 0:512], h2slc[0:_cj2(j), j, :],
                        w2b[0:_cj2(j), j, oc * 512:(oc + 1) * 512],
                        j == 0, j == NJ2 - 1) for j in range(NJ2)],
                   r=("yT", "w2b"), w=("pg2",), tag=f"dn{l}.{t_}.{oc}")
                tt(p, xstg[:, sl2, oc * 512:(oc + 1) * 512], pg2[:, 0:512],
                   xstg[:, sl2, oc * 512:(oc + 1) * 512], OP.add,
                   r=("pg2", f"xstg{sl2}"), w=(f"xstg{sl2}",))
            tt(p, xstg[:, sl2, :], xstg[:, sl2, :], b2b[:, :] if False else b2b[:, 0:D],
               OP.add, r=(f"xstg{sl2}", "b2"), w=(f"xstg{sl2}",))
            if l == 0:
                st(p, "act", xv[t_], xstg[:, sl2, :], r=(f"xstg{sl2}",), w=("x_d",))
            else:
                st(p, "act", ff2_o.rearrange("(tt q) c -> tt q c", q=128)[t_],
                   xstg[:, sl2, :], r=(f"xstg{sl2}",), w=("ff2_o",))
        if l == 0:
            for t_ in range(NT):
                sl2 = t_ % 2
                ld(p, "sp", xstg[:, sl2, :], xv[t_], w=(f"xstg{sl2}",), r=("x_d",))
                cp(p, "dve", xbstg[:, :], xstg[:, sl2, :], r=(f"xstg{sl2}",), w=("xb",))
                trans(p, [(ptb[:, c * 128:(c + 1) * 128], xbstg[:, c * 128:(c + 1) * 128])
                          for c in range(NCH)], idb, r=("xb",), w=("ptb",))
                cp(p, "dve", xTz[:, :, 1 + t_ * 128:1 + (t_ + 1) * 128],
                   ptb[:, 0:1024].rearrange("p (c q) -> p c q", q=128),
                   r=("ptb",), w=("xTzl", "ptb"))
    return p


def _rotary_tables():
    DIM_HEAD, WINDOW = 64, 512
    n = 2048
    half = np.arange(0, DIM_HEAD, 2, dtype=np.float64)
    inv = 1.0 / 10000.0 ** (half / DIM_HEAD)
    t = np.arange(n, dtype=np.float64)
    fr = t[:, None] * inv[None, :]
    fr = np.concatenate([fr, fr], -1)
    base = (half + 0.4 * DIM_HEAD) / (1.4 * DIM_HEAD)
    power = (t - n // 2) / float(WINDOW // 2)
    sc = base[None, :] ** power[:, None]
    sc = np.concatenate([sc, sc], -1)
    cos, sin = np.cos(fr), np.sin(fr)
    return (cos * sc, sin * sc, cos / sc, sin / sc)


def _lo_mask():
    m = np.zeros((128, 128), np.float32)
    m[np.tril_indices(128, -1)] = -1e30
    return m


def stage_lf2(xn_row, L2f):
    csq, snq, csk, snk = _rotary_tables()
    til = lambda a: np.ascontiguousarray(_tile_tok(a.astype(np.float32)))
    xnT = np.zeros((NCH, 128, T + 1), ml_dtypes.bfloat16)
    xnT[:, :, 1:] = xn_row.T.reshape(NCH, 128, T).astype(ml_dtypes.bfloat16)
    d = {
        "xnT_i": xnT, "xn_i": np.ascontiguousarray(xn_row, np.float32),
        "csq": til(csq), "snq": til(snq), "csk": til(csk), "snk": til(snk),
        "identb": np.eye(128, dtype=ml_dtypes.bfloat16),
        "masktri": _tri_mask(), "masklo": _lo_mask(),
    }
    for l in range(2):
        d[f"wqkv{l}"] = np.ascontiguousarray(L2f["wqkv"][l], np.float32)
        d[f"wo{l}"] = np.ascontiguousarray(L2f["wo"][l], np.float32)
        d[f"w1{l}"] = np.ascontiguousarray(L2f["w1"][l], np.float32)
        d[f"w2{l}"] = np.ascontiguousarray(L2f["w2"][l], np.float32)
        b1 = L2f["b1"][l]
        b1a = np.zeros((128, NJ2), np.float32)
        b1g = np.zeros((128, NJ2), np.float32)
        for j in range(NJ2):
            cj = _cj2(j)
            b1a[0:cj, j] = b1[j * 128:j * 128 + cj]
            b1g[0:cj, j] = b1[HID2 + j * 128:HID2 + j * 128 + cj]
        d[f"b1a{l}"] = b1a; d[f"b1g{l}"] = b1g
        d[f"b2bc{l}"] = np.ascontiguousarray(np.broadcast_to(L2f["b2"][l], (128, D)), np.float32)
        d[f"lg{l}"] = _bcol(L2f["ln_g"][l]); d[f"lb{l}"] = _bcol(L2f["ln_b"][l])
        d[f"qsbc{l}"] = np.ascontiguousarray(np.broadcast_to(L2f["q_scale"][l], (128, 64)), np.float32)
        d[f"ksbc{l}"] = np.ascontiguousarray(np.broadcast_to(L2f["k_scale"][l], (128, 64)), np.float32)
    return d



def build_lg():
    p = Prog("lg")
    x2_i = p.inp("x2", (T, D)); at2_i = p.inp("at2", (T, D)); ff2_i = p.inp("ff2", (T, D))
    wqT = p.inp("wqT", (D, D))
    wkT = [p.inp(f"wkT{k}", (D, D)) for k in range(2)]
    wvT = [p.inp(f"wvT{k}", (D, D)) for k in range(2)]
    woT = p.inp("woT", (D, D))
    bq_bc = p.inp("bq_bc", (128, D))
    bk_bc = [p.inp(f"bk_bc{k}", (128, D)) for k in range(2)]
    bv_bc = [p.inp(f"bv_bc{k}", (128, D)) for k in range(2)]
    bo_bc = p.inp("bo_bc", (128, D))
    lng_bc = p.inp("lng_bc", (128, D)); lnb_bc = p.inp("lnb_bc", (128, D))
    wrout = p.inp("wrout", (D, 32))
    iotaE = p.inp("iotaE", (128, 1))
    identb = p.inp("identb", (128, 128), BF16)
    identf = p.inp("identf", (128, 128), F32)

    x3_o = p.outp("x3", (T, D), F32)
    xn3_o = p.outp("xn3", (T, D), F32)
    at3_o = p.outp("at3", (T, D), F32)
    cnt_o = p.outp("cnt", (1, 32), F32)
    top1_o = p.outp("top1", (128, 16), F32)
    pos_o = p.outp("pos", (1, 2048), F32)
    gate_o = p.outp("gate", (128, 16), F32)
    r_d = p.dscratch("r_d", (16, 128), F32)
    lnT_d = p.dscratch("lnT_d", (NT, 128, 1024), BF16)

    alloc_ln(p, "l.", ptkey="pq")
    xnT = p.sb("xnT", [128, NCH, T], BF16)
    xn3b = p.sb("xn3b", [128, NT, D], BF16)
    kv0 = p.sb("kv0", [128, NT, D], BF16)
    _kv0b = kv0.ap().rearrange("p a b -> p (a b)")
    _kv0f = _kv0b.bitcast(F32)
    kv1 = p.sb("kv1", [128, NT, D], BF16)
    _kv1f = kv1.ap().rearrange("p a b -> p (a b)").bitcast(F32)
    ohT = _kv1f[0:32, 0:2048]
    cs = _kv1f[0:32, 2048:4096]
    onr = _kv1f[0:1, 4096:6144]
    posr = _kv1f[0:1, 6144:8192]
    wb = p.sb("wb", [128, 8, 1024], BF16)
    bqb = p.sb("bqb", [128, D], F32)
    bxb = p.sb("bxb", [128, D], F32)
    lngb = p.sb("lngb", [128, D], F32); lnbb = p.sb("lnbb", [128, D], F32)
    wrb = p.sb("wrb", [128, 8, 32], BF16)
    iotE = p.sb("iotE", [128, 1], F32)
    ones1g = p.sb("ones1g", [1, 32], F32)
    ones32 = p.sb("ones32", [32, 1], F32)
    idb = p.sb("ident", [128, 128], BF16)
    idf = p.sb("identf_s", [128, 128], F32)
    nrm = p.sb("nrm", [128, 16, 1], F32)
    ttq = p.sb("ttq", [128, 16, 64], F32)
    stk = p.sb("stk", [128, 2, 8, 64], F32)
    stbd = p.sb("stbd", [128, 8, 128], BF16)
    otile = p.sb("otile", [128, 1024], F32)
    lnob = _kv0b[:, 8192:9216]
    lnoT = _kv0b[:, 9216:10240].rearrange("p (c q) -> p c q", q=128)
    stg3 = _kv0f[:, 0:2048].rearrange("p (s c) -> p s c", c=1024)
    mx8 = p.sb("mx8", [128, 8], F32)
    idxu = p.sb("idxu", [128, 16], U32)
    idx8 = p.sb("idx8", [128, 8], U32)
    t1c = p.sb("t1c", [128, 16], F32)
    gatec = p.sb("gatec", [128, 16], F32)
    negm2 = p.sb("negm2", [128, 1], F32)
    lsum = p.sb("lsum", [128, 1], F32)
    oh = p.sb("oh", [128, 32], F32)
    cnts = p.sb("cnts", [32, 1], F32)
    kv1f = None
    pq = p.ps("pq", [128, 512], F32)
    pv = p.ps("pv", [128, 512], F32)
    pst = p.ps("pst", [128, 512], F32)
    ptb = p.ps("ptb", [128, 1024], BF16)
    ptf = p.ps("ptf", [128, 512], F32)

    ld(p, "sp", idb[:], identb[:], w=("ident",))
    ld(p, "sp", idf[:], identf[:], w=("identf",))
    ld(p, "sp", iotE[:], iotaE[:], w=("iotE",))
    memset(p, ones1g[0:1, 0:32], 1.0, w=("ones1g",))
    memset(p, ones32[0:32, 0:1], 1.0, w=("ones32",))
    ld(p, "sp", lngb[:], lng_bc[:], w=("lng",))
    ld(p, "sp", lnbb[:], lnb_bc[:], w=("lnb",))
    ld(p, "pool", wrb[:, :, :], wrout.rearrange("(c p) o -> p c o", p=128), w=("wr",))

    x2v = x2_i.rearrange("(tt q) c -> tt q c", q=128)
    a2v = at2_i.rearrange("(tt q) c -> tt q c", q=128)
    f2v = ff2_i.rearrange("(tt q) c -> tt q c", q=128)
    x3v = x3_o.rearrange("(tt q) c -> tt q c", q=128)
    xn3v = xn3_o.rearrange("(tt q) c -> tt q c", q=128)
    at3v = at3_o.rearrange("(tt q) c -> tt q c", q=128)
    bnl = p.t["l.bn"]; mvl = p.t["l.mv"]; sdl = p.t["l.sd"]
    rstdl = p.t["l.rstd"]; ngml = p.t["l.ngm"]; xnfl = p.t["l.xnf"]
    for t_ in range(NT):
        sl = t_ % 2
        ld(p, "sp", stg3[:, sl, :], x2v[t_], w=(f"stg3{sl}", "kv0"))
        ld(p, "sp", xnfl[:, sl, :], a2v[t_], w=(f"l.xnf{sl}",))
        tt(p, stg3[:, sl, :], stg3[:, sl, :], xnfl[:, sl, :], OP.add,
           r=(f"stg3{sl}", f"l.xnf{sl}", "kv0"), w=(f"stg3{sl}", "kv0"))
        ld(p, "sp", xnfl[:, sl, :], f2v[t_], w=(f"l.xnf{sl}",))
        tt(p, stg3[:, sl, :], stg3[:, sl, :], xnfl[:, sl, :], OP.add,
           r=(f"stg3{sl}", f"l.xnf{sl}", "kv0"), w=(f"stg3{sl}", "kv0"))
        st(p, "act", x3v[t_], stg3[:, sl, :], r=(f"stg3{sl}", "kv0"), w=("x3_o",))
        p.op("dve", lambda e, s=stg3[:, sl, 0:512], d=bnl[:, 0:6]: e.bn_stats(d, s),
             r=(f"stg3{sl}", "kv0"), w=("l.bn",))
        p.op("dve", lambda e, s=stg3[:, sl, 512:1024], d=bnl[:, 6:12]: e.bn_stats(d, s),
             r=(f"stg3{sl}", "kv0"), w=("l.bn",))
        p.op("dve", lambda e, s=bnl[:, 0:12], d=mvl[:, 0:2]: e.bn_aggr(d, s),
             r=("l.bn",), w=("l.mv",))
        act(p, sdl[:, 0:1], mvl[:, 1:2], AF.Sqrt, bias=f32(1e-5),
            r=("l.mv",), w=("l.sd",))
        p.op("dve", lambda e, d=rstdl[:, 0:1], s=sdl[:, 0:1]: e.reciprocal(d, s),
             r=("l.sd",), w=("l.rstd",))
        stt(p, ngml[:, 0:1], mvl[:, 0:1], f32(-1.0), rstdl[:, 0:1],
            OP.mult, OP.mult, r=("l.mv", "l.rstd"), w=("l.ngm",))
        act(p, xnfl[:, sl, :], stg3[:, sl, :], AF.Identity,
            bias=ngml[:, 0:1], scale=rstdl[:, 0:1],
            r=(f"stg3{sl}", "l.rstd", "l.ngm", "kv0"), w=(f"l.xnf{sl}",))
        st(p, "act", xn3v[t_], xnfl[:, sl, :], r=(f"l.xnf{sl}",), w=("xn3_o",))
        cp(p, "dve", xn3b[:, t_, :], xnfl[:, sl, :], r=(f"l.xnf{sl}",), w=("xn3b",))
        for g in range(2):
            trans(p, [(ptf[:, j * 128:(j + 1) * 128],
                       xnfl[:, sl, (g * 4 + j) * 128:(g * 4 + j + 1) * 128])
                      for j in range(4)], idf,
                  r=(f"l.xnf{sl}", "identf"), w=("ptf2",))
            cp(p, "dve", xnT[:, g * 4:(g + 1) * 4, t_ * 128:(t_ + 1) * 128],
               ptf[:, 0:512].rearrange("p (c q) -> p c q", q=128),
               r=("ptf2",), w=(f"xnT{t_}",))

    memset(p, stbd[:, :, :], 0.0, w=("stbd",))
    SC = 1.0 / np.sqrt(2048.0)
    for k2 in range(2):
        stk_k = stk
        for (dstb, wsrc, bsrc, feat) in ((kv0, wkT[k2], f"bk_bc{k2}", True),
                                         (kv1, wvT[k2], f"bv_bc{k2}", False)):
            key = "kv0" if feat else "kv1"
            ld(p, "pool", wb[:, :, :], wsrc.rearrange("(c p) o -> p c o", p=128), w=("wb",))
            ld(p, "sp", bxb[:, :], p.t[bsrc][:, :], w=("bx",))
            for t_ in range(NT):
                for oc in range(2):
                    mm(p, [(pv[:, 0:512], xnT[:, c, t_ * 128:(t_ + 1) * 128],
                            wb[:, c, oc * 512:(oc + 1) * 512], c == 0, c == 7)
                           for c in range(NCH)],
                       r=XNT_KEYS + ("wb",), w=("pv",), tag=f"kv{k2}.{t_}.{oc}")
                    tt(p, otile[:, oc * 512:(oc + 1) * 512], pv[:, 0:512],
                       xn3b[:, t_, oc * 512:(oc + 1) * 512], OP.add,
                       r=("pv", "xn3b"), w=("ot", "pv"))
                    tt(p, otile[:, oc * 512:(oc + 1) * 512],
                       otile[:, oc * 512:(oc + 1) * 512],
                       bxb[:, oc * 512:(oc + 1) * 512], OP.add,
                       r=("ot", "bx"), w=("ot",))
                if feat:
                    act(p, otile[:, :], otile[:, :], AF.Silu, r=("ot",), w=("ot",))
                    ts(p, otile[:, :], otile[:, :], f32(0.5), OP.add, r=("ot",), w=("ot",))
                    ot3 = otile[:].rearrange("p (h d) -> p h d", d=64)
                    tt(p, ttq[:, :, :], ot3, ot3, OP.mult, r=("ot",), w=("ttq",))
                    red(p, nrm[:, :, :], ttq[:, :, :], OP.add, r=("ttq",), w=("nrm",))
                    act(p, nrm[:, :, :], nrm[:, :, :], AF.Sqrt, r=("nrm",), w=("nrm",))
                    ts(p, nrm[:, :, :], nrm[:, :, :], f32(1e-6), OP.add, r=("nrm",), w=("nrm",))
                    p.op("dve", lambda e, d=nrm[:, :, :], s=nrm[:, :, :]:
                         e.reciprocal(d, s), r=("nrm",), w=("nrm",))
                    tt(p, ot3, ot3, nrm[:, :, :].broadcast_to((128, 16, 64)), OP.mult,
                       r=("ot", "nrm"), w=("ot",))
                ts(p, dstb[:, t_, :], otile[:, :], f32(SC), OP.mult, r=("ot",), w=(key,))
        for h in range(16):
            hp = (h % 2) * 64
            mm(p, [(pst[hp:hp + 64, (h // 2) * 64:(h // 2) * 64 + 64],
                    kv0[:, t_, h * 64:(h + 1) * 64], kv1[:, t_, h * 64:(h + 1) * 64],
                    t_ == 0, t_ == NT - 1) for t_ in range(NT)],
               r=("kv0", "kv1"), w=("pst",), tag=f"st{k2}.{h}")
        cp(p, "dve", stk_k[:, k2, :, :],
           pst[:, 0:512].rearrange("p (c q) -> p c q", q=64), r=("pst",), w=("stk",))
    tt(p, stk[:, 0, :, :], stk[:, 0, :, :], stk[:, 1, :, :], OP.mult,
       r=("stk",), w=("stk",))
    for j in range(8):
        cp(p, "dve", stbd[0:64, j, 0:64], stk[0:64, 0, j, :], r=("stk",), w=("stbd",))
        cp(p, "dve", stbd[64:128, j, 64:128], stk[64:128, 0, j, :], r=("stk",), w=("stbd",))

    ld(p, "pool", wb[:, :, :], wqT.rearrange("(c p) o -> p c o", p=128), w=("wb",))
    ld(p, "sp", bqb[:, :], bq_bc[:, :], w=("bq",))
    ld(p, "sp", bxb[:, :], bo_bc[:, :], w=("bx",))
    for t_ in range(NT):
        for oc in range(2):
            mm(p, [(pv[:, 0:512], xnT[:, c, t_ * 128:(t_ + 1) * 128],
                    wb[:, c, oc * 512:(oc + 1) * 512], c == 0, c == 7)
               for c in range(NCH)],
               r=XNT_KEYS + ("wb",), w=("pv",), tag=f"q{t_}.{oc}")
            tt(p, otile[:, oc * 512:(oc + 1) * 512], pv[:, 0:512],
               xn3b[:, t_, oc * 512:(oc + 1) * 512], OP.add,
               r=("pv", "xn3b"), w=("ot", "pv"))
            tt(p, otile[:, oc * 512:(oc + 1) * 512], otile[:, oc * 512:(oc + 1) * 512],
               bqb[:, oc * 512:(oc + 1) * 512], OP.add, r=("ot", "bq"), w=("ot",))
        act(p, otile[:, :], otile[:, :], AF.Silu, r=("ot",), w=("ot",))
        ts(p, otile[:, :], otile[:, :], f32(0.5), OP.add, r=("ot",), w=("ot",))
        ot3 = otile[:].rearrange("p (h d) -> p h d", d=64)
        tt(p, ttq[:, :, :], ot3, ot3, OP.mult, r=("ot",), w=("ttq",))
        red(p, nrm[:, :, :], ttq[:, :, :], OP.add, r=("ttq",), w=("nrm",))
        act(p, nrm[:, :, :], nrm[:, :, :], AF.Sqrt, r=("nrm",), w=("nrm",))
        ts(p, nrm[:, :, :], nrm[:, :, :], f32(1e-6), OP.add, r=("nrm",), w=("nrm",))
        p.op("dve", lambda e, d=nrm[:, :, :], s=nrm[:, :, :]: e.reciprocal(d, s),
             r=("nrm",), w=("nrm",))
        tt(p, ot3, ot3, nrm[:, :, :].broadcast_to((128, 16, 64)), OP.mult,
           r=("ot", "nrm"), w=("ot",))
        cp(p, "dve", lnob[:, :], otile[:, :], r=("ot",), w=("lnob", "kv0"))
        trans(p, [(ptb[:, c * 128:(c + 1) * 128], lnob[:, c * 128:(c + 1) * 128])
                  for c in range(NCH)], idb, r=("lnob", "kv0"), w=("ptb",))
        cp(p, "dve", lnoT[:, :, :], ptb[:, 0:1024].rearrange("p (c q) -> p c q", q=128),
           r=("ptb",), w=("lnoT", "kv0", "ptb"))
        for jg in range(2):
            mm(p, [(pq[:, (j - 4 * jg) * 128:(j - 4 * jg + 1) * 128],
                    lnoT[:, j, :], stbd[:, j, :], True, True)
                   for j in range(4 * jg, 4 * jg + 4)],
               r=("lnoT", "stbd", "kv0"), w=("pq",), tag=f"out{t_}.{jg}")
            cp(p, "dve", otile[:, jg * 512:(jg + 1) * 512], pq[:, 0:512],
               r=("pq",), w=("ot", "pq"))
        p.op("dve", lambda e, s=otile[:, 0:512], d=bnl[:, 0:6]: e.bn_stats(d, s),
             r=("ot",), w=("l.bn",))
        p.op("dve", lambda e, s=otile[:, 512:1024], d=bnl[:, 6:12]: e.bn_stats(d, s),
             r=("ot",), w=("l.bn",))
        p.op("dve", lambda e, s=bnl[:, 0:12], d=mvl[:, 0:2]: e.bn_aggr(d, s),
             r=("l.bn",), w=("l.mv",))
        act(p, sdl[:, 0:1], mvl[:, 1:2], AF.Sqrt, bias=f32(1e-5),
            r=("l.mv",), w=("l.sd",))
        p.op("dve", lambda e, d=rstdl[:, 0:1], s=sdl[:, 0:1]: e.reciprocal(d, s),
             r=("l.sd",), w=("l.rstd",))
        stt(p, ngml[:, 0:1], mvl[:, 0:1], f32(-1.0), rstdl[:, 0:1],
            OP.mult, OP.mult, r=("l.mv", "l.rstd"), w=("l.ngm",))
        act(p, otile[:, :], otile[:, :], AF.Identity,
            bias=ngml[:, 0:1], scale=rstdl[:, 0:1],
            r=("ot", "l.rstd", "l.ngm"), w=("ot",))
        tt(p, otile[:, :], otile[:, :], lngb[:, 0:D], OP.mult, r=("ot", "lng"), w=("ot",))
        tt(p, otile[:, :], otile[:, :], lnbb[:, 0:D], OP.add, r=("ot", "lnb"), w=("ot",))
        cp(p, "dve", lnob[:, :], otile[:, :], r=("ot",), w=("lnob", "kv0"))
        trans(p, [(ptb[:, c * 128:(c + 1) * 128], lnob[:, c * 128:(c + 1) * 128])
                  for c in range(NCH)], idb, r=("lnob", "kv0"), w=("ptb",))
        cp(p, "dve", lnoT[:, :, :], ptb[:, 0:1024].rearrange("p (c q) -> p c q", q=128),
           r=("ptb",), w=("lnoT", "kv0", "ptb"))
        st(p, "act", lnT_d[t_], lnoT[:, :, :].rearrange("p c q -> p (c q)"),
           r=("lnoT", "kv0"), w=("lnT_d",))
        mm(p, [(pq[:, 0:32], xnT[:, c, t_ * 128:(t_ + 1) * 128], wrb[:, c, :],
                c == 0, c == 7) for c in range(NCH)],
           r=XNT_KEYS + ("wr",), w=("pq",), tag=f"rt{t_}")
        cp(p, "dve", oh[:, :], pq[:, 0:32], r=("pq",), w=("oh", "pq"))
        p.op("dve", lambda e, d=mx8[:, 0:8], s=oh[:, :]: e.max(d, s),
             r=("oh",), w=("mx8",))
        p.op("dve", lambda e, d=idx8[:, 0:8], mxx=mx8[:, 0:8], s=oh[:, :]:
             e.max_index(d, mxx, s), r=("mx8", "oh"), w=("idx8",))
        cp(p, "dve", idxu[:, t_:t_ + 1], idx8[:, 0:1], r=("idx8",), w=("idxu",))
        ts(p, negm2[:, 0:1], mx8[:, 0:1], f32(-1.0), OP.mult, r=("mx8",), w=("negm2",))
        act(p, oh[:, :], oh[:, :], AF.Exp, bias=negm2[:, 0:1], accum=lsum[:, 0:1],
            r=("oh", "negm2"), w=("oh", "lsum"))
        p.op("dve", lambda e, d=gatec[:, t_:t_ + 1], s=lsum[:, 0:1]: e.reciprocal(d, s),
             r=("lsum",), w=("gatec",))
        cp(p, "dve", t1c[:, t_:t_ + 1], idxu[:, t_:t_ + 1], r=("idxu",), w=("t1c",))
    ld(p, "pool", wb[:, :, :], woT.rearrange("(c p) o -> p c o", p=128), w=("wb",))
    for t_ in range(NT):
        ld(p, "sp", lnoT[:, :, :], lnT_d.rearrange("t p co -> t p co")[t_].rearrange(
            "p (c q) -> p c q", q=128), w=("lnoT", "kv0"), r=("lnT_d",))
        for oc in range(2):
            mm(p, [(pv[:, 0:512], lnoT[:, ic, :], wb[:, ic, oc * 512:(oc + 1) * 512],
                    ic == 0, ic == 7) for ic in range(NCH)],
               r=("lnoT", "wb", "kv0"), w=("pv",), tag=f"wo3.{t_}.{oc}")
            tt(p, otile[:, oc * 512:(oc + 1) * 512], pv[:, 0:512],
               bxb[:, oc * 512:(oc + 1) * 512], OP.add, r=("pv", "bx"), w=("ot", "pv"))
        st(p, "act", at3v[t_], otile[:, :], r=("ot",), w=("at3_o",))
    st(p, "act", top1_o[:, :], t1c[:, :], r=("t1c",), w=("top1_o",))
    st(p, "act", gate_o[:, :], gatec[:, :], r=("gatec",), w=("gate_o",))
    trans(p, [(ptf[0:16, 0:128], t1c[:, :])], idf, r=("t1c", "identf"), w=("ptf2",))
    cp(p, "dve", ohT[0:16, 0:128], ptf[0:16, 0:128], r=("ptf2",), w=("ohT", "kv1"))
    st(p, "sp", r_d[:, :], ohT[0:16, 0:128], r=("ohT", "kv1"), w=("r_d",))
    onr_view = r_d.rearrange("a b -> (a b)").rearrange("(x n) -> x n", x=1)
    ld(p, "sp", onr[0:1, 0:2048], onr_view, w=("onr", "kv1"), r=("r_d",))
    for tc in range(4):
        mm(p, [(pq[0:32, 0:512], ones1g[0:1, 0:32], onr[0:1, tc * 512:(tc + 1) * 512],
                True, True)], r=("onr", "ones1g", "kv1"), w=("pq",), tag=f"bct{tc}")
        ts(p, ohT[0:32, tc * 512:(tc + 1) * 512], pq[0:32, 0:512], iotE[0:32, 0:1],
           OP.is_equal, r=("pq", "iotE"), w=("ohT", "kv1", "pq"))
    p.op("dve", lambda e, d=cs[0:32, 0:2048], s=ohT[0:32, 0:2048]:
         e.tensor_tensor_scan(d, s, s, 0.0, OP.add, OP.bypass),
         r=("ohT", "kv1"), w=("cs", "kv1"))
    cp(p, "dve", cnts[0:32, 0:1], cs[0:32, 2047:2048], r=("cs", "kv1"), w=("cnts",))
    trans(p, [(ptf[0:1, 0:32], cnts[0:32, 0:1])], idf[0:32, 0:32], r=("cnts", "identf"), w=("ptf2",))
    cp(p, "dve", oh[0:1, 0:32], ptf[0:1, 0:32], r=("ptf2",), w=("oh",))
    st(p, "act", cnt_o[:, :], oh[0:1, 0:32], r=("oh",), w=("cnt_o",))
    tt(p, ohT[0:32, 0:2048], ohT[0:32, 0:2048], cs[0:32, 0:2048], OP.mult,
       r=("ohT", "cs", "kv1"), w=("ohT", "kv1"))
    for tc in range(4):
        mm(p, [(ptf[0:1, 0:512], ones32[0:32, 0:1], ohT[0:32, tc * 512:(tc + 1) * 512],
                True, True)], r=("ohT", "ones32", "kv1"), w=("ptf2",), tag=f"ps{tc}")
        ts(p, posr[0:1, tc * 512:(tc + 1) * 512], ptf[0:1, 0:512], f32(-1.0), OP.add,
           r=("ptf2",), w=("posr", "kv1", "ptf2"))
    st(p, "act", pos_o[:, :], posr[0:1, 0:2048], r=("posr", "kv1"), w=("pos_o",))
    return p



EC = 640
TGLOB = 8 * T


def emit_offsets(p, cnt_ap, offs, sc32):
    ld(p, "sp", sc32[0:32, 0:8], cnt_ap, w=("sc32",))
    p.op("dve", lambda e: e.tensor_tensor_scan(offs[0:32, 0:8], sc32[0:32, 0:8],
         sc32[0:32, 0:8], 0.0, OP.add, OP.bypass), r=("sc32",), w=("offs",))
    cp(p, "dve", sc32[0:32, 1:8], offs[0:32, 0:7], r=("offs",), w=("sc32",))
    memset(p, sc32[0:32, 0:1], 0.0, w=("sc32",))
    cp(p, "dve", offs[0:32, 0:8], sc32[0:32, 0:8], r=("sc32",), w=("offs",))


def build_lh():
    p = Prog("lh")
    xnf_i = p.inp("xnf", (TGLOB + 1, D))
    cnt_i = p.inp("cnts", (32, 8))
    t1_i = p.inp("t1g", (128, 8, 16))
    pos_i = p.inp("posg", (128, 8, 16))
    wg_i = p.inp("wg", (4, D, D)); wu_i = p.inp("wu", (4, D, D)); wd_i = p.inp("wd", (4, D, D))
    cidk = p.inp("cidk", (128, 1))
    c4lo = p.inp("c4lo", (128, 1))
    c4hi = p.inp("c4hi", (128, 1))
    identb = p.inp("identb", (128, 128), BF16)
    identf = p.inp("identf", (128, 128), F32)
    y_o = p.outp("y_me", (4 * EC, D), BF16)
    buf_d = p.dscratch("buf_d", (4 * EC, D), F32)

    BIG = 1 << 22

    sc32 = p.sb("sc32", [32, 8], F32)
    offs = p.sb("offs", [32, 8], F32)
    t1c = p.sb("t1c", [128, 8, 16], F32)
    posc = p.sb("posc", [128, 8, 16], F32)
    iotE = p.sb("iotE", [128, 32], F32)
    ohb = p.sb("ohb", [128, 32], F32)
    ohbT = p.sb("ohbT", [32, 128], F32)
    offtok = p.sb("offtok", [128, 1], F32)
    m1 = p.sb("m1", [128, 1], F32)
    m2 = p.sb("m2", [128, 1], F32)
    slotc = p.sb("slotc", [128, 1], F32)
    idxi = p.sb("idxi", [128, 1], I32)
    xstg = p.sb("xstg", [128, 2, D], F32)
    zstg = p.sb("zstg", [128, D], F32)
    idb = p.sb("ident", [128, 128], BF16)
    idf = p.sb("identf_s", [128, 128], F32)
    bufb = p.sb("bufb", [128, D], BF16)
    bufT = p.sb("bufT", [128, NCH, EC], BF16)
    wgb = p.sb("wgb", [128, 8, D], BF16)
    wub = p.sb("wub", [128, 8, D], BF16)
    wdb = p.sb("wdb", [128, 8, D], BF16)
    gs = p.sb("gs", [128, 512], F32)
    us = p.sb("us", [128, 512], F32)
    hT = p.sb("hT", [128, 8, EC], BF16)
    ystg = p.sb("ystg", [128, D], BF16)
    pg = p.ps("pg", [128, 512], F32)
    pu = p.ps("pu", [128, 512], F32)
    ptf = p.ps("ptf", [128, 512], F32)
    ptb = p.ps("ptb", [128, 1024], BF16)

    ld(p, "sp", idb[:], identb[:], w=("ident",))
    ld(p, "sp", idf[:], identf[:], w=("identf",))
    ld(p, "sp", t1c[:, :, :], t1_i[:, :, :], w=("t1c",))
    ld(p, "sp", posc[:, :, :], pos_i[:, :, :], w=("posc",))
    cidk_s = p.sb("cidk_s", [128, 1], F32)
    c4lo_s = p.sb("c4lo_s", [128, 1], F32)
    c4hi_s = p.sb("c4hi_s", [128, 1], F32)
    ld(p, "sp", cidk_s[:, :], cidk[:, :], w=("cidk",))
    ld(p, "sp", c4lo_s[:, :], c4lo[:, :], w=("c4lo",))
    ld(p, "sp", c4hi_s[:, :], c4hi[:, :], w=("c4hi",))
    p.op("pool", lambda e: e.iota(iotE[:, :], [[1, 32]], base=0, channel_multiplier=0,
                                  allow_small_or_imprecise_dtypes=True),
         r=(), w=("iotE",))
    emit_offsets(p, cnt_i[:, :], offs, sc32)
    memset(p, zstg[:, :], 0.0, w=("zstg",))
    for q in range(4 * EC // 128):
        st(p, "sp", buf_d[q * 128:(q + 1) * 128, :], zstg[:, :], r=("zstg",), w=("buf_d",))
    xv = xnf_i[0:TGLOB, :].rearrange("(gt q) c -> gt q c", q=128)
    for g in range(8):
        for t_ in range(NT):
            gi = g * NT + t_
            sl = gi % 2
            tt(p, ohb[:, :], t1c[:, g, t_:t_ + 1].broadcast_to((128, 32)), iotE[:, :],
               OP.is_equal, r=("t1c", "iotE"), w=("ohb",))
            trans(p, [(ptf[0:32, 0:128], ohb[:, :])], idf,
                  r=("ohb", "identf"), w=("ptf2",))
            cp(p, "dve", ohbT[0:32, 0:128], ptf[0:32, 0:128], r=("ptf2",), w=("ohbT",))
            mm(p, [(ptf[0:128, 0:1], ohbT[0:32, 0:128], offs[0:32, g:g + 1], True, True)],
               r=("ohbT", "offs"), w=("ptf2",), tag=f"of{gi}")
            cp(p, "dve", offtok[:, 0:1], ptf[0:128, 0:1], r=("ptf2",), w=("offtok", "ptf2"))
            tt(p, offtok[:, 0:1], offtok[:, 0:1], posc[:, g, t_:t_ + 1], OP.add,
               r=("offtok", "posc"), w=("offtok",))
            ts(p, m1[:, 0:1], offtok[:, 0:1], f32(float(EC)), OP.is_lt,
               r=("offtok",), w=("m1",))
            ts(p, m2[:, 0:1], t1c[:, g, t_:t_ + 1], c4lo_s[:, 0:1], OP.is_gt,
               r=("t1c", "c4lo"), w=("m2",))
            tt(p, m1[:, 0:1], m1[:, 0:1], m2[:, 0:1], OP.mult, r=("m1", "m2"), w=("m1",))
            ts(p, m2[:, 0:1], t1c[:, g, t_:t_ + 1], c4hi_s[:, 0:1], OP.is_lt,
               r=("t1c", "c4hi"), w=("m2",))
            tt(p, m1[:, 0:1], m1[:, 0:1], m2[:, 0:1], OP.mult, r=("m1", "m2"), w=("m1",))
            stt(p, slotc[:, 0:1], t1c[:, g, t_:t_ + 1], f32(float(EC)), offtok[:, 0:1],
                OP.mult, OP.add, r=("t1c", "offtok"), w=("slotc",))
            ts(p, slotc[:, 0:1], slotc[:, 0:1], cidk_s[:, 0:1], OP.add,
               r=("slotc", "cidk"), w=("slotc",))
            tt(p, slotc[:, 0:1], slotc[:, 0:1], m1[:, 0:1], OP.mult,
               r=("slotc", "m1"), w=("slotc",))
            ts(p, slotc[:, 0:1], slotc[:, 0:1], f32(float(BIG)), OP.add,
               r=("slotc",), w=("slotc",))
            act(p, idxi[:, 0:1], slotc[:, 0:1], AF.Identity, r=("slotc",), w=("idxi",))
            ld(p, "sp", xstg[:, sl, :], xv[gi], w=(f"xstg{sl}",))
            p.dma("pool", lambda e, ii=idxi[:, 0:1], s=xstg[:, sl, :]:
                  e.indirect_dma_start(out=buf_d[:, :],
                                       out_offset=bass.IndirectOffsetOnAxis(ap=ii, axis=0),
                                       in_=s, in_offset=None, oob_is_err=False),
                  r=("idxi", f"xstg{sl}"), w=("buf_d",), tag=f"sc{gi}")
    for el in range(4):
        ld(p, "pool", wgb[:, :, :], wg_i[el].rearrange("(c p) f -> p c f", p=128), w=("wgb",))
        ld(p, "pool", wub[:, :, :], wu_i[el].rearrange("(c p) f -> p c f", p=128), w=("wub",))
        ld(p, "pool", wdb[:, :, :], wd_i[el].rearrange("(f p) o -> p f o", p=128), w=("wdb",))
        for rt in range(5):
            sl = rt % 2
            ld(p, "sp", xstg[:, sl, :], buf_d[el * EC + rt * 128: el * EC + (rt + 1) * 128, :],
               w=(f"xstg{sl}",), r=("buf_d",))
            cp(p, "dve", bufb[:, :], xstg[:, sl, :], r=(f"xstg{sl}",), w=("bufb",))
            trans(p, [(ptb[:, c * 128:(c + 1) * 128], bufb[:, c * 128:(c + 1) * 128])
                      for c in range(NCH)], idb, r=("bufb",), w=("ptb",))
            cp(p, "dve", bufT[:, :, rt * 128:(rt + 1) * 128],
               ptb[:, 0:1024].rearrange("p (c q) -> p c q", q=128),
               r=("ptb",), w=("bufT", "ptb"))
        for fc in range(NCH):
            for rc in range(2):
                w_ = 512 if rc == 0 else 128
                mm(p, [(pg[0:128, 0:w_], wgb[:, c, fc * 128:(fc + 1) * 128],
                        bufT[:, c, rc * 512:rc * 512 + w_], c == 0, c == 7)
                       for c in range(NCH)],
                   r=("wgb", "bufT"), w=("pg",), tag=f"g{el}.{fc}.{rc}")
                mm(p, [(pu[0:128, 0:w_], wub[:, c, fc * 128:(fc + 1) * 128],
                        bufT[:, c, rc * 512:rc * 512 + w_], c == 0, c == 7)
                       for c in range(NCH)],
                   r=("wub", "bufT"), w=("pu",), tag=f"u{el}.{fc}.{rc}")
                act(p, gs[:, 0:w_], pg[0:128, 0:w_], AF.Silu, r=("pg",), w=("gs", "pg"))
                cp(p, "dve", us[:, 0:w_], pu[0:128, 0:w_], r=("pu",), w=("us", "pu"))
                tt(p, us[:, 0:w_], gs[:, 0:w_], us[:, 0:w_], OP.mult,
                   r=("gs", "us"), w=("us",))
                cp(p, "dve", hT[:, fc, rc * 512:rc * 512 + w_], us[:, 0:w_],
                   r=("us",), w=("hT",))
        for rt in range(5):
            for oc in range(2):
                mm(p, [(pg[:, 0:512], hT[:, fc, rt * 128:(rt + 1) * 128],
                        wdb[:, fc, oc * 512:(oc + 1) * 512], fc == 0, fc == 7)
                       for fc in range(NCH)],
                   r=("hT", "wdb"), w=("pg",), tag=f"y{el}.{rt}.{oc}")
                cp(p, "dve", ystg[:, oc * 512:(oc + 1) * 512], pg[:, 0:512],
                   r=("pg",), w=("ystg", "pg"))
            st(p, "act", y_o[el * EC + rt * 128: el * EC + (rt + 1) * 128, :],
               ystg[:, :], r=("ystg",), w=("y_o",))
    return p


def build_li():
    p = Prog("li")
    y_i = p.inp("y_full", (32 * EC + 1, D), BF16)
    cnt_i = p.inp("cnts", (32, 8))
    t1_i = p.inp("t1", (128, 16))
    pos_i = p.inp("posn", (128, 16))
    gate_i = p.inp("gaten", (128, 16))
    x3_i = p.inp("x3", (T, D)); at3_i = p.inp("at3", (T, D)); xn3_i = p.inp("xn3", (T, D))
    cido = p.inp("cido", (32, 8))
    identf = p.inp("identf", (128, 128), F32)
    out_o = p.outp("out", (T, D), F32)

    BIGY = 32 * EC
    sc32 = p.sb("sc32", [32, 8], F32)
    offs = p.sb("offs", [32, 8], F32)
    offc = p.sb("offc", [32, 1], F32)
    cidob = p.sb("cidob", [32, 8], F32)
    t1c = p.sb("t1c", [128, 16], F32)
    posc = p.sb("posc", [128, 16], F32)
    gatec = p.sb("gatec", [128, 16], F32)
    iotE = p.sb("iotE", [128, 32], F32)
    ohb = p.sb("ohb", [128, 32], F32)
    ohbT = p.sb("ohbT", [32, 128], F32)
    offtok = p.sb("offtok", [128, 1], F32)
    m1 = p.sb("m1", [128, 1], F32)
    slotc = p.sb("slotc", [128, 1], F32)
    idxi = p.sb("idxi", [128, 1], I32)
    ygat = p.sb("ygat", [128, D], BF16)
    s1 = p.sb("s1", [128, 2, D], F32)
    s2 = p.sb("s2", [128, 2, D], F32)
    idf = p.sb("identf_s", [128, 128], F32)
    ptf = p.ps("ptf", [128, 512], F32)

    ld(p, "sp", idf[:], identf[:], w=("identf",))
    ld(p, "sp", t1c[:, :], t1_i[:, :], w=("t1c",))
    ld(p, "sp", posc[:, :], pos_i[:, :], w=("posc",))
    ld(p, "sp", gatec[:, :], gate_i[:, :], w=("gatec",))
    ld(p, "sp", cidob[:, :], cido[:, :], w=("cido",))
    p.op("pool", lambda e: e.iota(iotE[:, :], [[1, 32]], base=0, channel_multiplier=0,
                                  allow_small_or_imprecise_dtypes=True),
         r=(), w=("iotE",))
    emit_offsets(p, cnt_i[:, :], offs, sc32)
    tt(p, sc32[0:32, 0:8], offs[0:32, 0:8], cidob[0:32, 0:8], OP.mult,
       r=("offs", "cido"), w=("sc32",))
    red(p, offc[0:32, 0:1], sc32[0:32, 0:8], OP.add, r=("sc32",), w=("offc",))
    x3v = x3_i.rearrange("(tt q) c -> tt q c", q=128)
    a3v = at3_i.rearrange("(tt q) c -> tt q c", q=128)
    n3v = xn3_i.rearrange("(tt q) c -> tt q c", q=128)
    ov = out_o.rearrange("(tt q) c -> tt q c", q=128)
    for t_ in range(NT):
        sl = t_ % 2
        tt(p, ohb[:, :], t1c[:, t_:t_ + 1].broadcast_to((128, 32)), iotE[:, :],
           OP.is_equal, r=("t1c", "iotE"), w=("ohb",))
        trans(p, [(ptf[0:32, 0:128], ohb[:, :])], idf, r=("ohb", "identf"), w=("ptf2",))
        cp(p, "dve", ohbT[0:32, 0:128], ptf[0:32, 0:128], r=("ptf2",), w=("ohbT",))
        mm(p, [(ptf[0:128, 0:1], ohbT[0:32, 0:128], offc[0:32, 0:1], True, True)],
           r=("ohbT", "offc"), w=("ptf2",), tag=f"of{t_}")
        cp(p, "dve", offtok[:, 0:1], ptf[0:128, 0:1], r=("ptf2",), w=("offtok", "ptf2"))
        tt(p, offtok[:, 0:1], offtok[:, 0:1], posc[:, t_:t_ + 1], OP.add,
           r=("offtok", "posc"), w=("offtok",))
        ts(p, m1[:, 0:1], offtok[:, 0:1], f32(float(EC)), OP.is_lt,
           r=("offtok",), w=("m1",))
        stt(p, slotc[:, 0:1], t1c[:, t_:t_ + 1], f32(float(EC)), offtok[:, 0:1],
            OP.mult, OP.add, r=("t1c", "offtok"), w=("slotc",))
        ts(p, slotc[:, 0:1], slotc[:, 0:1], f32(float(-BIGY)), OP.add,
           r=("slotc",), w=("slotc",))
        tt(p, slotc[:, 0:1], slotc[:, 0:1], m1[:, 0:1], OP.mult,
           r=("slotc", "m1"), w=("slotc",))
        ts(p, slotc[:, 0:1], slotc[:, 0:1], f32(float(BIGY)), OP.add,
           r=("slotc",), w=("slotc",))
        act(p, idxi[:, 0:1], slotc[:, 0:1], AF.Identity, r=("slotc",), w=("idxi",))
        p.dma("pool", lambda e, ii=idxi[:, 0:1], d=ygat[:, :]:
              e.indirect_dma_start(out=d, out_offset=None, in_=y_i[:, :],
                                   in_offset=bass.IndirectOffsetOnAxis(ap=ii, axis=0),
                                   oob_is_err=False),
              r=("idxi",), w=("ygat",), tag=f"gat{t_}")
        ld(p, "sp", s1[:, sl, :], n3v[t_], w=(f"s1{sl}",))
        ts(p, s2[:, sl, :], ygat[:, :], gatec[:, t_:t_ + 1], OP.mult,
           r=("ygat", "gatec"), w=(f"s2{sl}",))
        tt(p, s1[:, sl, :], s1[:, sl, :], s2[:, sl, :], OP.add,
           r=(f"s1{sl}", f"s2{sl}"), w=(f"s1{sl}",))
        ld(p, "sp", s2[:, sl, :], x3v[t_], w=(f"s2{sl}",))
        tt(p, s1[:, sl, :], s1[:, sl, :], s2[:, sl, :], OP.add,
           r=(f"s1{sl}", f"s2{sl}"), w=(f"s1{sl}",))
        ld(p, "sp", s2[:, sl, :], a3v[t_], w=(f"s2{sl}",))
        tt(p, s1[:, sl, :], s1[:, sl, :], s2[:, sl, :], OP.add,
           r=(f"s1{sl}", f"s2{sl}"), w=(f"s1{sl}",))
        st(p, "act", ov[t_], s1[:, sl, :], r=(f"s1{sl}",), w=("out_o",))
    return p


def stage_lg(x2, at2, ff2, L3a, L3f):
    bc = lambda v: np.ascontiguousarray(np.broadcast_to(v, (128, D)), np.float32)
    return {
        "x2": np.ascontiguousarray(x2, np.float32),
        "at2": np.ascontiguousarray(at2, np.float32),
        "ff2": np.ascontiguousarray(ff2, np.float32),
        "wqT": np.ascontiguousarray(L3a["Wq"].T, np.float32),
        "wkT0": np.ascontiguousarray(L3a["Wk"][0].T, np.float32),
        "wkT1": np.ascontiguousarray(L3a["Wk"][1].T, np.float32),
        "wvT0": np.ascontiguousarray(L3a["Wv"][0].T, np.float32),
        "wvT1": np.ascontiguousarray(L3a["Wv"][1].T, np.float32),
        "woT": np.ascontiguousarray(L3a["Wo"].T, np.float32),
        "bq_bc": bc(L3a["bq"]),
        "bk_bc0": bc(L3a["bk"][0]), "bk_bc1": bc(L3a["bk"][1]),
        "bv_bc0": bc(L3a["bv"][0]), "bv_bc1": bc(L3a["bv"][1]),
        "bo_bc": bc(L3a["bo"]),
        "lng_bc": bc(L3a["ln_g"]), "lnb_bc": bc(L3a["ln_b"]),
        "wrout": np.ascontiguousarray(L3f["w_router"], np.float32),
        "iotaE": np.arange(128, dtype=np.float32).reshape(128, 1),
        "identb": np.eye(128, dtype=ml_dtypes.bfloat16),
        "identf": np.eye(128, dtype=np.float32),
    }


def stage_lh(c, xn3_all, cnts, t1_all, pos_all, L3f):
    BIG = 1 << 22
    xnf = np.zeros((TGLOB + 1, D), np.float32)
    xnf[0:TGLOB] = xn3_all
    return {
        "xnf": xnf, "cnts": np.ascontiguousarray(cnts, np.float32),
        "t1g": np.ascontiguousarray(t1_all.transpose(1, 0, 2), np.float32),
        "posg": np.ascontiguousarray(pos_all.transpose(1, 0, 2), np.float32),
        "wg": np.ascontiguousarray(L3f["w_gate"][4 * c:4 * c + 4], np.float32),
        "wu": np.ascontiguousarray(L3f["w_up"][4 * c:4 * c + 4], np.float32),
        "wd": np.ascontiguousarray(L3f["w_down"][4 * c:4 * c + 4], np.float32),
        "cidk": np.full((128, 1), -2560.0 * c - BIG, np.float32),
        "c4lo": np.full((128, 1), 4 * c - 0.5, np.float32),
        "c4hi": np.full((128, 1), 4 * c + 3.5, np.float32),
        "identb": np.eye(128, dtype=ml_dtypes.bfloat16),
        "identf": np.eye(128, dtype=np.float32),
    }


def stage_li(c, y_all, cnts, t1, posn, gaten, x3, at3, xn3):
    yf = np.zeros((32 * EC + 1, D), ml_dtypes.bfloat16)
    yf[0:32 * EC] = y_all
    oh = np.zeros((1, 8), np.float32); oh[0, c] = 1.0
    return {
        "y_full": yf, "cnts": np.ascontiguousarray(cnts, np.float32),
        "t1": np.ascontiguousarray(t1, np.float32),
        "posn": np.ascontiguousarray(posn, np.float32),
        "gaten": np.ascontiguousarray(gaten, np.float32),
        "x3": np.ascontiguousarray(x3, np.float32),
        "at3": np.ascontiguousarray(at3, np.float32),
        "xn3": np.ascontiguousarray(xn3, np.float32),
        "cido": np.ascontiguousarray(np.broadcast_to(oh, (32, 8)), np.float32),
        "identf": np.eye(128, dtype=np.float32),
    }


def _pos_row_to_tok(row):
    return np.ascontiguousarray(row.reshape(16, 128).T, np.float32)


def build_ln2():
    p = Prog("ln2")
    x_i = p.inp("x", (T, D))
    identf = p.inp("identf", (128, 128), F32)
    xn_o = p.outp("xn_o", (NT, 128, D), F32)
    xnT_o = p.outp("xnT_o", (NCH, 128, T), BF16)
    alloc_ln(p, "l.")
    idf = p.sb("identf_s", [128, 128], F32)
    xnT = p.sb("xnT", [128, NCH, T], BF16)
    pq = p.ps("pq", [128, 512], F32)
    ld(p, "sp", idf[:], identf[:], w=("identf",))
    emit_ln_from_hbm_rot(p, x_i, "l.", xnT, pq, None, xn_store_f32=xn_o)
    for c in range(NCH):
        st(p, "act", xnT_o[c], xnT[:, c, :], r=XNT_KEYS, w=("xnT_o",))
    return p



def _r8(build, stagers):
    p = build()
    res = p.run(stagers)
    return res.results


def kernel(**inp):
    X = np.ascontiguousarray(np.asarray(inp["L0_attn_x"]), np.float32)
    L0a = {k.replace("L0_attn_", ""): np.asarray(v) for k, v in inp.items()
           if k.startswith("L0_attn_") and k != "L0_attn_x"}
    L0f = {k.replace("L0_ffn_", ""): np.asarray(v) for k, v in inp.items() if k.startswith("L0_ffn_")}
    L1 = {k.replace("L1_attn_", ""): np.asarray(v) for k, v in inp.items() if k.startswith("L1_attn_")}
    L1f = {k.replace("L1_ffn_", ""): np.asarray(v) for k, v in inp.items() if k.startswith("L1_ffn_")}
    L2a = {k.replace("L2_attn_", ""): np.asarray(v) for k, v in inp.items() if k.startswith("L2_attn_")}
    L2f = {k.replace("L2_ffn_", ""): np.asarray(v) for k, v in inp.items() if k.startswith("L2_ffn_")}
    L3a = {k.replace("L3_attn_", ""): np.asarray(v) for k, v in inp.items() if k.startswith("L3_attn_")}
    L3f = {k.replace("L3_ffn_", ""): np.asarray(v) for k, v in inp.items() if k.startswith("L3_ffn_")}

    ra = _r8(build_la, [stage_la(X[c], L0a) for c in range(NC)])
    m8 = np.array([float(ra[c]["mx"][0, 0]) for c in range(NC)], np.float32)
    rb = _r8(build_lb, [stage_lb(np.asarray(ra[c]["xnT_o"]), m8, L0f) for c in range(NC)])
    mup8 = np.array([float(rb[c]["mup"][0, 0]) for c in range(NC)], np.float32)
    mgt8 = np.array([float(rb[c]["mgt"][0, 0]) for c in range(NC)], np.float32)
    rc = _r8(build_lc, [stage_lc(np.asarray(rb[c]["up_o"]), np.asarray(rb[c]["gt_o"]),
                                 mup8, mgt8) for c in range(NC)])
    mg28 = np.array([float(rc[c]["mg2"][0, 0]) for c in range(NC)], np.float32)
    rd = _r8(build_ld, [stage_ld(np.asarray(rc[c]["g2_o"]), np.asarray(rc[c]["upq_o"]),
                                 mg28) for c in range(NC)])
    mo8 = np.array([float(rd[c]["mo"][0, 0]) for c in range(NC)], np.float32)
    re_ = _r8(build_le, [stage_le(np.asarray(rd[c]["or_o"]), mo8, L0f, X[c],
                                  np.asarray(ra[c]["a0"])) for c in range(NC)])
    X1 = [np.asarray(re_[c]["x1"]) for c in range(NC)]

    r1a = _r8(build_l1a, [stage_l1a(X1[c], L1) for c in range(NC)])
    r1f = _r8(build_l1f, [stage_l1f(X1[c], np.asarray(r1a[c]["a1"]),
                                    np.asarray(r1a[c]["xn_o"]), L1f) for c in range(NC)])
    X2 = [np.asarray(r1f[c]["x2"]) for c in range(NC)]

    rl = _r8(build_ln2, [{"x": X2[c], "identf": np.eye(128, dtype=np.float32)}
                         for c in range(NC)])
    xn2 = np.stack([np.asarray(rl[c]["xn_o"]).reshape(T, D)
                    for c in range(NC)], 0)
    xn2T = [np.asarray(rl[c]["xnT_o"]) for c in range(NC)]

    lf1_maps = []
    for j in range(NC):
        h_bsz = np.ascontiguousarray(
            xn2[:, j * 256:(j + 1) * 256, :].reshape(T, D), np.float32)
        hT = np.empty((NCH, 128, T), ml_dtypes.bfloat16)
        for s in range(8):
            hT[:, :, s * 256:(s + 1) * 256] = xn2T[s][:, :, j * 256:(j + 1) * 256]
        lf1_maps.append(stage_lf1(np.ascontiguousarray(hT), h_bsz, L2a))
    rf1 = _r8(build_lf1, lf1_maps)
    at2g = np.empty((8, 2048, D), np.float32)
    for j in range(NC):
        at2g[:, j * 256:(j + 1) * 256, :] = np.asarray(rf1[j]["at2"]).reshape(8, 256, D)

    rf2 = _r8(build_lf2, [stage_lf2(xn2[c], L2f) for c in range(NC)])

    rg = _r8(build_lg, [stage_lg(X2[c], at2g[c], np.asarray(rf2[c]["ff2"]), L3a, L3f)
                        for c in range(NC)])
    xn3_all = np.concatenate([np.asarray(rg[c]["xn3"]) for c in range(NC)], 0)
    cnts = np.stack([np.asarray(rg[c]["cnt"])[0] for c in range(NC)], 1)
    t1_all = np.stack([np.asarray(rg[c]["top1"]) for c in range(NC)], 0)
    pos_all = np.stack([_pos_row_to_tok(np.asarray(rg[c]["pos"])) for c in range(NC)], 0)

    rh = _r8(build_lh, [stage_lh(c, xn3_all, cnts, t1_all, pos_all, L3f)
                        for c in range(NC)])
    y_all = np.concatenate([np.asarray(rh[c]["y_me"]) for c in range(NC)], 0)
    ri = _r8(build_li, [stage_li(c, y_all, cnts, t1_all[c], pos_all[c],
                                 np.asarray(rg[c]["gate"]), np.asarray(rg[c]["x3"]),
                                 np.asarray(rg[c]["at3"]), np.asarray(rg[c]["xn3"]))
                        for c in range(NC)])
    out = np.stack([np.asarray(ri[c]["out"], np.float32) for c in range(NC)], 0)
    return out


# version 07 (final)
# speedup vs baseline: 1.0585x; 1.0585x over previous
import contextlib
import numpy as np
import ml_dtypes
import concourse.bass as bass
import concourse.mybir as mybir
from concourse.alu_op_type import AluOpType as OP
from concourse.bass_utils import run_bass_kernel_spmd

F32 = mybir.dt.float32
BF16 = mybir.dt.bfloat16
I32 = mybir.dt.int32
U8 = mybir.dt.uint8
U32 = mybir.dt.uint32
AF = mybir.ActivationFunctionType
AX = mybir.AxisListType
f32 = np.float32
NC = 8
T = 2048
D = 1024
NT = T // 128
NCH = D // 128

ENGS = ("pe", "act", "dve", "pool", "sp")


class Prog:

    def __init__(self, name="prog"):
        self.nc = bass.Bass("TRN2", target_bir_lowering=False, debug=False)
        self.name = name
        self.stack = contextlib.ExitStack()
        self.ops = []
        self.last_w = {}
        self.readers = {}
        self._eng_cnt = {e: 0 for e in ENGS}
        self._sems = {}
        self._dma_cnt = {}
        self.t = {}
        self._done = False

    def inp(self, name, shape, dtype=F32):
        ap = self.nc.dram_tensor(name, list(shape), dtype, kind="ExternalInput").ap()
        self.t[name] = ap
        return ap

    def outp(self, name, shape, dtype=F32):
        ap = self.nc.dram_tensor(name, list(shape), dtype, kind="ExternalOutput").ap()
        self.t[name] = ap
        return ap

    def dscratch(self, name, shape, dtype=F32):
        ap = self.nc.dram_tensor(name, list(shape), dtype).ap()
        self.t[name] = ap
        return ap

    def sb(self, name, shape, dtype=F32):
        h = self.stack.enter_context(self.nc.sbuf_tensor(name, list(shape), dtype))
        self.t[name] = h
        return h

    def ps(self, name, shape, dtype=F32):
        h = self.stack.enter_context(self.nc.psum_tensor(name, list(shape), dtype))
        self.t[name] = h
        return h

    def _record(self, eng, fn, r, w, dma, tag):
        i = len(self.ops)
        deps = set()
        for k in r:
            j = self.last_w.get(k)
            if j is not None:
                deps.add(j)
        for k in w:
            j = self.last_w.get(k)
            if j is not None:
                deps.add(j)
            deps.update(self.readers.get(k, ()))
        deps.discard(i)
        for k in r:
            self.readers.setdefault(k, []).append(i)
        for k in w:
            self.last_w[k] = i
            self.readers[k] = []
        if dma:
            skey = "d:" + eng + ":" + "|".join(sorted(w))
            n = self._dma_cnt.get(skey, 0)
            self._dma_cnt[skey] = n + 1
            val = 16 * (n + 1)
        else:
            skey = "c:" + eng
            n = self._eng_cnt[eng]
            self._eng_cnt[eng] = n + 1
            val = n + 1
        self.ops.append(dict(i=i, eng=eng, fn=fn, deps=deps, sem=skey, val=val,
                             dma=dma, tag=tag))

    def op(self, eng, fn, r=(), w=(), tag=""):
        self._record(eng, fn, tuple(r), tuple(w), False, tag)

    def dma(self, eng, fn, r=(), w=(), tag=""):
        self._record(eng, fn, tuple(r), tuple(w), True, tag)

    def _sem(self, key):
        if key not in self._sems:
            self._sems[key] = self.stack.enter_context(
                self.nc.semaphore(f"s{len(self._sems)}"))
        return self._sems[key]

    def finalize(self):
        assert not self._done
        self._done = True
        for rec in self.ops:
            self._sem(rec["sem"])
        finals = {}
        for rec in self.ops:
            finals[rec["sem"]] = max(finals.get(rec["sem"], 0), rec["val"])
        block = self.stack.enter_context(self.nc.Block())

        def mk(engname):
            def section(e):
                wm = {}
                for rec in self.ops:
                    if rec["eng"] != engname:
                        continue
                    needed = {}
                    for j in rec["deps"]:
                        d = self.ops[j]
                        if (engname == "pe" and not d["dma"] and d["eng"] == "pe"):
                            continue
                        k = d["sem"]
                        if needed.get(k, 0) < d["val"]:
                            needed[k] = d["val"]
                    for k in sorted(needed):
                        v = needed[k]
                        if wm.get(k, 0) >= v:
                            continue
                        e.wait_ge(self._sem(k), v)
                        wm[k] = v
                    inst = rec["fn"](e)
                    if inst is None:
                        inst = e.drain()
                    inst.then_inc(self._sem(rec["sem"]), 16 if rec["dma"] else 1)
                if engname == "sp":
                    for k in sorted(finals):
                        if wm.get(k, 0) < finals[k]:
                            e.wait_ge(self._sem(k), finals[k])
            return section

        block.sync(mk("sp"))
        block.scalar(mk("act"))
        block.vector(mk("dve"))
        block.gpsimd(mk("pool"))
        block.tensor(mk("pe"))
        self.stack.close()
        return self.nc

    def run(self, in_maps, **kw):
        nc = self.finalize()
        return run_bass_kernel_spmd(nc, in_maps, core_ids=list(range(len(in_maps))), **kw)



def ld(p, eng, dst_ap, src_ap, w, r=(), tag="ld"):
    p.dma(eng, lambda e, d=dst_ap, s=src_ap: e.dma_start(out=d, in_=s), r=r, w=w, tag=tag)


def st(p, eng, dst_ap, src_ap, r, w, tag="st"):
    p.dma(eng, lambda e, d=dst_ap, s=src_ap: e.dma_start(out=d, in_=s), r=r, w=w, tag=tag)


def mm(p, mms, r, w, tag="mm"):
    def fn(e, mms=mms):
        inst = None
        for (o, lh, rh, s0, s1) in mms:
            inst = e.matmul(o, lh, rh, start=s0, stop=s1)
        return inst
    p.op("pe", fn, r=r, w=w, tag=tag)


def trans(p, pairs, ident, r, w, tag="tr"):
    idap = ident[:, :] if hasattr(ident, "ap") else ident
    def fn(e, pairs=pairs, idap=idap):
        inst = None
        for (o, i_) in pairs:
            inst = e.transpose(o, i_, idap)
        return inst
    p.op("pe", fn, r=tuple(r) + ("ident",), w=w, tag=tag)


def cp(p, eng, dst, src, r, w, tag="cp"):
    if eng == "act":
        p.op("act", lambda e, d=dst, s=src: e.activation(d, s, AF.Identity, bias=f32(0.0)),
             r=r, w=w, tag=tag)
    else:
        p.op(eng, lambda e, d=dst, s=src: e.tensor_copy(d, s), r=r, w=w, tag=tag)


def _co(v):
    return float(v) if isinstance(v, (np.floating, int)) else v


def ts(p, dst, src, s1, op1, s2=None, op2=None, r=(), w=(), eng="dve", tag="ts", accum=None):
    op2 = OP.bypass if op2 is None else op2
    s1 = _co(s1); s2 = _co(s2)
    def fn(e, d=dst, s=src, a=s1, b=s2, o1=op1, o2=op2, acc=accum):
        return e.tensor_scalar(d, s, a, b, o1, o2, accum_out=acc)
    p.op(eng, fn, r=r, w=w, tag=tag)


def tt(p, dst, a, b, alu, r=(), w=(), eng="dve", tag="tt"):
    p.op(eng, lambda e, d=dst, x=a, y=b, o=alu: e.tensor_tensor(d, x, y, o), r=r, w=w, tag=tag)


def stt(p, dst, in0, scalar, in1, op0, op1, r=(), w=(), tag="stt"):
    scalar = _co(scalar)
    p.op("dve", lambda e, d=dst, a=in0, s=scalar, b=in1, o0=op0, o1=op1:
         e.scalar_tensor_tensor(d, a, s, b, o0, o1), r=r, w=w, tag=tag)


def act(p, dst, src, func, bias=f32(0.0), scale=f32(1.0), accum=None, r=(), w=(), tag="act"):
    def fn(e, d=dst, s=src, fu=func, b=bias, sc=scale, acc=accum):
        return e.activation(d, s, fu, bias=b, scale=sc, accum_out=acc)
    p.op("act", fn, r=r, w=w, tag=tag)


def red(p, dst, src, alu, absval=None, negate=None, r=(), w=(), tag="red"):
    p.op("dve", lambda e, d=dst, s=src, o=alu, a=absval, n=negate:
         e.tensor_reduce(d, s, AX.X, o, apply_absolute_value=a, negate=n),
         r=r, w=w, tag=tag)


def memset(p, ap_, val, w, tag="memset"):
    p.op("pool", lambda e, a=ap_, v=val: e.memset(a, v), r=(), w=w, tag=tag)



def emit_ln_from_hbm(p, x_hbm, pfx, xn, xnT, pt_bf, with_absmax=None):
    stg = p.t[pfx + "stg"]
    bn = p.t[pfx + "bn"]
    mv = p.t[pfx + "mv"]
    sd = p.t[pfx + "sd"]
    rstd = p.t[pfx + "rstd"]
    ngm = p.t[pfx + "ngm"]
    xv = x_hbm.rearrange("(tt q) c -> tt q c", q=128)
    for t_ in range(NT):
        sl = t_ % 2
        kstg = f"{pfx}stg{sl}"
        ld(p, "sp", stg[:, sl, :], xv[t_], w=(kstg,))
        p.op("dve", lambda e, s=stg[:, sl, 0:512], d=bn[:, 0:6]: e.bn_stats(d, s),
             r=(kstg,), w=(pfx + "bn",))
        p.op("dve", lambda e, s=stg[:, sl, 512:1024], d=bn[:, 6:12]: e.bn_stats(d, s),
             r=(kstg,), w=(pfx + "bn",))
        p.op("dve", lambda e, s=bn[:, 0:12], d=mv[:, 0:2]: e.bn_aggr(d, s),
             r=(pfx + "bn",), w=(pfx + "mv",))
        act(p, sd[:, 0:1], mv[:, 1:2], AF.Sqrt, bias=f32(1e-5),
            r=(pfx + "mv",), w=(pfx + "sd",))
        p.op("dve", lambda e, d=rstd[:, t_:t_ + 1], s=sd[:, 0:1]: e.reciprocal(d, s),
             r=(pfx + "sd",), w=(pfx + "rstd",))
        stt(p, ngm[:, t_:t_ + 1], mv[:, 0:1], f32(-1.0), rstd[:, t_:t_ + 1],
            OP.mult, OP.mult, r=(pfx + "mv", pfx + "rstd"), w=(pfx + "ngm",))
        act(p, xn[:, t_, :], stg[:, sl, :], AF.Identity,
            bias=ngm[:, t_:t_ + 1], scale=rstd[:, t_:t_ + 1],
            r=(kstg, pfx + "rstd", pfx + "ngm"), w=(f"xn{t_}",))
        if with_absmax is not None:
            red(p, with_absmax[:, t_:t_ + 1], xn[:, t_, :], OP.max, absval=True,
                r=(f"xn{t_}",), w=(pfx + "amx",))
        trans(p, [(pt_bf[:, c * 128:(c + 1) * 128], xn[:, t_, c * 128:(c + 1) * 128])
                  for c in range(NCH)], p.t["ident"],
              r=(f"xn{t_}",), w=("ptb",))
        cp(p, "dve", xnT[:, :, t_ * 128:(t_ + 1) * 128], pt_bf[:, 0:1024].rearrange("p (c q) -> p c q", q=128),
           r=("ptb",), w=(f"xnT{t_}",))


def alloc_ln(p, pfx, ptkey="pq"):
    p.t[pfx + "ptkey"] = ptkey
    p.sb(pfx + "xnf", [128, 2, 1024], F32)
    p.sb(pfx + "stg", [128, 2, 1024], F32)
    p.sb(pfx + "bn", [128, 12], F32)
    p.sb(pfx + "mv", [128, 2], F32)
    p.sb(pfx + "sd", [128, 1], F32)
    p.sb(pfx + "rstd", [128, 16], F32)
    p.sb(pfx + "ngm", [128, 16], F32)


XN_KEYS = tuple(f"xn{t}" for t in range(NT))
XNT_KEYS = tuple(f"xnT{t}" for t in range(NT))



def emit_causal_attn_head(p, h, qh, kh, vtok, yT, masktri, smax_chunks=4):
    sf = p.t["sf"]; abf = p.t["abf"]; at = p.t["at"]
    mx = p.t["mx4"]; lp = p.t["lp4"]; negm = p.t["negm"]; rec = p.t["rec"]
    ps = [p.t["ps0"], p.t["ps1"]]
    pt = p.t["ptb"]; py = p.t["pyb"]
    hp = (h % 2) * 64
    for qi in range(NT):
        kmax = (qi + 1) * 128
        nkc = (kmax + 511) // 512
        for kc in range(nkc):
            klo = kc * 512
            kw = min(512, kmax - klo)
            b = kc % 2
            pk = f"ps{b}"
            mm(p, [(ps[b][0:128, 0:kw], qh[0:64, qi * 128:(qi + 1) * 128],
                    kh[0:64, klo:klo + kw], True, True)],
               r=("qh", "kh"), w=(pk,), tag=f"s{h}.{qi}.{kc}")
            if klo <= qi * 128 < klo + kw:
                off = qi * 128 - klo
                tt(p, ps[b][0:128, off:off + 128], ps[b][0:128, off:off + 128],
                   masktri[:, 0:128], OP.add, r=(pk, "masktri"), w=(pk,))
            red(p, mx[:, kc:kc + 1], ps[b][0:128, 0:kw], OP.max, r=(pk,), w=("mx4",))
            cp(p, "dve", sf[:, klo:klo + kw], ps[b][0:128, 0:kw], r=(pk,), w=("sf",))
        red(p, negm[:, 0:1], mx[:, 0:nkc], OP.max, negate=True, r=("mx4",), w=("negm",))
        for kc in range(nkc):
            klo = kc * 512
            kw = min(512, kmax - klo)
            act(p, abf[:, klo:klo + kw], sf[:, klo:klo + kw], AF.Exp,
                bias=negm[:, 0:1], accum=lp[:, kc:kc + 1],
                r=("sf", "negm"), w=("abf", "lp4"))
        red(p, rec[:, 0:1], lp[:, 0:nkc], OP.add, r=("lp4",), w=("rec",))
        p.op("dve", lambda e, d=rec[:, 0:1], s=rec[:, 0:1]: e.reciprocal(d, s),
             r=("rec",), w=("rec",))
        ts(p, abf[:, 0:kmax], abf[:, 0:kmax], rec[:, 0:1], OP.mult,
           r=("abf", "rec"), w=("abf",))
        nkb = qi + 1
        for g in range(0, nkb, 8):
            ge = min(g + 8, nkb)
            trans(p, [(pt[:, (kb - g) * 128:(kb - g + 1) * 128],
                       abf[:, kb * 128:(kb + 1) * 128]) for kb in range(g, ge)],
                  p.t["ident"], r=("abf",), w=("ptb",))
            cp(p, "dve", at[:, g:ge, :],
               pt[:, 0:(ge - g) * 128].rearrange("p (c q) -> p c q", q=128),
               r=("ptb",), w=("at",))
        mm(p, [(py[hp:hp + 64, 0:128], vtok[:, kb, h * 64:(h + 1) * 64],
                at[:, kb, :], kb == 0, kb == nkb - 1) for kb in range(nkb)],
           r=("vtok", "at"), w=("pyb",), tag=f"av{h}.{qi}")
        cp(p, "dve", yT[hp:hp + 64, h // 2, qi * 128:(qi + 1) * 128],
           py[hp:hp + 64, 0:128], r=("pyb",), w=("yT",))



def build_la():
    p = Prog("la")
    x = p.inp("x", (T, D))
    wqT = p.inp("wqT", (D, D)); wkT = p.inp("wkT", (D, D))
    wvT = p.inp("wvT", (D, D)); woT = p.inp("woT", (D, D))
    bq_s = p.inp("bq_s", (128, 8)); bk_s = p.inp("bk_s", (128, 8))
    bv_bc = p.inp("bv_bc", (128, D)); bo_bc = p.inp("bo_bc", (128, D))
    identb = p.inp("identb", (128, 128), BF16)
    identf = p.inp("identf", (128, 128), F32)
    masktri = p.inp("masktri", (128, 128), F32)

    a0_o = p.outp("a0", (T, D), F32)
    xnT_o = p.outp("xnT_o", (NCH, 128, T), BF16)
    mx_o = p.outp("mx", (1, 1), F32)

    alloc_ln(p, "l.")
    xnT = p.sb("xnT", [128, NCH, T], BF16)
    amx = p.sb("amx", [128, 16], F32)
    wqb = p.sb("wqb", [128, 8, 1024], BF16)
    wkb = p.sb("wkb", [128, 8, 1024], BF16)
    wvb = p.sb("wvb", [128, 8, 1024], BF16)
    bqs = p.sb("bqs", [128, 8], F32); bks = p.sb("bks", [128, 8], F32)
    bvb = p.sb("bvb", [128, 1024], F32); bob = p.sb("bob", [128, 1024], F32)
    idb = p.sb("ident", [128, 128], BF16)
    idf = p.sb("identf_s", [128, 128], F32)
    mtri = p.sb("mtri", [128, 128], F32)
    vtok = p.sb("vtok", [128, NT, 1024], BF16)
    yT = p.sb("yT", [128, NCH, T], BF16)
    qh2 = p.sb("qh2", [64, T], BF16)
    kh2 = p.sb("kh2", [64, T], BF16)
    sf = p.sb("sf", [128, T], F32)
    abf = p.sb("abf", [128, T], BF16)
    at = p.sb("at", [128, 16, 128], BF16)
    mx4 = p.sb("mx4", [128, 4], F32)
    lp4 = p.sb("lp4", [128, 4], F32)
    negm = p.sb("negm", [128, 1], F32)
    rec = p.sb("rec", [128, 1], F32)
    a0stg = p.sb("a0stg", [128, 2, 1024], F32)
    mxc = p.sb("mxc", [128, 1], F32)
    mxr = p.sb("mxr", [1, 1], F32)
    ps0 = p.ps("ps0", [128, 512], F32)
    ps1 = p.ps("ps1", [128, 512], F32)
    ptb = p.ps("ptb", [128, 1024], BF16)
    pyb = p.ps("pyb", [128, 512], F32)
    pq = p.ps("pq", [128, 512], F32)
    pv = p.ps("pv", [128, 512], F32)
    pmx = p.ps("pmx", [128, 512], F32)

    ld(p, "sp", idb[:], identb[:], w=("ident",))
    ld(p, "sp", idf[:], identf[:], w=("identf",))
    ld(p, "sp", mtri[:], masktri[:], w=("masktri",))
    ld(p, "sp", bqs[:], bq_s[:], w=("bq",))
    ld(p, "sp", bks[:], bk_s[:], w=("bk",))
    ld(p, "sp", bvb[:], bv_bc[:], w=("bv",))
    ld(p, "sp", bob[:], bo_bc[:], w=("bo",))
    for wb, wd, key in ((wqb, wqT, "wq"), (wkb, wkT, "wk"), (wvb, wvT, "wv")):
        ld(p, "pool", wb[:, :, :], wd.rearrange("(c p) o -> p c o", p=128), w=(key,))
    ts(p, bqs[:], bqs[:], f32(0.125), OP.mult, r=("bq",), w=("bq",))

    emit_ln_from_hbm_rot(p, x, "l.", xnT, pq, amx)

    for c in range(NCH):
        st(p, "act", xnT_o[c], xnT[:, c, :], r=XNT_KEYS, w=("xnT_o",))

    red(p, mxc[:, 0:1], amx[:, 0:16], OP.max, r=("l.amx",), w=("mxc",))
    trans(p, [(pmx[0:1, 0:128], mxc[:, 0:1])], idf,
          r=("mxc", "identf"), w=("pmx",))
    red(p, mxr[0:1, 0:1], pmx[0:1, 0:128], OP.max, r=("pmx",), w=("mxr",))
    st(p, "act", mx_o[:], mxr[0:1, 0:1], r=("mxr",), w=("mx_o",))

    for t_ in range(NT):
        for oc in range(2):
            mm(p, [(pv[:, 0:512], xnT[:, ci, t_ * 128:(t_ + 1) * 128],
                    wvb[:, ci, oc * 512:(oc + 1) * 512], ci == 0, ci == 7)
                   for ci in range(NCH)],
               r=XNT_KEYS + ("wv",), w=("pv",), tag=f"v{t_}.{oc}")
            tt(p, vtok[:, t_, oc * 512:(oc + 1) * 512], pv[:, 0:512],
               bvb[:, oc * 512:(oc + 1) * 512], OP.add, r=("pv", "bv"), w=("vtok",))

    for h in range(16):
        for tc in range(4):
            mm(p, [(pq[0:64, 0:512], wqb[:, ci, h * 64:(h + 1) * 64],
                    xnT[:, ci, tc * 512:(tc + 1) * 512], ci == 0, ci == 7)
                   for ci in range(NCH)],
               r=XNT_KEYS + ("wq",), w=("pq",), tag=f"q{h}.{tc}")
            act(p, qh2[0:64, tc * 512:(tc + 1) * 512], pq[0:64, 0:512], AF.Identity,
                bias=bqs[(h % 2) * 64:(h % 2) * 64 + 64, h // 2:h // 2 + 1],
                scale=f32(0.125), r=("pq", "bq"), w=("qh",))
            mm(p, [(pq[0:64, 0:512], wkb[:, ci, h * 64:(h + 1) * 64],
                    xnT[:, ci, tc * 512:(tc + 1) * 512], ci == 0, ci == 7)
                   for ci in range(NCH)],
               r=XNT_KEYS + ("wk",), w=("pq",), tag=f"k{h}.{tc}")
            act(p, kh2[0:64, tc * 512:(tc + 1) * 512], pq[0:64, 0:512], AF.Identity,
                bias=bks[(h % 2) * 64:(h % 2) * 64 + 64, h // 2:h // 2 + 1],
                r=("pq", "bk"), w=("kh",))
        emit_causal_attn_head(p, h, qh2, kh2, vtok, yT, mtri)

    ld(p, "pool", wqb[:, :, :], woT.rearrange("(c p) o -> p c o", p=128), w=("wq",))
    for t_ in range(NT):
        sl = t_ % 2
        for oc in range(2):
            mm(p, [(pv[:, 0:512], yT[:, ic, t_ * 128:(t_ + 1) * 128],
                    wqb[:, ic, oc * 512:(oc + 1) * 512], ic == 0, ic == 7)
                   for ic in range(NCH)],
               r=("yT", "wq"), w=("pv",), tag=f"o{t_}.{oc}")
            tt(p, a0stg[:, sl, oc * 512:(oc + 1) * 512], pv[:, 0:512],
               bob[:, oc * 512:(oc + 1) * 512], OP.add,
               r=("pv", "bo"), w=(f"a0s{sl}",))
        st(p, "act", a0_o.rearrange("(tt q) c -> tt q c", q=128)[t_],
           a0stg[:, sl, :], r=(f"a0s{sl}",), w=("a0_o",))
    return p


def emit_ln_from_hbm_rot(p, x_hbm, pfx, xnT, ptf32, amx, xn_store_f32=None):
    stg = p.t[pfx + "stg"]
    bn = p.t[pfx + "bn"]
    mv = p.t[pfx + "mv"]
    sd = p.t[pfx + "sd"]
    rstd = p.t[pfx + "rstd"]
    ngm = p.t[pfx + "ngm"]
    xnf = p.t[pfx + "xnf"]
    ptk = p.t[pfx + "ptkey"]
    xv = x_hbm.rearrange("(tt q) c -> tt q c", q=128)
    for t_ in range(NT):
        sl = t_ % 2
        kstg = f"{pfx}stg{sl}"
        kxn = f"{pfx}xnf{sl}"
        ld(p, "sp", stg[:, sl, :], xv[t_], w=(kstg,))
        p.op("dve", lambda e, s=stg[:, sl, 0:512], d=bn[:, 0:6]: e.bn_stats(d, s),
             r=(kstg,), w=(pfx + "bn",))
        p.op("dve", lambda e, s=stg[:, sl, 512:1024], d=bn[:, 6:12]: e.bn_stats(d, s),
             r=(kstg,), w=(pfx + "bn",))
        p.op("dve", lambda e, s=bn[:, 0:12], d=mv[:, 0:2]: e.bn_aggr(d, s),
             r=(pfx + "bn",), w=(pfx + "mv",))
        act(p, sd[:, 0:1], mv[:, 1:2], AF.Sqrt, bias=f32(1e-5),
            r=(pfx + "mv",), w=(pfx + "sd",))
        p.op("dve", lambda e, d=rstd[:, t_:t_ + 1], s=sd[:, 0:1]: e.reciprocal(d, s),
             r=(pfx + "sd",), w=(pfx + "rstd",))
        stt(p, ngm[:, t_:t_ + 1], mv[:, 0:1], f32(-1.0), rstd[:, t_:t_ + 1],
            OP.mult, OP.mult, r=(pfx + "mv", pfx + "rstd"), w=(pfx + "ngm",))
        act(p, xnf[:, sl, :], stg[:, sl, :], AF.Identity,
            bias=ngm[:, t_:t_ + 1], scale=rstd[:, t_:t_ + 1],
            r=(kstg, pfx + "rstd", pfx + "ngm"), w=(kxn,))
        if amx is not None:
            red(p, amx[:, t_:t_ + 1], xnf[:, sl, :], OP.max, absval=True,
                r=(kxn,), w=(pfx + "amx",))
        if xn_store_f32 is not None:
            st(p, "act", xn_store_f32[t_], xnf[:, sl, :], r=(kxn,), w=("xnf_store",))
        for g in range(2):
            trans(p, [(ptf32[:, j * 128:(j + 1) * 128],
                       xnf[:, sl, (g * 4 + j) * 128:(g * 4 + j + 1) * 128])
                      for j in range(4)], p.t["identf_s"],
                  r=(kxn, "identf"), w=(ptk,))
            cp(p, "dve", xnT[:, g * 4:(g + 1) * 4, t_ * 128:(t_ + 1) * 128],
               ptf32[:, 0:512].rearrange("p (c q) -> p c q", q=128),
               r=(ptk,), w=(f"xnT{t_}",))


def _bcol(v):
    return np.ascontiguousarray(v.reshape(8, 128).T, dtype=np.float32)


def _tri_mask():
    m = np.zeros((128, 128), np.float32)
    m[np.triu_indices(128, 1)] = -1e30
    return m


def stage_la(xb, L0):
    return {
        "x": np.ascontiguousarray(xb, np.float32),
        "wqT": np.ascontiguousarray(L0["W_q"].T, np.float32),
        "wkT": np.ascontiguousarray(L0["W_k"].T, np.float32),
        "wvT": np.ascontiguousarray(L0["W_v"].T, np.float32),
        "woT": np.ascontiguousarray(L0["W_o"].T, np.float32),
        "bq_s": _bcol(L0["b_q"]), "bk_s": _bcol(L0["b_k"]),
        "bv_bc": np.ascontiguousarray(np.broadcast_to(L0["b_v"], (128, D)), np.float32),
        "bo_bc": np.ascontiguousarray(np.broadcast_to(L0["b_o"], (128, D)), np.float32),
        "identb": np.eye(128, dtype=ml_dtypes.bfloat16),
        "identf": np.eye(128, dtype=np.float32),
        "masktri": _tri_mask(),
    }



def emit_scale_from_max8(p, m8_ap, qmax, svec, rvec, s11, r11, pfx):
    m8s = p.t[pfx + "m8s"]
    ones = p.t[pfx + "ones"]
    pbc = p.t[pfx + "pbc"]
    memset(p, ones[0:1, 0:128], 1.0, w=(pfx + "ones",))
    ld(p, "sp", m8s[0:1, 0:8], m8_ap, w=(pfx + "m8",))
    red(p, s11[0:1, 0:1], m8s[0:1, 0:8], OP.max, r=(pfx + "m8",), w=(pfx + "s11",))
    ts(p, s11[0:1, 0:1], s11[0:1, 0:1], f32(1.0 / qmax), OP.mult, f32(1e-12), OP.max,
       r=(pfx + "s11",), w=(pfx + "s11",))
    p.op("dve", lambda e, d=s11[0:1, 1:2], s=s11[0:1, 0:1]: e.reciprocal(d, s),
         r=(pfx + "s11",), w=(pfx + "r11",))
    mm(p, [(pbc[0:128, 0:2], ones[0:1, 0:128], s11[0:1, 0:2], True, True)],
       r=(pfx + "s11", pfx + "r11", pfx + "ones"), w=(pfx + "pbc",))
    cp(p, "dve", svec[:, 0:1], pbc[0:128, 0:1], r=(pfx + "pbc",), w=(pfx + "sv",))
    cp(p, "dve", rvec[:, 0:1], pbc[0:128, 1:2], r=(pfx + "pbc",), w=(pfx + "rv",))


def alloc_scale8(p, pfx):
    p.sb(pfx + "m8s", [1, 8], F32)
    p.sb(pfx + "s11", [1, 2], F32)
    p.sb(pfx + "ones", [1, 128], F32)
    p.ps(pfx + "pbc", [128, 512], F32)
    sv = p.sb(pfx + "svec", [128, 1], F32)
    rv = p.sb(pfx + "rvec", [128, 1], F32)
    return sv, rv


def emit_qdq_i16(p, dst, src, rvec, svec, lo, hi, i32stg, rk, wk, pfx, extra_r=()):
    act(p, i32stg, src, AF.Identity, scale=rvec[:, 0:1],
        r=tuple(extra_r) + (rk, pfx + "rv"), w=("qi32",))
    ts(p, i32stg, i32stg, f32(hi), OP.min, f32(lo), OP.max, r=("qi32",), w=("qi32",))
    ts(p, dst, i32stg, svec[:, 0:1], OP.mult, r=("qi32", pfx + "sv"), w=(wk,))


def emit_lpbq_chunk(p, wstg_ap, nI, t1, ti, sc, rb, bq_out, kstg, kout):
    nb = nI // 32
    red(p, sc[:, 0:nb].rearrange("p (b o) -> p b o", o=1),
        wstg_ap.rearrange("p (b k) -> p b k", k=32), OP.max,
        absval=True, r=(kstg,), w=("lpb.sc",))
    ts(p, sc[:, 0:nb], sc[:, 0:nb], f32(1.0 / 7.0), OP.mult, f32(1e-12), OP.max,
       r=("lpb.sc",), w=("lpb.sc",))
    p.op("dve", lambda e, d=rb[:, 0:nb], s=sc[:, 0:nb]: e.reciprocal(d, s),
         r=("lpb.sc",), w=("lpb.rb",))
    tt(p, t1[:, 0:nI].rearrange("p (b k) -> p b k", k=32),
       wstg_ap.rearrange("p (b k) -> p b k", k=32),
       rb[:, 0:nb].rearrange("p (b k) -> p b k", k=1).broadcast_to((128, nb, 32)),
       OP.mult, r=(kstg, "lpb.rb"), w=("lpb.t1",))
    act(p, ti[:, 0:nI], t1[:, 0:nI], AF.Identity, r=("lpb.t1",), w=("lpb.ti",))
    ts(p, ti[:, 0:nI], ti[:, 0:nI], f32(7.0), OP.min, f32(-8.0), OP.max,
       r=("lpb.ti",), w=("lpb.ti",))
    tt(p, bq_out[:, 0:nI].rearrange("p (b k) -> p b k", k=32),
       ti[:, 0:nI].rearrange("p (b k) -> p b k", k=32),
       sc[:, 0:nb].rearrange("p (b k) -> p b k", k=1).broadcast_to((128, nb, 32)),
       OP.mult, r=("lpb.ti", "lpb.sc"), w=(kout,))


IB0 = 24


def build_lb():
    p = Prog("lb")
    xnT_i = p.inp("xnT_i", (NCH, 128, T), BF16)
    m8 = p.inp("m8", (1, 8), F32)
    wup = p.inp("wupN", (3072, D), F32)
    wgt = p.inp("wgtN", (3072, D), F32)
    identb = p.inp("identb", (128, 128), BF16)
    up_o = p.outp("up_o", (IB0, 128, T), F32)
    gt_o = p.outp("gt_o", (IB0, 128, T), F32)
    mup_o = p.outp("mup", (1, 1), F32)
    mgt_o = p.outp("mgt", (1, 1), F32)

    xnT = p.sb("xnT", [128, NCH, T], BF16)
    qx = p.sb("qx", [128, NCH, T], BF16)
    qi = p.sb("qi", [128, T], I32)
    sv, rv = alloc_scale8(p, "sx.")
    wstg = p.sb("wstg", [128, 2, D], F32)
    t1 = p.sb("t1", [128, D], F32)
    ti = p.sb("ti", [128, D], I32)
    sc = p.sb("sc", [128, 32], F32)
    rb = p.sb("rb", [128, 32], F32)
    wqn = p.sb("wqn", [128, D], BF16)
    wqT = p.sb("wqT", [128, 2, NCH, 128], BF16)
    idb = p.sb("ident", [128, 128], BF16)
    ustg = p.sb("ustg", [128, 2, T], F32)
    amx = p.sb("amx", [128, 2 * IB0], F32)
    amxg = p.sb("amxg", [128, 2 * IB0], F32)
    mc = p.sb("mc", [128, 1], F32)
    mr = p.sb("mr", [1, 1], F32)
    idf = p.sb("identf_s", [128, 128], F32)
    identf = p.inp("identf", (128, 128), F32)
    pup = p.ps("pup", [128, 512], F32)
    ptb = p.ps("ptb", [128, 1024], BF16)
    pmx = p.ps("pmx", [128, 512], F32)

    ld(p, "sp", idb[:], identb[:], w=("ident",))
    ld(p, "sp", idf[:], identf[:], w=("identf",))
    ld(p, "pool", xnT[:, :, :], xnT_i.rearrange("c p t -> p c t"), w=("xnTl",))
    emit_scale_from_max8(p, m8[:, :], 32767.0, sv, rv, p.t["sx.s11"], None, "sx.")
    for c in range(NCH):
        emit_qdq_i16(p, qx[:, c, :], xnT[:, c, :], rv, sv, -32768.0, 32767.0,
                     qi[:, :], "xnTl", f"qx{c}", "sx.")
    QXK = tuple(f"qx{c}" for c in range(NCH))

    def one_weight(w_hbm, out_hbm, amx_t, col0, tag):
        wv = w_hbm.rearrange("(ib q) c -> ib q c", q=128)
        ov = out_hbm
        for ib in range(IB0):
            sl = ib % 2
            kstg = f"wstg{sl}"
            ld(p, "sp", wstg[:, sl, :], wv[ib], w=(kstg,))
            emit_lpbq_chunk(p, wstg[:, sl, :], D, t1, ti, sc, rb, wqn, kstg, "wqn")
            trans(p, [(ptb[:, c * 128:(c + 1) * 128], wqn[:, c * 128:(c + 1) * 128])
                      for c in range(NCH)], idb, r=("wqn",), w=("ptb",))
            cp(p, "dve", wqT[:, sl, :, :],
               ptb[:, 0:1024].rearrange("p (c q) -> p c q", q=128),
               r=("ptb",), w=(f"wqT{sl}",))
            usl = ib % 2
            for tc in range(4):
                mm(p, [(pup[:, 0:512], wqT[:, sl, c, :], qx[:, c, tc * 512:(tc + 1) * 512],
                        c == 0, c == 7) for c in range(NCH)],
                   r=(f"wqT{sl}",) + QXK, w=("pup",), tag=f"{tag}{ib}.{tc}")
                cp(p, "dve", ustg[:, usl, tc * 512:(tc + 1) * 512], pup[:, 0:512],
                   r=("pup",), w=(f"ustg{usl}",))
            red(p, amx_t[:, col0 + ib:col0 + ib + 1], ustg[:, usl, :], OP.max,
                absval=True, r=(f"ustg{usl}",), w=("amxk" + tag,))
            st(p, "act", ov[ib], ustg[:, usl, :], r=(f"ustg{usl}",), w=("out" + tag,))

    one_weight(wup, up_o, amx, 0, "u")
    one_weight(wgt, gt_o, amxg, 0, "g")
    for amx_t, key, out in ((amx, "amxku", mup_o), (amxg, "amxkg", mgt_o)):
        red(p, mc[:, 0:1], amx_t[:, 0:IB0], OP.max, r=(key,), w=("mc",))
        trans(p, [(pmx[0:1, 0:128], mc[:, 0:1])], idf, r=("mc", "identf"), w=("pmx",))
        red(p, mr[0:1, 0:1], pmx[0:1, 0:128], OP.max, r=("pmx",), w=("mr",))
        st(p, "act", out[:], mr[0:1, 0:1], r=("mr",), w=("m_o" + key,))
    return p


def stage_lb(xnT, m8, L0f):
    return {
        "xnT_i": xnT, "m8": m8.reshape(1, 8).astype(np.float32),
        "wupN": np.ascontiguousarray(L0f["w_up"], np.float32),
        "wgtN": np.ascontiguousarray(L0f["w_gate"], np.float32),
        "identb": np.eye(128, dtype=ml_dtypes.bfloat16),
        "identf": np.eye(128, dtype=np.float32),
    }


def build_lc():
    p = Prog("lc")
    up_i = p.inp("up_i", (IB0, 128, T), F32)
    gt_i = p.inp("gt_i", (IB0, 128, T), F32)
    mup8 = p.inp("mup8", (1, 8), F32)
    mgt8 = p.inp("mgt8", (1, 8), F32)
    upq_o = p.outp("upq_o", (IB0, 128, T), BF16)
    g2_o = p.outp("g2_o", (IB0, 128, T), F32)
    mg2_o = p.outp("mg2", (1, 1), F32)
    identf = p.inp("identf", (128, 128), F32)

    stg = p.sb("stg", [128, 2, T], F32)
    qi = p.sb("qi", [128, T], I32)
    gq = p.sb("gq", [128, T], BF16)
    sg = p.sb("sg", [128, T], F32)
    sgi = p.sb("sgi", [128, T], I32)
    g2 = p.sb("g2", [128, 2, T], F32)
    uq = p.sb("uq", [128, 2, T], BF16)
    amx = p.sb("amx", [128, IB0], F32)
    mc = p.sb("mc", [128, 1], F32)
    mr = p.sb("mr", [1, 1], F32)
    idf = p.sb("identf_s", [128, 128], F32)
    svu, rvu = alloc_scale8(p, "su.")
    svg, rvg = alloc_scale8(p, "sg.")
    pmx = p.ps("pmx", [128, 512], F32)

    ld(p, "sp", idf[:], identf[:], w=("identf",))
    emit_scale_from_max8(p, mup8[:, :], 32767.0, svu, rvu, p.t["su.s11"], None, "su.")
    emit_scale_from_max8(p, mgt8[:, :], 32767.0, svg, rvg, p.t["sg.s11"], None, "sg.")
    for ib in range(IB0):
        sl = ib % 2
        ld(p, "sp", stg[:, sl, :], up_i[ib], w=(f"stg{sl}",))
        emit_qdq_i16(p, uq[:, sl, :], stg[:, sl, :], rvu, svu, -32768.0, 32767.0,
                     qi[:, :], f"stg{sl}", f"uq{sl}", "su.")
        st(p, "act", upq_o[ib], uq[:, sl, :], r=(f"uq{sl}",), w=("upq_o",))
        ld(p, "sp", stg[:, sl, :], gt_i[ib], w=(f"stg{sl}",))
        emit_qdq_i16(p, gq[:, :], stg[:, sl, :], rvg, svg, -32768.0, 32767.0,
                     qi[:, :], f"stg{sl}", "gq", "sg.")
        act(p, sg[:, :], gq[:, :], AF.Sigmoid, r=("gq",), w=("sg",))
        act(p, sgi[:, :], sg[:, :], AF.Identity, scale=f32(65536.0), r=("sg",), w=("sgi",))
        ts(p, sgi[:, :], sgi[:, :], f32(65535.0), OP.min, f32(0.0), OP.max,
           r=("sgi",), w=("sgi",))
        stt(p, g2[:, sl, :], sgi[:, :], f32(1.0 / 65536.0), gq[:, :], OP.mult, OP.mult,
            r=("sgi", "gq"), w=(f"g2{sl}",))
        red(p, amx[:, ib:ib + 1], g2[:, sl, :], OP.max, absval=True,
            r=(f"g2{sl}",), w=("amxk",))
        st(p, "act", g2_o[ib], g2[:, sl, :], r=(f"g2{sl}",), w=("g2_o",))
    red(p, mc[:, 0:1], amx[:, 0:IB0], OP.max, r=("amxk",), w=("mc",))
    trans(p, [(pmx[0:1, 0:128], mc[:, 0:1])], idf, r=("mc", "identf"), w=("pmx",))
    red(p, mr[0:1, 0:1], pmx[0:1, 0:128], OP.max, r=("pmx",), w=("mr",))
    st(p, "act", mg2_o[:], mr[0:1, 0:1], r=("mr",), w=("mg2_o",))
    return p


def build_ld():
    p = Prog("ld")
    g2_i = p.inp("g2_i", (IB0, 128, T), F32)
    upq_i = p.inp("upq_i", (IB0, 128, T), BF16)
    mg28 = p.inp("mg28", (1, 8), F32)
    or_o = p.outp("or_o", (IB0, 128, T), F32)
    mo_o = p.outp("mo", (1, 1), F32)
    identf = p.inp("identf", (128, 128), F32)

    stg = p.sb("stg", [128, 2, T], F32)
    ustg = p.sb("ustg", [128, 2, T], BF16)
    qi = p.sb("qi", [128, T], I32)
    g2q = p.sb("g2q", [128, T], BF16)
    orr = p.sb("orr", [128, 2, T], F32)
    amx = p.sb("amx", [128, IB0], F32)
    mc = p.sb("mc", [128, 1], F32)
    mr = p.sb("mr", [1, 1], F32)
    idf = p.sb("identf_s", [128, 128], F32)
    sv, rv = alloc_scale8(p, "sg2.")
    pmx = p.ps("pmx", [128, 512], F32)

    ld(p, "sp", idf[:], identf[:], w=("identf",))
    emit_scale_from_max8(p, mg28[:, :], 32767.0, sv, rv, p.t["sg2.s11"], None, "sg2.")
    for ib in range(IB0):
        sl = ib % 2
        ld(p, "sp", stg[:, sl, :], g2_i[ib], w=(f"stg{sl}",))
        ld(p, "sp", ustg[:, sl, :], upq_i[ib], w=(f"ustg{sl}",))
        emit_qdq_i16(p, g2q[:, :], stg[:, sl, :], rv, sv, -32768.0, 32767.0,
                     qi[:, :], f"stg{sl}", "g2q", "sg2.")
        tt(p, orr[:, sl, :], g2q[:, :], ustg[:, sl, :], OP.mult,
           r=("g2q", f"ustg{sl}"), w=(f"or{sl}",))
        red(p, amx[:, ib:ib + 1], orr[:, sl, :], OP.max, absval=True,
            r=(f"or{sl}",), w=("amxk",))
        st(p, "act", or_o[ib], orr[:, sl, :], r=(f"or{sl}",), w=("or_o",))
    red(p, mc[:, 0:1], amx[:, 0:IB0], OP.max, r=("amxk",), w=("mc",))
    trans(p, [(pmx[0:1, 0:128], mc[:, 0:1])], idf, r=("mc", "identf"), w=("pmx",))
    red(p, mr[0:1, 0:1], pmx[0:1, 0:128], OP.max, r=("pmx",), w=("mr",))
    st(p, "act", mo_o[:], mr[0:1, 0:1], r=("mr",), w=("mo_o",))
    return p


def build_le():
    p = Prog("le")
    or_i = p.inp("or_i", (IB0, 128, T), F32)
    mo8 = p.inp("mo8", (1, 8), F32)
    wdn = p.inp("wdnN", (D, 3072), F32)
    x_i = p.inp("x", (T, D), F32)
    a0_i = p.inp("a0", (T, D), F32)
    identb = p.inp("identb", (128, 128), BF16)
    x1_o = p.outp("x1", (T, D), F32)
    qo_d = p.dscratch("qo_d", (IB0, 128, T), BF16)

    stg = p.sb("stg", [128, 2, T], F32)
    qi = p.sb("qi", [128, T], I32)
    qo = p.sb("qo", [128, 2, T], BF16)
    wstg = p.sb("wstg", [128, 2, 3072], F32)
    t1 = p.sb("t1", [128, 3072], F32)
    ti = p.sb("ti", [128, 3072], I32)
    sc = p.sb("sc", [128, 96], F32)
    rb = p.sb("rb", [128, 96], F32)
    wqn = p.sb("wqn", [128, 3072], BF16)
    wdT = p.sb("wdT", [128, IB0, 1024], BF16)
    idb = p.sb("ident", [128, 128], BF16)
    qos = p.sb("qos", [128, IB0, 128], BF16)
    xst = p.sb("xst", [128, 2, D], F32)
    ast = p.sb("ast", [128, 2, D], F32)
    x1s = p.sb("x1s", [128, 2, D], F32)
    sv, rv = alloc_scale8(p, "so.")
    pdn = p.ps("pdn", [128, 512], F32)
    ptb = p.ps("ptb", [128, 1024], BF16)

    ld(p, "sp", idb[:], identb[:], w=("ident",))
    emit_scale_from_max8(p, mo8[:, :], 32767.0, sv, rv, p.t["so.s11"], None, "so.")
    for ib in range(IB0):
        sl = ib % 2
        ld(p, "sp", stg[:, sl, :], or_i[ib], w=(f"stg{sl}",))
        emit_qdq_i16(p, qo[:, sl, :], stg[:, sl, :], rv, sv, -32768.0, 32767.0,
                     qi[:, :], f"stg{sl}", f"qo{sl}", "so.")
        st(p, "act", qo_d[ib], qo[:, sl, :], r=(f"qo{sl}",), w=("qo_d",))
    wv = wdn.rearrange("(oc q) i -> oc q i", q=128)
    for oc in range(NCH):
        sl = oc % 2
        ld(p, "sp", wstg[:, sl, :], wv[oc], w=(f"wstg{sl}",))
        emit_lpbq_chunk(p, wstg[:, sl, :], 3072, t1, ti, sc, rb, wqn, f"wstg{sl}", "wqn")
        for g in range(3):
            trans(p, [(ptb[:, j * 128:(j + 1) * 128],
                       wqn[:, (g * 8 + j) * 128:(g * 8 + j + 1) * 128]) for j in range(8)],
                  idb, r=("wqn",), w=("ptb",))
            cp(p, "dve", wdT[:, g * 8:(g + 1) * 8, oc * 128:(oc + 1) * 128],
               ptb[:, 0:1024].rearrange("p (c q) -> p c q", q=128),
               r=("ptb",), w=("wdT",))
    qview = qo_d.rearrange("ib p t -> p ib t")
    xv = x_i.rearrange("(tt q) c -> tt q c", q=128)
    av = a0_i.rearrange("(tt q) c -> tt q c", q=128)
    x1v = x1_o.rearrange("(tt q) c -> tt q c", q=128)
    for t_ in range(NT):
        sl = t_ % 2
        ld(p, "sp", qos[:, :, :], qview[:, :, t_ * 128:(t_ + 1) * 128],
           w=("qos",), r=("qo_d",))
        ld(p, "sp", xst[:, sl, :], xv[t_], w=(f"xst{sl}",))
        ld(p, "sp", ast[:, sl, :], av[t_], w=(f"ast{sl}",))
        for oc in range(2):
            mm(p, [(pdn[:, 0:512], qos[:, ib, :], wdT[:, ib, oc * 512:(oc + 1) * 512],
                    ib == 0, ib == IB0 - 1) for ib in range(IB0)],
               r=("qos", "wdT"), w=("pdn",), tag=f"dn{t_}.{oc}")
            tt(p, x1s[:, sl, oc * 512:(oc + 1) * 512], pdn[:, 0:512],
               xst[:, sl, oc * 512:(oc + 1) * 512], OP.add,
               r=("pdn", f"xst{sl}"), w=(f"x1s{sl}",))
        tt(p, x1s[:, sl, :], x1s[:, sl, :], ast[:, sl, :], OP.add,
           r=(f"x1s{sl}", f"ast{sl}"), w=(f"x1s{sl}",))
        st(p, "act", x1v[t_], x1s[:, sl, :], r=(f"x1s{sl}",), w=("x1_o",))
    return p


def stage_lc(up, gt, mup8, mgt8):
    return {"up_i": up, "gt_i": gt, "mup8": mup8.reshape(1, 8).astype(np.float32),
            "mgt8": mgt8.reshape(1, 8).astype(np.float32),
            "identf": np.eye(128, dtype=np.float32)}


def stage_ld(g2, upq, mg28):
    return {"g2_i": g2, "upq_i": upq, "mg28": mg28.reshape(1, 8).astype(np.float32),
            "identf": np.eye(128, dtype=np.float32)}


def stage_le(orr, mo8, L0f, xb, a0):
    return {"or_i": orr, "mo8": mo8.reshape(1, 8).astype(np.float32),
            "wdnN": np.ascontiguousarray(L0f["w_down"], np.float32),
            "x": xb, "a0": a0,
            "identb": np.eye(128, dtype=ml_dtypes.bfloat16)}



FP4_MID = (0.25, 0.75, 1.25, 1.75, 2.5, 3.5, 5.0)
FP4_STEP = (0.5, 0.5, 0.5, 0.5, 1.0, 1.0, 2.0)


def emit_fp4_qdq(p, dst, src, F, sfx, rk, wk, out_extra=None):
    nb = F // 16
    sc = p.t["f4.sc"]; rb = p.t["f4.rb"]; u = p.t["f4.u"]; au = p.t["f4.au"]
    a1 = p.t["f4.acc"]; a2 = p.t["f4.acc2"]; sg = p.t["f4.sg"]
    K = lambda n: "f4." + n
    red(p, sc[:, 0:nb].rearrange("p (b o) -> p b o", o=1),
        src.rearrange("p (b k) -> p b k", k=16), OP.max, absval=True,
        r=(rk,), w=(K("sc"),))
    ts(p, sc[:, 0:nb], sc[:, 0:nb], f32(1.0 / 6.0), OP.mult, f32(1e-12), OP.max,
       r=(K("sc"),), w=(K("sc"),))
    p.op("dve", lambda e, d=rb[:, 0:nb], s=sc[:, 0:nb]: e.reciprocal(d, s),
         r=(K("sc"),), w=(K("rb"),))
    tt(p, u[:, 0:F].rearrange("p (b k) -> p b k", k=16),
       src.rearrange("p (b k) -> p b k", k=16),
       rb[:, 0:nb].rearrange("p (b k) -> p b k", k=1).broadcast_to((128, nb, 16)),
       OP.mult, r=(rk, K("rb")), w=(K("u"),))
    act(p, au[:, 0:F], u[:, 0:F], AF.Abs, r=(K("u"),), w=(K("au"),))
    accs = (a1, a2)
    sg2 = p.t["f4.sg2"]
    sgs = (sg, sg2)
    for j in range(7):
        sgj = sgs[j % 2]
        sgk = K("sg") if j % 2 == 0 else K("sg2")
        act(p, sgj[:, 0:F], au[:, 0:F], AF.Sign, bias=f32(-FP4_MID[j]),
            r=(K("au"),), w=(sgk,))
        if j == 0:
            ts(p, a1[:, 0:F], sgj[:, 0:F], float(FP4_STEP[0]), OP.mult,
               r=(sgk,), w=(K("acc"),))
        else:
            stt(p, accs[j % 2][:, 0:F], sgj[:, 0:F], float(FP4_STEP[j]),
                accs[(j + 1) % 2][:, 0:F], OP.mult, OP.add,
                r=(sgk, K("acc") if j % 2 == 1 else K("acc2")),
                w=(K("acc2") if j % 2 == 1 else K("acc"),))
    fin = accs[6 % 2]
    ts(p, a1[:, 0:F], fin[:, 0:F], f32(0.5), OP.mult, f32(3.0), OP.add,
       r=(K("acc"),), w=(K("acc"),))
    act(p, sg[:, 0:F], u[:, 0:F], AF.Sign, r=(K("u"),), w=(K("sg"),))
    tt(p, a2[:, 0:F].rearrange("p (b k) -> p b k", k=16),
       a1[:, 0:F].rearrange("p (b k) -> p b k", k=16),
       sc[:, 0:nb].rearrange("p (b k) -> p b k", k=1).broadcast_to((128, nb, 16)),
       OP.mult, r=(K("acc"), K("sc")), w=(K("acc2"),))
    tt(p, dst, a2[:, 0:F], sg[:, 0:F], OP.mult, r=(K("acc2"), K("sg")), w=(wk,))


def alloc_fp4(p, F):
    p.sb("f4.sc", [128, F // 16], F32)
    p.sb("f4.rb", [128, F // 16], F32)
    p.sb("f4.u", [128, F], F32)
    p.sb("f4.au", [128, F], F32)
    p.sb("f4.acc", [128, F], BF16)
    p.sb("f4.acc2", [128, F], BF16)
    p.sb("f4.sg", [128, F], BF16)
    p.sb("f4.sg2", [128, F], BF16)


def rope_tables(t, dim):
    inv = 1.0 / (10000.0 ** (np.arange(0, dim, 2, dtype=np.float64) / dim))
    freqs = np.arange(t, dtype=np.float64)[:, None] * inv[None, :]
    emb = np.concatenate([freqs, freqs], axis=-1)
    return np.cos(emb).astype(np.float32), np.sin(emb).astype(np.float32)


def _tile_tok(a):
    return np.ascontiguousarray(a.reshape(16, 128, -1).transpose(1, 0, 2))


def build_l1a(upto=99):
    p = Prog("l1a")
    x_i = p.inp("x", (T, D))
    wqd = p.inp("wqd", (D, D)); wkd = p.inp("wkd", (D, D)); wvd = p.inp("wvd", (D, D))
    wc_i = p.inp("wc", (D, D))
    wupq0 = p.inp("wupq0", (128, 64))
    wupq1 = p.inp("wupq1", (128, 64))
    wupk0 = p.inp("wupk0", (128, 64))
    wupk1 = p.inp("wupk1", (128, 64))
    wupv0 = p.inp("wupv0", (128, 64))
    wupv1 = p.inp("wupv1", (128, 64))
    cs_ev = p.inp("cs_ev", (128, 16, 16)); cs_od = p.inp("cs_od", (128, 16, 16))
    sn_ev = p.inp("sn_ev", (128, 16, 16)); sn_od = p.inp("sn_od", (128, 16, 16))
    identb = p.inp("identb", (128, 128), BF16)
    masktri = p.inp("masktri", (128, 128), F32)

    a1_o = p.outp("a1", (T, D), F32)
    xn_o = p.outp("xn_o", (NT, 128, D), F32)
    xnT_o = p.outp("xnT_o", (NCH, 128, T), BF16)

    alloc_ln(p, "l.")
    xnT = p.sb("xnT", [128, NCH, T], BF16)
    wqdb = p.sb("wqdb", [128, 8, 1024], BF16)
    wkdb = p.sb("wkdb", [128, 8, 1024], BF16)
    wvdb = p.sb("wvdb", [128, 8, 1024], BF16)
    wupqb = [p.sb(f"wupqb{i}", [128, 64], BF16) for i in range(2)]
    wupkb = [p.sb(f"wupkb{i}", [128, 64], BF16) for i in range(2)]
    wupvb = [p.sb(f"wupvb{i}", [128, 64], BF16) for i in range(2)]
    csod = {n: p.sb(n + "_s", [128, 16, 16], BF16) for n in ("cs_ev", "cs_od", "sn_ev", "sn_od")}
    idb = p.sb("ident", [128, 128], BF16)
    idf = p.sb("identf_s", [128, 128], F32)
    identf = p.inp("identf", (128, 128), F32)
    mtri = p.sb("mtri", [128, 128], F32)
    latq = p.sb("latq", [128, T], BF16)
    latk = p.sb("latk", [128, T], BF16)
    latv = p.sb("latv", [128, T], BF16)
    qtok = p.sb("qtok", [128, 16, 64], BF16)
    ktok = p.sb("ktok", [128, 16, 64], BF16)
    rt_ev = p.sb("rt_ev", [128, 16, 16], BF16)
    rt_od = p.sb("rt_od", [128, 16, 16], BF16)
    vtok = p.sb("vtok", [128, NT, 1024], BF16)
    yT = p.sb("yT", [128, NCH, T], BF16)
    qh2 = p.sb("qh2", [64, T], BF16)
    kh2 = p.sb("kh2", [64, T], BF16)
    sf = p.sb("sf", [128, T], F32)
    abf = p.sb("abf", [128, T], BF16)
    at = p.sb("at", [128, 16, 128], BF16)
    mx4 = p.sb("mx4", [128, 4], F32)
    lp4 = p.sb("lp4", [128, 4], F32)
    negm = p.sb("negm", [128, 1], F32)
    rec = p.sb("rec", [128, 1], F32)
    ps0 = p.ps("ps0", [128, 512], F32)
    ps1 = p.ps("ps1", [128, 512], F32)
    ptb = p.ps("ptb", [128, 1024], BF16)
    pyb = p.ps("pyb", [128, 512], F32)
    pq = p.ps("pq", [128, 512], F32)
    pv = p.ps("pv", [128, 512], F32)

    ld(p, "sp", idb[:], identb[:], w=("ident",))
    ld(p, "sp", idf[:], identf[:], w=("identf",))
    ld(p, "sp", mtri[:], masktri[:], w=("masktri",))
    for n in ("cs_ev", "cs_od", "sn_ev", "sn_od"):
        ld(p, "pool", csod[n][:, :, :], p.t[n][:, :, :], w=(n + "s",))
    for wb, wd, key in ((wqdb, wqd, "wqd"), (wkdb, wkd, "wkd"), (wvdb, wvd, "wvd")):
        ld(p, "pool", wb[:, :, :], wd.rearrange("(c p) o -> p c o", p=128), w=(key,))
    for i in range(2):
        for wb, wd, key in ((wupqb[i], p.t[f"wupq{i}"], "wupq"),
                            (wupkb[i], p.t[f"wupk{i}"], "wupk"),
                            (wupvb[i], p.t[f"wupv{i}"], "wupv")):
            ld(p, "pool", wb[:, :], wd[:, :], w=(key,))

    emit_ln_from_hbm_rot(p, x_i, "l.", xnT, pq, None, xn_store_f32=xn_o)
    for c in range(NCH):
        st(p, "act", xnT_o[c], xnT[:, c, :], r=XNT_KEYS, w=("xnT_o",))

    ROPE_T = tuple(n + "s" for n in ("cs_ev", "cs_od", "sn_ev", "sn_od"))

    def rope_inplace(p, tok, key):
        qe = tok[:, :, 32:64].rearrange("p t (d e) -> p t d e", e=2)
        ev = qe[:, :, :, 0:1]
        od = qe[:, :, :, 1:2]
        c_ev = csod["cs_ev"][:, :, :].rearrange("p t (d o) -> p t d o", o=1)
        c_od = csod["cs_od"][:, :, :].rearrange("p t (d o) -> p t d o", o=1)
        s_ev = csod["sn_ev"][:, :, :].rearrange("p t (d o) -> p t d o", o=1)
        s_od = csod["sn_od"][:, :, :].rearrange("p t (d o) -> p t d o", o=1)
        t_ev = rt_ev[:, :, :].rearrange("p t (d o) -> p t d o", o=1)
        t_od = rt_od[:, :, :].rearrange("p t (d o) -> p t d o", o=1)
        u_ev = p.t["f4u_ev"][:, :, :].rearrange("p t (d o) -> p t d o", o=1)
        u_od = p.t["f4u_od"][:, :, :].rearrange("p t (d o) -> p t d o", o=1)
        tt(p, t_ev, ev, c_ev, OP.mult, r=(key,) + ROPE_T, w=("rt_ev",))
        tt(p, u_ev, od, s_ev, OP.mult, r=(key,) + ROPE_T, w=("u_ev",))
        tt(p, t_ev, t_ev, u_ev, OP.subtract, r=("rt_ev", "u_ev"), w=("rt_ev",))
        tt(p, t_od, od, c_od, OP.mult, r=(key,) + ROPE_T, w=("rt_od",))
        tt(p, u_od, ev, s_od, OP.mult, r=(key,) + ROPE_T, w=("u_od",))
        tt(p, t_od, t_od, u_od, OP.add, r=("rt_od", "u_od"), w=("rt_od",))
        cp(p, "dve", ev, t_ev, r=("rt_ev",), w=(key,))
        cp(p, "dve", od, t_od, r=("rt_od",), w=(key,))

    p.sb("f4u_ev", [128, 16, 16], BF16)
    p.sb("f4u_od", [128, 16, 16], BF16)

    for hp in range(8 if upto >= 4 else (1 if upto >= 1 else 0)):
        if upto < 1:
            break
        for lat, wdk, key in ((latq, "wqd", "latq"), (latk, "wkd", "latk"), (latv, "wvd", "latv")):
            wb = p.t[wdk + "b"]
            for tc in range(4):
                mm(p, [(pq[:, 0:512], wb[:, c, hp * 128:(hp + 1) * 128],
                        xnT[:, c, tc * 512:(tc + 1) * 512], c == 0, c == 7)
                       for c in range(NCH)],
                   r=XNT_KEYS + (wdk,), w=("pq",), tag=f"lat{hp}")
                cp(p, "dve", lat[:, tc * 512:(tc + 1) * 512], pq[:, 0:512],
                   r=("pq",), w=(key,))
        for hi in range(2 if upto >= 2 else 0):
            h = hp * 2 + hi
            b0 = hi * 64
            for t_ in range(NT):
                mm(p, [(pv[:, 0:64], latq[:, t_ * 128:(t_ + 1) * 128],
                        wupqb[hi][:, 0:64], True, True),
                       (pv[:, 64:128], latk[:, t_ * 128:(t_ + 1) * 128],
                        wupkb[hi][:, 0:64], True, True),
                       (pv[:, 128:192], latv[:, t_ * 128:(t_ + 1) * 128],
                        wupvb[hi][:, 0:64], True, True)],
                   r=("latq", "latk", "latv", "wupq", "wupk", "wupv"),
                   w=("pv",), tag=f"up{h}.{t_}")
                act(p, qtok[:, t_, :], pv[:, 0:64], AF.Identity, scale=f32(0.125),
                    r=("pv",), w=("qtok", "pv"))
                cp(p, "dve", ktok[:, t_, :], pv[:, 64:128], r=("pv",), w=("ktok", "pv"))
                cp(p, "dve", vtok[:, t_, h * 64:(h + 1) * 64], pv[:, 128:192],
                   r=("pv",), w=("vtok", "pv"))
            if upto >= 3.5:
                rope_inplace(p, qtok, "qtok")
                rope_inplace(p, ktok, "ktok")
            for g in range(2 if upto >= 3 else 0):
                trans(p, [(ptb[0:64, j * 128:(j + 1) * 128], qtok[:, g * 8 + j, :])
                          for j in range(8)], idb, r=("qtok",), w=("ptb",))
                cp(p, "dve", qh2[0:64, g * 1024:(g + 1) * 1024], ptb[0:64, 0:1024],
                   r=("ptb",), w=("qh",))
                trans(p, [(ptb[0:64, j * 128:(j + 1) * 128], ktok[:, g * 8 + j, :])
                          for j in range(8)], idb, r=("ktok",), w=("ptb",))
                cp(p, "dve", kh2[0:64, g * 1024:(g + 1) * 1024], ptb[0:64, 0:1024],
                   r=("ptb",), w=("kh",))
            if upto >= 4:
                emit_causal_attn_head(p, h, qh2, kh2, vtok, yT, mtri)

    ld(p, "pool", wqdb[:, :, :], wc_i.rearrange("(c p) o -> p c o", p=128), w=("wqd",))
    for t_ in range(NT if upto >= 4 else 0):
        sl = t_ % 2
        for oc in range(2):
            mm(p, [(pv[:, 0:512], yT[:, ic, t_ * 128:(t_ + 1) * 128],
                    wqdb[:, ic, oc * 512:(oc + 1) * 512], ic == 0, ic == 7)
                   for ic in range(NCH)],
               r=("yT", "wqd"), w=("pv",), tag=f"wc{t_}.{oc}")
            cp(p, "dve", p.t["l.stg"][:, sl, oc * 512:(oc + 1) * 512], pv[:, 0:512],
               r=("pv",), w=(f"l.stg{sl}",))
        st(p, "act", a1_o.rearrange("(tt q) c -> tt q c", q=128)[t_],
           p.t["l.stg"][:, sl, :], r=(f"l.stg{sl}",), w=("a1_o",))
    if upto < 4:
        memset(p, p.t["l.stg"][:, 0, :], 0.0, w=("l.stg0",))
        for t_ in range(NT):
            st(p, "act", a1_o.rearrange("(tt q) c -> tt q c", q=128)[t_],
               p.t["l.stg"][:, 0, :], r=("l.stg0",), w=("a1_o",))
    return p


def stage_l1a(xb, L1):
    cos, sin = rope_tables(T, 32)
    csT = _tile_tok(cos)
    snT = _tile_tok(sin)
    def dup(a, half):
        z = np.zeros_like(a)
        parts = [a, z] if half == 0 else [z, a]
        return np.ascontiguousarray(np.concatenate(parts, 0), np.float32)
    return {
        "x": np.ascontiguousarray(xb, np.float32),
        "wqd": np.ascontiguousarray(L1["Wq_down"], np.float32),
        "wkd": np.ascontiguousarray(L1["Wk_down"], np.float32),
        "wvd": np.ascontiguousarray(L1["Wv_down"], np.float32),
        "wc": np.ascontiguousarray(L1["Wc"], np.float32),
        "wupq0": dup(np.concatenate([L1["Wq_up_c"], L1["Wq_up_e"]], 1), 0),
        "wupq1": dup(np.concatenate([L1["Wq_up_c"], L1["Wq_up_e"]], 1), 1),
        "wupk0": dup(np.concatenate([L1["Wk_up_c"], L1["Wk_up_e"]], 1), 0),
        "wupk1": dup(np.concatenate([L1["Wk_up_c"], L1["Wk_up_e"]], 1), 1),
        "wupv0": dup(L1["Wv_up"], 0),
        "wupv1": dup(L1["Wv_up"], 1),
        "cs_ev": np.ascontiguousarray(csT[:, :, 0::2]),
        "cs_od": np.ascontiguousarray(csT[:, :, 1::2]),
        "sn_ev": np.ascontiguousarray(snT[:, :, 0::2]),
        "sn_od": np.ascontiguousarray(snT[:, :, 1::2]),
        "identb": np.eye(128, dtype=ml_dtypes.bfloat16),
        "identf": np.eye(128, dtype=np.float32),
        "masktri": _tri_mask(),
    }


IB1 = 22


def build_l1f():
    p = Prog("l1f")
    x_i = p.inp("x", (T, D))
    a1_i = p.inp("a1", (T, D))
    xn_i = p.inp("xn_i", (NT, 128, D))
    wg_i = p.inp("wgN", (2816, D))
    wu_i = p.inp("wuN", (2816, D))
    wd_i = p.inp("wdN", (D, 2816))
    identb = p.inp("identb", (128, 128), BF16)
    x2_o = p.outp("x2", (T, D), F32)
    h_d = p.dscratch("h_d", (NT, 128, 2816), F32)

    alloc_fp4(p, 2816)
    xn = p.sb("xn", [128, 2, D], F32)
    qxn = p.sb("qxn", [128, 2, D], BF16)
    qxnT = p.sb("qxnT", [128, NCH, T], BF16)
    idb = p.sb("ident", [128, 128], BF16)
    wstg = p.sb("wstg", [128, 2, D], F32)
    wq4 = p.sb("wq4", [128, D], BF16)
    wqT = p.sb("wqT", [128, 2, NCH, 128], BF16)
    wu4T = p.sb("wu4T", [128, 2, NCH, 128], BF16)
    gs = p.sb("gs", [128, 128], F32)
    us = p.sb("us", [128, 128], F32)
    hs = p.sb("hs", [128, 128], F32)
    hstg = p.sb("hstg", [128, 2816], F32)
    qh = p.sb("qh", [128, 2816], BF16)
    qhT = p.sb("qhT", [128, IB1, 128], BF16)
    wdstg = p.sb("wdstg", [128, 2, 2816], F32)
    wd4 = p.sb("wd4", [128, 2816], BF16)
    wdT = p.sb("wdT", [128, IB1, 1024], BF16)
    pg = p.ps("pg", [128, 512], F32)
    pu = p.ps("pu", [128, 512], F32)
    ptb = p.ps("ptb", [128, 1024], BF16)
    pdn = p.ps("pdn", [128, 512], F32)

    ld(p, "sp", idb[:], identb[:], w=("ident",))
    for t_ in range(NT):
        sl = t_ % 2
        ld(p, "sp", xn[:, sl, :], xn_i[t_], w=(f"xn{sl}",))
        emit_fp4_qdq(p, qxn[:, sl, :], xn[:, sl, :], D, f"x{t_}", f"xn{sl}", f"qxn{sl}")
        trans(p, [(ptb[:, c * 128:(c + 1) * 128], qxn[:, sl, c * 128:(c + 1) * 128])
                  for c in range(NCH)], idb, r=(f"qxn{sl}",), w=("ptb",))
        cp(p, "dve", qxnT[:, :, t_ * 128:(t_ + 1) * 128],
           ptb[:, 0:1024].rearrange("p (c q) -> p c q", q=128),
           r=("ptb",), w=("qxnT",))
    wdv = wd_i.rearrange("(oc q) i -> oc q i", q=128)
    for oc in range(NCH):
        sl = oc % 2
        ld(p, "sp", wdstg[:, sl, :], wdv[oc], w=(f"wdstg{sl}",))
        emit_fp4_qdq(p, wd4[:, :], wdstg[:, sl, :], 2816, f"wd{oc}", f"wdstg{sl}", "wd4")
        for g in range(3):
            n = min(8, IB1 - g * 8)
            trans(p, [(ptb[:, j * 128:(j + 1) * 128],
                       wd4[:, (g * 8 + j) * 128:(g * 8 + j + 1) * 128]) for j in range(n)],
                  idb, r=("wd4",), w=("ptb",))
            cp(p, "dve", wdT[:, g * 8:g * 8 + n, oc * 128:(oc + 1) * 128],
               ptb[:, 0:n * 128].rearrange("p (c q) -> p c q", q=128),
               r=("ptb",), w=("wdT",))
    wgv = wg_i.rearrange("(ib q) c -> ib q c", q=128)
    wuv = wu_i.rearrange("(ib q) c -> ib q c", q=128)
    for ib in range(IB1):
        sl = ib % 2
        ld(p, "sp", wstg[:, sl, :], wgv[ib], w=(f"wstg{sl}",))
        emit_fp4_qdq(p, wq4[:, :], wstg[:, sl, :], D, f"wg{ib}", f"wstg{sl}", "wq4")
        trans(p, [(ptb[:, c * 128:(c + 1) * 128], wq4[:, c * 128:(c + 1) * 128])
                  for c in range(NCH)], idb, r=("wq4",), w=("ptb",))
        cp(p, "dve", wqT[:, sl, :, :], ptb[:, 0:1024].rearrange("p (c q) -> p c q", q=128),
           r=("ptb",), w=(f"wqT{sl}",))
        ld(p, "sp", wstg[:, sl, :], wuv[ib], w=(f"wstg{sl}",))
        emit_fp4_qdq(p, wq4[:, :], wstg[:, sl, :], D, f"wu{ib}", f"wstg{sl}", "wq4")
        trans(p, [(ptb[:, c * 128:(c + 1) * 128], wq4[:, c * 128:(c + 1) * 128])
                  for c in range(NCH)], idb, r=("wq4",), w=("ptb",))
        cp(p, "dve", wu4T[:, sl, :, :], ptb[:, 0:1024].rearrange("p (c q) -> p c q", q=128),
           r=("ptb",), w=(f"wu4T{sl}",))
        for t_ in range(NT):
            mm(p, [(pg[:, 0:128], qxnT[:, c, t_ * 128:(t_ + 1) * 128], wqT[:, sl, c, :],
                    c == 0, c == 7) for c in range(NCH)],
               r=(f"wqT{sl}", "qxnT"), w=("pg",), tag=f"g{ib}.{t_}")
            mm(p, [(pu[:, 0:128], qxnT[:, c, t_ * 128:(t_ + 1) * 128], wu4T[:, sl, c, :],
                    c == 0, c == 7) for c in range(NCH)],
               r=(f"wu4T{sl}", "qxnT"), w=("pu",), tag=f"u{ib}.{t_}")
            act(p, gs[:, :], pg[:, 0:128], AF.Silu, r=("pg",), w=("gs", "pg"))
            cp(p, "dve", us[:, :], pu[:, 0:128], r=("pu",), w=("us", "pu"))
            tt(p, hs[:, :], gs[:, :], us[:, :], OP.mult, r=("gs", "us"), w=("hs",))
            st(p, "act", h_d[t_, :, ib * 128:(ib + 1) * 128],
               hs[:, :], r=("hs",), w=("h_d",))
    hv = h_d
    xv = x_i.rearrange("(tt q) c -> tt q c", q=128)
    av = a1_i.rearrange("(tt q) c -> tt q c", q=128)
    x2v = x2_o.rearrange("(tt q) c -> tt q c", q=128)
    for t_ in range(NT):
        sl = t_ % 2
        ld(p, "sp", hstg[:, :], hv[t_], w=("hstg",), r=("h_d",))
        emit_fp4_qdq(p, qh[:, :], hstg[:, :], 2816, f"h{t_}", "hstg", "qh")
        for g in range(3):
            n = min(8, IB1 - g * 8)
            trans(p, [(ptb[:, j * 128:(j + 1) * 128],
                       qh[:, (g * 8 + j) * 128:(g * 8 + j + 1) * 128]) for j in range(n)],
                  idb, r=("qh",), w=("ptb",))
            cp(p, "dve", qhT[:, g * 8:g * 8 + n, :],
               ptb[:, 0:n * 128].rearrange("p (c q) -> p c q", q=128),
               r=("ptb",), w=("qhT",))
        xst = wdstg[:, sl, 0:1024]
        ast = wdstg[:, sl, 1024:2048]
        kst = f"wdstg{sl}"
        ld(p, "sp", xst, xv[t_], w=(kst,))
        ld(p, "sp", ast, av[t_], w=(kst,))
        for oc in range(2):
            mm(p, [(pdn[:, 0:512], qhT[:, ib, :], wdT[:, ib, oc * 512:(oc + 1) * 512],
                    ib == 0, ib == IB1 - 1) for ib in range(IB1)],
               r=("qhT", "wdT"), w=("pdn",), tag=f"dn{t_}.{oc}")
            tt(p, xst[:, oc * 512:(oc + 1) * 512], pdn[:, 0:512],
               xst[:, oc * 512:(oc + 1) * 512], OP.add,
               r=("pdn", kst), w=(kst,))
        tt(p, xst, xst, ast, OP.add, r=(kst,), w=(kst,))
        st(p, "act", x2v[t_], xst, r=(kst,), w=("x2_o",))
    return p


def stage_l1f(xb, a1, xn, L1f):
    return {"x": xb, "a1": a1, "xn_i": xn,
            "wgN": np.ascontiguousarray(L1f["w_gate"], np.float32),
            "wuN": np.ascontiguousarray(L1f["w_up"], np.float32),
            "wdN": np.ascontiguousarray(L1f["w_down"], np.float32),
            "identb": np.eye(128, dtype=ml_dtypes.bfloat16)}



def build_lf1():
    p = Prog("lf1")
    hT_i = p.inp("hT_i", (NCH, 128, T), BF16)
    h_i = p.inp("h_i", (T, D))
    wqkvT = p.inp("wqkvT", (D, 3088))
    woT = p.inp("woT", (D, D))
    lng_bc = p.inp("lng_bc", (128, D))
    lnb_bc = p.inp("lnb_bc", (128, D))
    at_o = p.outp("at2", (T, D), F32)

    xnT = p.sb("xnT", [128, NCH, T], BF16)
    wqh = p.sb("wqh", [128, 8, 208], BF16)
    qkb = p.sb("qkb", [128, 16, 208], BF16)
    x2b = p.sb("x2b", [128, 16, 128], F32)
    hq3 = p.sb("hq3", [128, 16, 256], F32)
    hk3 = p.sb("hk3", [128, 16, 256], F32)
    dacc = p.sb("dacc", [128, 16, 256], F32)
    hv = p.sb("hv", [128, 16, 64], BF16)
    hb = p.sb("hb", [128, 16, 1], F32)
    kd = p.sb("kd", [128, 16, 1], F32)
    den = p.sb("den", [128, 16, 1], F32)
    rtmp = p.sb("rtmp", [128, 16, 1], F32)
    amat = p.sb("amat", [128, 2, 8, 8], F32)
    cmat = p.sb("cmat", [128, 2, 8, 8], F32)
    nv = p.sb("nv", [128, 2, 8, 64], F32)
    ttp = p.sb("ttp", [128, 256], F32)
    ttp2 = p.sb("ttp2", [128, 256], F32)
    acc1 = p.sb("acc1", [128, 64], F32)
    acc2 = p.sb("acc2", [128, 64], F32)
    lo = p.sb("lo", [128, 16, 1024], BF16)
    loT = p.sb("loT", [128, NCH, T], BF16)
    wob = p.sb("wob", [128, 8, 1024], BF16)
    lngb = p.sb("lngb", [128, D], F32)
    lnbb = p.sb("lnbb", [128, D], F32)
    hstg = p.sb("hstg", [128, 2, D], F32)
    idb = p.sb("ident", [128, 128], BF16)
    identb = p.inp("identb", (128, 128), BF16)
    for nm, shp in (("l.bn", [128, 12]), ("l.mv", [128, 2]), ("l.sd", [128, 1]),
                    ("l.rstd", [128, 1]), ("l.ngm", [128, 1])):
        p.sb(nm, shp, F32)
    pq = p.ps("pq", [128, 512], F32)
    pv = p.ps("pv", [128, 512], F32)
    ptb = p.ps("ptb", [128, 1024], BF16)

    ld(p, "sp", idb[:], identb[:], w=("ident",))
    ld(p, "pool", xnT[:, :, :], hT_i.rearrange("c p t -> p c t"), w=("xnTl",))
    ld(p, "pool", wob[:, :, :], woT.rearrange("(c p) o -> p c o", p=128), w=("wo",))
    ld(p, "sp", lngb[:], lng_bc[:], w=("lng",))
    ld(p, "sp", lnbb[:], lnb_bc[:], w=("lnb",))

    EPS = 1e-5

    def ttr(dst, in0, in1, rr, ww):
        p.op("dve", lambda e, d=dst, a=in0, b=in1:
             e.tensor_tensor_reduce(d, a, b, 1.0, 0.0, OP.mult, OP.add, accum_out=None)
             if False else e.tensor_tensor_scan(d, a, b, 0.0, OP.bypass, OP.bypass),
             r=rr, w=ww)

    for h in range(16):
        ld(p, "pool", wqh[:, :, 0:193],
           wqkvT.rearrange("(c p) o -> p c o", p=128)[:, :, h * 193:(h + 1) * 193],
           w=("wqh",))
        for t_ in range(NT):
            pb = (pq, pv)[t_ % 2]
            pbk = ("pq", "pv")[t_ % 2]
            mm(p, [(pb[:, 0:193], xnT[:, c, t_ * 128:(t_ + 1) * 128], wqh[:, c, 0:193],
                    c == 0, c == 7) for c in range(NCH)],
               r=("wqh", "xnTl"), w=(pbk,), tag=f"qkb{h}.{t_}")
            cp(p, "dve", qkb[:, t_, 0:193], pb[:, 0:193], r=(pbk,), w=("qkb", pbk))
        act(p, hb[:, :, :], qkb[:, :, 192:193], AF.Sigmoid, r=("qkb",), w=("hb",))
        for (dst, src0) in ((hq3, 0), (hk3, 64)):
            k = "hq3" if src0 == 0 else "hk3"
            act(p, x2b[:, :, 0:64], qkb[:, :, src0:src0 + 64], AF.Relu,
                r=("qkb",), w=("x2b",))
            act(p, x2b[:, :, 64:128], qkb[:, :, src0:src0 + 64], AF.Relu,
                scale=f32(-1.0), r=("qkb",), w=("x2b",))
            for i_ in (1, 2):
                o0 = 128 * (i_ - 1)
                tt(p, dst[:, :, o0 + i_:o0 + 128], x2b[:, :, i_:128],
                   x2b[:, :, 0:128 - i_], OP.mult, r=("x2b",), w=(k,))
                tt(p, dst[:, :, o0:o0 + i_], x2b[:, :, 0:i_],
                   x2b[:, :, 128 - i_:128], OP.mult, r=("x2b",), w=(k,))
            red(p, rtmp[:, :, :], dst[:, :, :], OP.add, r=(k,), w=("rtmp",))
            p.op("dve", lambda e, d=rtmp[:, :, :], s=rtmp[:, :, :]: e.reciprocal(d, s),
                 r=("rtmp",), w=("rtmp",))
            tt(p, dst[:, :, :], dst[:, :, :],
               rtmp[:, :, :].broadcast_to((128, 16, 256)), OP.mult,
               r=(k, "rtmp"), w=(k,))
        cp(p, "dve", hv[:, :, :], qkb[:, :, 128:192], r=("qkb",), w=("hv",))
        for bh in range(2):
            cp(p, "dve", dacc[:, bh, :].rearrange("p (a b) -> p a b", b=256),
               hk3[:, bh, :].rearrange("p (a b) -> p a b", b=256), r=("hk3",), w=("dacc",))
        for s in range(1, 8):
            for bh in range(2):
                t_ = s * 2 + bh
                tt(p, dacc[:, t_, :], dacc[:, t_ - 2, :], hk3[:, t_, :], OP.add,
                   r=("dacc", "hk3"), w=("dacc",))
        memset(p, kd[:, 0:2, :], 1.0, w=("kd",))
        for s in range(1, 8):
            for bh in range(2):
                t_ = s * 2 + bh
                tt(p, ttp[:, 0:256], dacc[:, t_ - 2, :], hk3[:, t_, :], OP.mult,
                   r=("dacc", "hk3"), w=("ttp",))
                red(p, kd[:, t_, :], ttp[:, 0:256], OP.add, r=("ttp",), w=("kd",))
        tt(p, hb[:, :, :], hb[:, :, :], kd[:, :, :], OP.mult, r=("hb", "kd"), w=("hb",))
        ts(p, rtmp[:, :, :], kd[:, :, :], f32(EPS), OP.add, r=("kd",), w=("rtmp",))
        p.op("dve", lambda e, d=rtmp[:, :, :], s=rtmp[:, :, :]: e.reciprocal(d, s),
             r=("rtmp",), w=("rtmp",))
        tt(p, hk3[:, :, :], hk3[:, :, :],
           rtmp[:, :, :].broadcast_to((128, 16, 256)), OP.mult,
           r=("hk3", "rtmp"), w=("hk3",))
        for t_ in range(NT):
            tt(p, ttp[:, 0:256], dacc[:, t_, :], hq3[:, t_, :], OP.mult,
               r=("dacc", "hq3"), w=("ttp",))
            red(p, den[:, t_, :], ttp[:, 0:256], OP.add, r=("ttp",), w=("den",))
        for bh in range(2):
            for s in range(8):
                for sp_ in range(s + 1):
                    if sp_ < s:
                        par = (s + sp_) % 2
                        tb, tbk = ((ttp, "ttp"), (ttp2, "ttp2"))[par]
                        tt(p, tb[:, 0:256], hk3[:, s * 2 + bh, :],
                           hk3[:, sp_ * 2 + bh, :], OP.mult, r=("hk3",), w=(tbk,),
                           eng=("dve", "pool")[par])
                        red(p, amat[:, bh, s, sp_:sp_ + 1], tb[:, 0:256], OP.add,
                            r=(tbk,), w=("amat",))
                    par = (s + sp_ + 1) % 2
                    tb, tbk = ((ttp, "ttp"), (ttp2, "ttp2"))[par]
                    tt(p, tb[:, 0:256], hq3[:, s * 2 + bh, :],
                       hk3[:, sp_ * 2 + bh, :], OP.mult, r=("hk3", "hq3"), w=(tbk,),
                       eng=("dve", "pool")[par])
                    red(p, cmat[:, bh, s, sp_:sp_ + 1], tb[:, 0:256], OP.add,
                        r=(tbk,), w=("cmat",))
        for bh in range(2):
            for s in range(8):
                t_ = s * 2 + bh
                if s == 0:
                    tt(p, nv[:, bh, 0, :], hv[:, t_, :],
                       hb[:, t_, :].broadcast_to((128, 64)), OP.mult,
                       r=("hv", "hb"), w=("nv",))
                    continue
                ts(p, acc1[:, :], nv[:, bh, 0, :], amat[:, bh, s, 0:1], OP.mult,
                   r=("nv", "amat"), w=("acc1",))
                cur = 0
                for sp_ in range(1, s):
                    src = (acc1, acc2)[cur]
                    dst = (acc1, acc2)[1 - cur]
                    stt(p, dst[:, :], nv[:, bh, sp_, :], amat[:, bh, s, sp_:sp_ + 1],
                        src[:, :], OP.mult, OP.add,
                        r=("nv", "amat", ("acc1", "acc2")[cur]),
                        w=(("acc1", "acc2")[1 - cur],))
                    cur = 1 - cur
                fin = ("acc1", "acc2")[cur]
                tt(p, (acc2, acc1)[cur][:, :], hv[:, t_, :], (acc1, acc2)[cur][:, :],
                   OP.subtract, r=("hv", fin), w=(("acc2", "acc1")[cur],))
                ts(p, nv[:, bh, s, :], (acc2, acc1)[cur][:, :], hb[:, t_, :], OP.mult,
                   r=(("acc2", "acc1")[cur], "hb"), w=("nv",))
        for bh in range(2):
            for s in range(8):
                t_ = s * 2 + bh
                ts(p, acc1[:, :], nv[:, bh, 0, :], cmat[:, bh, s, 0:1], OP.mult,
                   r=("nv", "cmat"), w=("acc1",))
                cur = 0
                for sp_ in range(1, s + 1):
                    src = (acc1, acc2)[cur]
                    dst = (acc1, acc2)[1 - cur]
                    stt(p, dst[:, :], nv[:, bh, sp_, :], cmat[:, bh, s, sp_:sp_ + 1],
                        src[:, :], OP.mult, OP.add,
                        r=("nv", "cmat", ("acc1", "acc2")[cur]),
                        w=(("acc1", "acc2")[1 - cur],))
                    cur = 1 - cur
                fin = ("acc1", "acc2")[cur]
                ts(p, rtmp[:, t_, :], den[:, t_, :], f32(EPS), OP.add,
                   r=("den",), w=("rtmp",))
                p.op("dve", lambda e, d=rtmp[:, t_, :], s2=rtmp[:, t_, :]:
                     e.reciprocal(d, s2), r=("rtmp",), w=("rtmp",))
                ts(p, lo[:, t_, h * 64:(h + 1) * 64], (acc1, acc2)[cur][:, :],
                   rtmp[:, t_, :], OP.mult, f32(0.125), OP.mult,
                   r=(fin, "rtmp"), w=("lo",))
    for t_ in range(NT):
        trans(p, [(ptb[:, c * 128:(c + 1) * 128], lo[:, t_, c * 128:(c + 1) * 128])
                  for c in range(NCH)], idb, r=("lo",), w=("ptb",))
        cp(p, "dve", loT[:, :, t_ * 128:(t_ + 1) * 128],
           ptb[:, 0:1024].rearrange("p (c q) -> p c q", q=128),
           r=("ptb",), w=("loT",))
    hview = h_i.rearrange("(tt q) c -> tt q c", q=128)
    oview = at_o.rearrange("(tt q) c -> tt q c", q=128)
    bnl = p.t["l.bn"]; mvl = p.t["l.mv"]; sdl = p.t["l.sd"]
    rstdl = p.t["l.rstd"]; ngml = p.t["l.ngm"]
    for t_ in range(NT):
        sl = t_ % 2
        ld(p, "sp", hstg[:, sl, :], hview[t_], w=(f"hstg{sl}",))
        for oc in range(2):
            mm(p, [(pv[:, 0:512], loT[:, ic, t_ * 128:(t_ + 1) * 128],
                    wob[:, ic, oc * 512:(oc + 1) * 512], ic == 0, ic == 7)
                   for ic in range(NCH)],
               r=("loT", "wo"), w=("pv",), tag=f"wo{t_}.{oc}")
            tt(p, hstg[:, sl, oc * 512:(oc + 1) * 512], pv[:, 0:512],
               hstg[:, sl, oc * 512:(oc + 1) * 512], OP.add,
               r=("pv", f"hstg{sl}"), w=(f"hstg{sl}",))
        p.op("dve", lambda e, s=hstg[:, sl, 0:512], d=bnl[:, 0:6]: e.bn_stats(d, s),
             r=(f"hstg{sl}",), w=("l.bn",))
        p.op("dve", lambda e, s=hstg[:, sl, 512:1024], d=bnl[:, 6:12]: e.bn_stats(d, s),
             r=(f"hstg{sl}",), w=("l.bn",))
        p.op("dve", lambda e, s=bnl[:, 0:12], d=mvl[:, 0:2]: e.bn_aggr(d, s),
             r=("l.bn",), w=("l.mv",))
        act(p, sdl[:, 0:1], mvl[:, 1:2], AF.Sqrt, bias=f32(1e-5),
            r=("l.mv",), w=("l.sd",))
        p.op("dve", lambda e, d=rstdl[:, 0:1], s=sdl[:, 0:1]: e.reciprocal(d, s),
             r=("l.sd",), w=("l.rstd",))
        stt(p, ngml[:, 0:1], mvl[:, 0:1], f32(-1.0), rstdl[:, 0:1],
            OP.mult, OP.mult, r=("l.mv", "l.rstd"), w=("l.ngm",))
        act(p, hstg[:, sl, :], hstg[:, sl, :], AF.Identity,
            bias=ngml[:, 0:1], scale=rstdl[:, 0:1],
            r=(f"hstg{sl}", "l.rstd", "l.ngm"), w=(f"hstg{sl}",))
        tt(p, hstg[:, sl, :], hstg[:, sl, :], lngb[:, 0:D], OP.mult,
           r=(f"hstg{sl}", "lng"), w=(f"hstg{sl}",))
        tt(p, hstg[:, sl, :], hstg[:, sl, :], lnbb[:, 0:D], OP.add,
           r=(f"hstg{sl}", "lnb"), w=(f"hstg{sl}",))
        st(p, "act", oview[t_], hstg[:, sl, :], r=(f"hstg{sl}",), w=("at_o",))
    return p


def stage_lf1(hT_bsz_bf16, h_bsz_f32, L2a):
    return {
        "hT_i": hT_bsz_bf16, "h_i": h_bsz_f32,
        "wqkvT": np.ascontiguousarray(L2a["W_qkvb"].T, np.float32),
        "woT": np.ascontiguousarray(L2a["W_o"].T, np.float32),
        "lng_bc": np.ascontiguousarray(np.broadcast_to(L2a["ln_g"], (128, D)), np.float32),
        "lnb_bc": np.ascontiguousarray(np.broadcast_to(L2a["ln_b"], (128, D)), np.float32),
        "identb": np.eye(128, dtype=ml_dtypes.bfloat16),
    }



HID2 = 2730
NJ2 = 22


def _cj2(j):
    return 128 if j < 21 else 42


def build_lf2():
    p = Prog("lf2")
    xnT_i = p.inp("xnT_i", (NCH, 128, T + 1), BF16)
    xn_i = p.inp("xn_i", (T, D))
    wqkv_i = [p.inp(f"wqkv{l}", (D, 1536)) for l in range(2)]
    wo_i = [p.inp(f"wo{l}", (512, D)) for l in range(2)]
    w1_i = [p.inp(f"w1{l}", (D, 2 * HID2)) for l in range(2)]
    w2_i = [p.inp(f"w2{l}", (HID2, D)) for l in range(2)]
    b1a_s = [p.inp(f"b1a{l}", (128, NJ2)) for l in range(2)]
    b1g_s = [p.inp(f"b1g{l}", (128, NJ2)) for l in range(2)]
    b2_bc = [p.inp(f"b2bc{l}", (128, D)) for l in range(2)]
    lg_s = [p.inp(f"lg{l}", (128, 8)) for l in range(2)]
    lb_s = [p.inp(f"lb{l}", (128, 8)) for l in range(2)]
    qs_bc = [p.inp(f"qsbc{l}", (128, 64)) for l in range(2)]
    ks_bc = [p.inp(f"ksbc{l}", (128, 64)) for l in range(2)]
    csq = p.inp("csq", (128, 16, 64)); snq = p.inp("snq", (128, 16, 64))
    csk = p.inp("csk", (128, 16, 64)); snk = p.inp("snk", (128, 16, 64))
    identb = p.inp("identb", (128, 128), BF16)
    masktri = p.inp("masktri", (128, 128), F32)
    masklo = p.inp("masklo", (128, 128), F32)
    ff2_o = p.outp("ff2", (T, D), F32)
    x_d = p.dscratch("x_d", (T, D), F32)
    h2_d = p.dscratch("h2_d", (NJ2, 128, T), BF16)

    xTz = p.sb("xTz", [128, NCH, T + 1], BF16)
    big = p.sb("big", [128, 16, 1536], BF16)
    lnhT = big.ap().rearrange("p a b -> p (a b)")[:, 0:NCH * T].rearrange(
        "p (c t) -> p c t", t=T)
    w2b = p.sb("w2b", [128, NJ2, D], BF16)
    yT = p.sb("yT", [128, 4, T], BF16)
    wob = p.sb("wob", [128, 4, D], BF16)
    w1a = p.sb("w1a", [128, 2, 8, 128], BF16)
    w1g = p.sb("w1g", [128, 2, 8, 128], BF16)
    idb = p.sb("ident", [128, 128], BF16)
    mtri = p.sb("mtri", [128, 128], F32)
    mlo = p.sb("mlo", [128, 128], F32)
    csqb = p.sb("csqb", [128, 16, 64], BF16); snqb = p.sb("snqb", [128, 16, 64], BF16)
    cskb = p.sb("cskb", [128, 16, 64], BF16); snkb = p.sb("snkb", [128, 16, 64], BF16)
    qsb = p.sb("qsb", [128, 64], BF16); ksb = p.sb("ksb", [128, 64], BF16)
    lgb = p.sb("lgb", [128, 8], F32); lbb = p.sb("lbb", [128, 8], F32)
    b1ab = p.sb("b1ab", [128, NJ2], F32); b1gb = p.sb("b1gb", [128, NJ2], F32)
    b2b = p.sb("b2b", [128, D], F32)
    nrm = p.sb("nrm", [128, 16, 1], F32)
    ttq = p.sb("ttq", [128, 16, 64], BF16)
    rt1 = p.sb("rt1", [128, 16, 32], BF16)
    rt2 = p.sb("rt2", [128, 16, 32], BF16)
    qh2 = p.sb("qh2", [64, T], BF16)
    kh2 = p.sb("kh2", [64, T], BF16)
    sf = p.sb("sf", [128, 1024], F32)
    abf = p.sb("abf", [128, 1024], BF16)
    at = p.sb("at", [128, 8, 128], BF16)
    mx4 = p.sb("mx4", [128, 2], F32)
    lp4 = p.sb("lp4", [128, 2], F32)
    negm = p.sb("negm", [128, 1], F32)
    rec = p.sb("rec", [128, 1], F32)
    xstg = p.sb("xstg", [128, 2, D], F32)
    xbstg = p.sb("xbstg", [128, D], BF16)
    sqst = p.sb("sqst", [128, 512], F32)
    ast2 = p.sb("ast2", [128, 512], F32)
    gst2 = p.sb("gst2", [128, 512], F32)
    h2j = p.sb("h2j", [128, 512], BF16)
    rows = big.ap().rearrange("p a b -> p (a b)").bitcast(F32)[0:1, 8192:10240]
    ones1 = p.sb("ones1", [1, 128], F32)
    cmeanb = p.sb("cmeanb", [128, 1], BF16)
    cmeanf = p.sb("cmeanf", [128, 1], F32)
    h2slc = yT.ap().rearrange("p a t -> p (a t)")[:, 0:NJ2 * 128].rearrange(
        "p (j q) -> p j q", q=128)
    ps0 = p.ps("ps0", [128, 512], F32)
    ps1 = p.ps("ps1", [128, 512], F32)
    ptb = p.ps("ptb", [128, 1024], BF16)
    pyb = p.ps("pyb", [128, 512], F32)
    pq = p.ps("pq", [128, 512], F32)
    pr = p.ps("pr", [128, 512], F32)
    pn = p.ps("pn", [128, 512], F32)
    pg2 = p.ps("pg2", [128, 512], F32)

    ld(p, "sp", idb[:], identb[:], w=("ident",))
    ld(p, "sp", mtri[:], masktri[:], w=("masktri",))
    ld(p, "sp", mlo[:], masklo[:], w=("masklo",))
    for dst, srcn in ((csqb, "csq"), (snqb, "snq"), (cskb, "csk"), (snkb, "snk")):
        ld(p, "pool", dst[:, :, :], p.t[srcn][:, :, :], w=(srcn + "s",))
    memset(p, ones1[0:1, 0:128], 1.0, w=("ones1",))
    memset(p, cmeanb[:, 0:1], 1.0 / 1024.0, w=("cmb",))
    memset(p, cmeanf[:, 0:1], 1.0 / 1024.0, w=("cmf",))
    memset(p, xTz[:, :, 0:1], 0.0, w=("xTz0",))
    memset(p, sqst[:, 0:512], 0.0, w=("sqst",))
    for tc4 in range(4):
        st(p, "pool", h2_d[NJ2 - 1, 42:128, tc4 * 512:(tc4 + 1) * 512],
           sqst[0:86, 0:512], r=("sqst",), w=("h2_d",))
    ld(p, "pool", xTz[:, :, 1:T + 1], xnT_i.rearrange("c p t -> p c t")[:, :, 1:T + 1],
       w=("xTzl",))
    st(p, "sp", x_d[:, :], xn_i[:, :], r=(), w=("x_d",))

    XK = ("xTz0", "xTzl")

    def shifted(c, t0, width=128):
        off = 0 if c < 4 else -1
        return xTz[:, c, 1 + t0 + off: 1 + t0 + off + width]

    for l in range(2):
        wqkvb = w2b.ap().rearrange("p j o -> p (j o)")[:, 0:8 * 1536].rearrange("p (j o) -> p j o", o=1536)
        ld(p, "pool", wqkvb, wqkv_i[l].rearrange("(c p) o -> p c o", p=128), w=("w2b",))
        ld(p, "pool", wob[:, :, :], wo_i[l].rearrange("(c p) o -> p c o", p=128), w=("wo",))
        ld(p, "pool", qsb[:, :], qs_bc[l][:, :], w=("qs",))
        ld(p, "pool", ksb[:, :], ks_bc[l][:, :], w=("ks",))
        ld(p, "sp", lgb[:, :], lg_s[l][:, :], w=("lg",))
        ld(p, "sp", lbb[:, :], lb_s[l][:, :], w=("lb",))
        ld(p, "sp", b1ab[:, :], b1a_s[l][:, :], w=("b1a",))
        ld(p, "sp", b1gb[:, :], b1g_s[l][:, :], w=("b1g",))
        ld(p, "sp", b2b[:, :], b2_bc[l][:, :], w=("b2",))
        for t_ in range(NT):
            for oc in range(3):
                pb = (pq, pg2)[(t_ * 3 + oc) % 2]
                pbk = ("pq", "pg2")[(t_ * 3 + oc) % 2]
                mm(p, [(pb[:, 0:512], shifted(c, t_ * 128),
                        wqkvb[:, c, oc * 512:(oc + 1) * 512], c == 0, c == 7)
                       for c in range(NCH)],
                   r=XK + ("w2b",), w=(pbk,), tag=f"qkv{l}.{t_}.{oc}")
                cp(p, "dve", big[:, t_, oc * 512:(oc + 1) * 512], pb[:, 0:512],
                   r=(pbk,), w=("big", pbk))
        for h in range(8):
            qsl = big[:, :, h * 64:(h + 1) * 64]
            ksl = big[:, :, 512 + h * 64:512 + (h + 1) * 64]
            for (sl, key, scb, csb, snb) in ((qsl, "q", qsb, csqb, snqb),
                                             (ksl, "k", ksb, cskb, snkb)):
                tt(p, ttq[:, :, :], sl, sl, OP.mult, r=("big",), w=("ttq",))
                red(p, nrm[:, :, :], ttq[:, :, :], OP.add, r=("ttq",), w=("nrm",))
                act(p, nrm[:, :, :], nrm[:, :, :], AF.Sqrt, r=("nrm",), w=("nrm",))
                ts(p, nrm[:, :, :], nrm[:, :, :], f32(1e-12), OP.max, r=("nrm",), w=("nrm",))
                p.op("dve", lambda e, d=nrm[:, :, :], s=nrm[:, :, :]: e.reciprocal(d, s),
                     r=("nrm",), w=("nrm",))
                tt(p, sl, sl, nrm[:, :, :].broadcast_to((128, 16, 64)), OP.mult,
                   r=("big", "nrm"), w=("big",))
                tt(p, sl, sl, scb[:, :].rearrange("p (a d) -> p a d", a=1)
                   .broadcast_to((128, 16, 64)), OP.mult,
                   r=("big", "qs", "ks"), w=("big",))
                x1 = sl[:, :, 0:32]; x2 = sl[:, :, 32:64]
                tt(p, rt1[:, :, :], x1, csb[:, :, 0:32], OP.mult,
                   r=("big", "csqs", "snqs", "csks", "snks"), w=("rt1",))
                tt(p, rt2[:, :, :], x2, snb[:, :, 0:32], OP.mult,
                   r=("big", "csqs", "snqs", "csks", "snks"), w=("rt2",))
                tt(p, rt1[:, :, :], rt1[:, :, :], rt2[:, :, :], OP.subtract,
                   r=("rt1", "rt2"), w=("rt1",))
                tt(p, rt2[:, :, :], x2, csb[:, :, 32:64], OP.mult,
                   r=("big",), w=("rt2",))
                tt(p, ttq[:, :, 0:32], x1, snb[:, :, 32:64], OP.mult,
                   r=("big",), w=("ttq",))
                tt(p, rt2[:, :, :], rt2[:, :, :], ttq[:, :, 0:32], OP.add,
                   r=("rt2", "ttq"), w=("rt2",))
                cp(p, "dve", x1, rt1[:, :, :], r=("rt1",), w=("big",))
                cp(p, "dve", x2, rt2[:, :, :], r=("rt2",), w=("big",))
            for g in range(2):
                trans(p, [(ptb[0:64, j * 128:(j + 1) * 128], qsl[:, g * 8 + j, :])
                          for j in range(8)], idb, r=("big",), w=("ptb",))
                ts(p, qh2[0:64, g * 1024:(g + 1) * 1024], ptb[0:64, 0:1024],
                   f32(8.0), OP.mult, r=("ptb",), w=("qh", "ptb"))
                trans(p, [(ptb[0:64, j * 128:(j + 1) * 128], ksl[:, g * 8 + j, :])
                          for j in range(8)], idb, r=("big", "qh"), w=("ptb",))
                cp(p, "dve", kh2[0:64, g * 1024:(g + 1) * 1024], ptb[0:64, 0:1024],
                   r=("ptb",), w=("kh", "ptb"))
            for wi in range(4):
                for qt in range(4):
                    t_ = wi * 4 + qt
                    ktiles = []
                    if wi > 0:
                        ktiles += [((wi - 1) * 4 + ck, "lo" if ck == qt else None)
                                   for ck in range(qt, 4)]
                    ktiles += [(wi * 4 + ck, "tri" if ck == qt else None)
                               for ck in range(qt + 1)]
                    nk = len(ktiles)
                    for ki, (kt, mk) in enumerate(ktiles):
                        b = ki % 2
                        psb = (ps0, ps1)[b]
                        pk = f"ps{b}"
                        mm(p, [(psb[0:128, 0:128], qh2[0:64, t_ * 128:(t_ + 1) * 128],
                                kh2[0:64, kt * 128:(kt + 1) * 128], True, True)],
                           r=("qh", "kh"), w=(pk,), tag=f"w{l}s{h}.{t_}.{ki}")
                        if mk == "tri":
                            tt(p, psb[0:128, 0:128], psb[0:128, 0:128], mtri[:, :],
                               OP.add, r=(pk, "masktri"), w=(pk,))
                        elif mk == "lo":
                            tt(p, psb[0:128, 0:128], psb[0:128, 0:128], mlo[:, :],
                               OP.add, r=(pk, "masklo"), w=(pk,))
                        red(p, mx4[:, ki % 2:ki % 2 + 1] if False else mx4[:, 0:1] if ki == 0 else mx4[:, 1:2],
                            psb[0:128, 0:128], OP.max, r=(pk,), w=("mx4", pk))
                        if ki > 0:
                            red(p, mx4[:, 0:1], mx4[:, 0:2], OP.max, r=("mx4",), w=("mx4",))
                        cp(p, "dve", sf[:, ki * 128:(ki + 1) * 128], psb[0:128, 0:128],
                           r=(pk,), w=("sf",))
                    ts(p, negm[:, 0:1], mx4[:, 0:1], f32(-1.0), OP.mult,
                       r=("mx4",), w=("negm",))
                    act(p, abf[:, 0:nk * 128], sf[:, 0:nk * 128], AF.Exp,
                        bias=negm[:, 0:1], accum=lp4[:, 0:1],
                        r=("sf", "negm"), w=("abf", "lp4"))
                    p.op("dve", lambda e, d=rec[:, 0:1], s=lp4[:, 0:1]: e.reciprocal(d, s),
                         r=("lp4",), w=("rec",))
                    ts(p, abf[:, 0:nk * 128], abf[:, 0:nk * 128], rec[:, 0:1], OP.mult,
                       r=("abf", "rec"), w=("abf",))
                    trans(p, [(ptb[:, ki * 128:(ki + 1) * 128],
                               abf[:, ki * 128:(ki + 1) * 128]) for ki in range(nk)],
                          idb, r=("abf",), w=("ptb",))
                    cp(p, "dve", at[:, 0:nk, :],
                       ptb[:, 0:nk * 128].rearrange("p (c q) -> p c q", q=128),
                       r=("ptb",), w=("at", "ptb"))
                    mm(p, [(pyb[0:64, 0:128], big[:, kt, 1024 + h * 64:1024 + (h + 1) * 64],
                            at[:, ki, :], ki == 0, ki == nk - 1)
                           for ki, (kt, mk) in enumerate(ktiles)],
                       r=("big", "at"), w=("pyb",), tag=f"av{l}.{h}.{t_}")
                    cp(p, "dve", yT[h % 2 * 64:h % 2 * 64 + 64, h // 2, t_ * 128:(t_ + 1) * 128],
                       pyb[0:64, 0:128], r=("pyb",), w=("yT", "pyb"))
        xv = x_d.rearrange("(tt q) c -> tt q c", q=128)
        for t_ in range(NT):
            sl2 = t_ % 2
            ld(p, "sp", xstg[:, sl2, :], xv[t_], w=(f"xstg{sl2}",), r=("x_d",))
            for oc in range(2):
                mm(p, [(pq[:, 0:512], yT[:, ic, t_ * 128:(t_ + 1) * 128],
                        wob[:, ic, oc * 512:(oc + 1) * 512], ic == 0, ic == 3)
                       for ic in range(4)],
                   r=("yT", "wo"), w=("pq",), tag=f"wo{l}.{t_}.{oc}")
                tt(p, xstg[:, sl2, oc * 512:(oc + 1) * 512], pq[:, 0:512],
                   xstg[:, sl2, oc * 512:(oc + 1) * 512], OP.add,
                   r=("pq", f"xstg{sl2}"), w=(f"xstg{sl2}",))
            st(p, "act", xv[t_], xstg[:, sl2, :], r=(f"xstg{sl2}",), w=("x_d",))
        for t_ in range(NT):
            sl2 = t_ % 2
            ld(p, "sp", xstg[:, sl2, :], xv[t_], w=(f"xstg{sl2}",), r=("x_d",))
            cp(p, "dve", xbstg[:, :], xstg[:, sl2, :], r=(f"xstg{sl2}",), w=("xb",))
            trans(p, [(ptb[:, c * 128:(c + 1) * 128], xbstg[:, c * 128:(c + 1) * 128])
                      for c in range(NCH)], idb, r=("xb",), w=("ptb",))
            cp(p, "dve", xTz[:, :, 1 + t_ * 128:1 + (t_ + 1) * 128],
               ptb[:, 0:1024].rearrange("p (c q) -> p c q", q=128),
               r=("ptb",), w=("xTzl", "ptb"))
        for tc in range(4):
            t0 = tc * 512
            mm(p, [(pr[0:1, 0:512], cmeanb[:, 0:1], shifted(c, t0, 512), c == 0, c == 7)
                   for c in range(NCH)], r=XK + ("cmb",), w=("pr",), tag=f"mu{l}.{tc}")
            cp(p, "dve", rows[:, 0:512], pr[0:1, 0:512], r=("pr",), w=("rows", "big", "pr"))
            for c in range(NCH):
                act(p, sqst[:, 0:512], shifted(c, t0, 512), AF.Square,
                    r=XK, w=("sqst",))
                mm(p, [(pr[0:1, 0:512], cmeanf[:, 0:1], sqst[:, 0:512], c == 0, c == 7)],
                   r=("sqst", "cmf"), w=("pr",), tag=f"msq{l}.{tc}.{c}")
            cp(p, "dve", rows[:, 512:1024], pr[0:1, 0:512],
               r=("pr",), w=("rows", "big", "pr"))
            tt(p, rows[:, 1024:1536], rows[:, 0:512], rows[:, 0:512], OP.mult,
               r=("rows", "big"), w=("rows", "big"))
            tt(p, rows[:, 1024:1536], rows[:, 512:1024], rows[:, 1024:1536],
               OP.subtract, r=("rows", "big"), w=("rows", "big"))
            act(p, rows[:, 1024:1536], rows[:, 1024:1536], AF.Sqrt, bias=f32(1e-5),
                r=("rows", "big"), w=("rows", "big"))
            p.op("dve", lambda e, d=rows[:, 1024:1536], s=rows[:, 1024:1536]:
                 e.reciprocal(d, s), r=("rows", "big"), w=("rows", "big"))
            stt(p, rows[:, 1536:2048], rows[:, 0:512], f32(-1.0),
                rows[:, 1024:1536], OP.mult, OP.mult, r=("rows", "big"), w=("rows", "big"))
            mm(p, [(pr[0:128, 0:512], ones1[0:1, 0:128], rows[:, 1024:1536],
                    True, True)], r=("rows", "ones1"), w=("pr",), tag=f"brs{l}.{tc}")
            mm(p, [(pn[0:128, 0:512], ones1[0:1, 0:128], rows[:, 1536:2048],
                    True, True)], r=("rows", "ones1"), w=("pn",), tag=f"bng{l}.{tc}")
            for c in range(NCH):
                tt(p, sqst[:, 0:512], shifted(c, t0, 512), pr[0:128, 0:512], OP.mult,
                   r=XK + ("pr",), w=("sqst", "pr"))
                tt(p, sqst[:, 0:512], sqst[:, 0:512], pn[0:128, 0:512], OP.add,
                   r=("sqst", "pn"), w=("sqst", "pn"))
                act(p, lnhT[:, c, t0:t0 + 512], sqst[:, 0:512], AF.Identity,
                    scale=lgb[:, c:c + 1], bias=lbb[:, c:c + 1],
                    r=("sqst", "lg", "lb"), w=("big",))
        for j in range(NJ2):
            cj = _cj2(j)
            sj = j % 2
            ld(p, "pool", w1a[:, sj, :, 0:cj],
               w1_i[l].rearrange("(c p) o -> p c o", p=128)[:, :, j * 128:j * 128 + cj],
               w=(f"w1a{sj}",))
            ld(p, "pool", w1g[:, sj, :, 0:cj],
               w1_i[l].rearrange("(c p) o -> p c o", p=128)[:, :, HID2 + j * 128:HID2 + j * 128 + cj],
               w=(f"w1g{sj}",))
            for tc in range(4):
                t0 = tc * 512
                mm(p, [(ps0[0:cj, 0:512], w1a[:, sj, c, 0:cj], lnhT[:, c, t0:t0 + 512],
                        c == 0, c == 7) for c in range(NCH)],
                   r=(f"w1a{sj}", "big"), w=("ps0",), tag=f"a{l}.{j}.{tc}")
                mm(p, [(ps1[0:cj, 0:512], w1g[:, sj, c, 0:cj], lnhT[:, c, t0:t0 + 512],
                        c == 0, c == 7) for c in range(NCH)],
                   r=(f"w1g{sj}", "big"), w=("ps1",), tag=f"g{l}.{j}.{tc}")
                act(p, ast2[0:cj, 0:512], ps0[0:cj, 0:512], AF.Identity,
                    bias=b1ab[0:cj, j:j + 1],
                    r=("ps0", "b1a"), w=("ast2", "ps0"))
                act(p, gst2[0:cj, 0:512], ps1[0:cj, 0:512], AF.Gelu,
                    bias=b1gb[0:cj, j:j + 1], r=("ps1", "b1g"), w=("gst2", "ps1"))
                tt(p, h2j[0:cj, 0:512], ast2[0:cj, 0:512], gst2[0:cj, 0:512], OP.mult,
                   r=("ast2", "gst2"), w=("h2j",))
                st(p, "act", h2_d[j, 0:cj, t0:t0 + 512], h2j[0:cj, 0:512],
                   r=("h2j",), w=("h2_d",))
        ld(p, "pool", w2b[:, 0:21, :],
           w2_i[l][0:2688, :].rearrange("(j p) o -> p j o", p=128), w=("w2b",))
        ld(p, "pool", w2b[0:42, 21, :], w2_i[l][2688:2730, :], w=("w2b",))
        hv2 = h2_d.rearrange("j p t -> p j t")
        for t_ in range(NT):
            sl2 = t_ % 2
            ld(p, "sp", h2slc, hv2[:, :, t_ * 128:(t_ + 1) * 128],
               w=("yT",), r=("h2_d",))
            ld(p, "sp", xstg[:, sl2, :], xv[t_], w=(f"xstg{sl2}",), r=("x_d",))
            for oc in range(2):
                mm(p, [(pg2[:, 0:512], h2slc[0:_cj2(j), j, :],
                        w2b[0:_cj2(j), j, oc * 512:(oc + 1) * 512],
                        j == 0, j == NJ2 - 1) for j in range(NJ2)],
                   r=("yT", "w2b"), w=("pg2",), tag=f"dn{l}.{t_}.{oc}")
                tt(p, xstg[:, sl2, oc * 512:(oc + 1) * 512], pg2[:, 0:512],
                   xstg[:, sl2, oc * 512:(oc + 1) * 512], OP.add,
                   r=("pg2", f"xstg{sl2}"), w=(f"xstg{sl2}",))
            tt(p, xstg[:, sl2, :], xstg[:, sl2, :], b2b[:, :] if False else b2b[:, 0:D],
               OP.add, r=(f"xstg{sl2}", "b2"), w=(f"xstg{sl2}",))
            if l == 0:
                st(p, "act", xv[t_], xstg[:, sl2, :], r=(f"xstg{sl2}",), w=("x_d",))
            else:
                st(p, "act", ff2_o.rearrange("(tt q) c -> tt q c", q=128)[t_],
                   xstg[:, sl2, :], r=(f"xstg{sl2}",), w=("ff2_o",))
        if l == 0:
            for t_ in range(NT):
                sl2 = t_ % 2
                ld(p, "sp", xstg[:, sl2, :], xv[t_], w=(f"xstg{sl2}",), r=("x_d",))
                cp(p, "dve", xbstg[:, :], xstg[:, sl2, :], r=(f"xstg{sl2}",), w=("xb",))
                trans(p, [(ptb[:, c * 128:(c + 1) * 128], xbstg[:, c * 128:(c + 1) * 128])
                          for c in range(NCH)], idb, r=("xb",), w=("ptb",))
                cp(p, "dve", xTz[:, :, 1 + t_ * 128:1 + (t_ + 1) * 128],
                   ptb[:, 0:1024].rearrange("p (c q) -> p c q", q=128),
                   r=("ptb",), w=("xTzl", "ptb"))
    return p


def _rotary_tables():
    DIM_HEAD, WINDOW = 64, 512
    n = 2048
    half = np.arange(0, DIM_HEAD, 2, dtype=np.float64)
    inv = 1.0 / 10000.0 ** (half / DIM_HEAD)
    t = np.arange(n, dtype=np.float64)
    fr = t[:, None] * inv[None, :]
    fr = np.concatenate([fr, fr], -1)
    base = (half + 0.4 * DIM_HEAD) / (1.4 * DIM_HEAD)
    power = (t - n // 2) / float(WINDOW // 2)
    sc = base[None, :] ** power[:, None]
    sc = np.concatenate([sc, sc], -1)
    cos, sin = np.cos(fr), np.sin(fr)
    return (cos * sc, sin * sc, cos / sc, sin / sc)


def _lo_mask():
    m = np.zeros((128, 128), np.float32)
    m[np.tril_indices(128, -1)] = -1e30
    return m


def stage_lf2(xn_row, L2f):
    csq, snq, csk, snk = _rotary_tables()
    til = lambda a: np.ascontiguousarray(_tile_tok(a.astype(np.float32)))
    xnT = np.zeros((NCH, 128, T + 1), ml_dtypes.bfloat16)
    xnT[:, :, 1:] = xn_row.T.reshape(NCH, 128, T).astype(ml_dtypes.bfloat16)
    d = {
        "xnT_i": xnT, "xn_i": np.ascontiguousarray(xn_row, np.float32),
        "csq": til(csq), "snq": til(snq), "csk": til(csk), "snk": til(snk),
        "identb": np.eye(128, dtype=ml_dtypes.bfloat16),
        "masktri": _tri_mask(), "masklo": _lo_mask(),
    }
    for l in range(2):
        d[f"wqkv{l}"] = np.ascontiguousarray(L2f["wqkv"][l], np.float32)
        d[f"wo{l}"] = np.ascontiguousarray(L2f["wo"][l], np.float32)
        d[f"w1{l}"] = np.ascontiguousarray(L2f["w1"][l], np.float32)
        d[f"w2{l}"] = np.ascontiguousarray(L2f["w2"][l], np.float32)
        b1 = L2f["b1"][l]
        b1a = np.zeros((128, NJ2), np.float32)
        b1g = np.zeros((128, NJ2), np.float32)
        for j in range(NJ2):
            cj = _cj2(j)
            b1a[0:cj, j] = b1[j * 128:j * 128 + cj]
            b1g[0:cj, j] = b1[HID2 + j * 128:HID2 + j * 128 + cj]
        d[f"b1a{l}"] = b1a; d[f"b1g{l}"] = b1g
        d[f"b2bc{l}"] = np.ascontiguousarray(np.broadcast_to(L2f["b2"][l], (128, D)), np.float32)
        d[f"lg{l}"] = _bcol(L2f["ln_g"][l]); d[f"lb{l}"] = _bcol(L2f["ln_b"][l])
        d[f"qsbc{l}"] = np.ascontiguousarray(np.broadcast_to(L2f["q_scale"][l], (128, 64)), np.float32)
        d[f"ksbc{l}"] = np.ascontiguousarray(np.broadcast_to(L2f["k_scale"][l], (128, 64)), np.float32)
    return d



def build_lg():
    p = Prog("lg")
    x2_i = p.inp("x2", (T, D)); at2_i = p.inp("at2", (T, D)); ff2_i = p.inp("ff2", (T, D))
    wqT = p.inp("wqT", (D, D))
    wkT = [p.inp(f"wkT{k}", (D, D)) for k in range(2)]
    wvT = [p.inp(f"wvT{k}", (D, D)) for k in range(2)]
    woT = p.inp("woT", (D, D))
    bq_bc = p.inp("bq_bc", (128, D))
    bk_bc = [p.inp(f"bk_bc{k}", (128, D)) for k in range(2)]
    bv_bc = [p.inp(f"bv_bc{k}", (128, D)) for k in range(2)]
    bo_bc = p.inp("bo_bc", (128, D))
    lng_bc = p.inp("lng_bc", (128, D)); lnb_bc = p.inp("lnb_bc", (128, D))
    wrout = p.inp("wrout", (D, 32))
    iotaE = p.inp("iotaE", (128, 1))
    identb = p.inp("identb", (128, 128), BF16)
    identf = p.inp("identf", (128, 128), F32)

    x3_o = p.outp("x3", (T, D), F32)
    xn3_o = p.outp("xn3", (T, D), F32)
    at3_o = p.outp("at3", (T, D), F32)
    cnt_o = p.outp("cnt", (1, 32), F32)
    top1_o = p.outp("top1", (128, 16), F32)
    pos_o = p.outp("pos", (1, 2048), F32)
    gate_o = p.outp("gate", (128, 16), F32)
    r_d = p.dscratch("r_d", (16, 128), F32)
    lnT_d = p.dscratch("lnT_d", (NT, 128, 1024), BF16)

    alloc_ln(p, "l.", ptkey="pq")
    xnT = p.sb("xnT", [128, NCH, T], BF16)
    xn3b = p.sb("xn3b", [128, NT, D], BF16)
    kv0 = p.sb("kv0", [128, NT, D], BF16)
    _kv0b = kv0.ap().rearrange("p a b -> p (a b)")
    _kv0f = _kv0b.bitcast(F32)
    kv1 = p.sb("kv1", [128, NT, D], BF16)
    _kv1f = kv1.ap().rearrange("p a b -> p (a b)").bitcast(F32)
    ohT = _kv1f[0:32, 0:2048]
    cs = _kv1f[0:32, 2048:4096]
    onr = _kv1f[0:1, 4096:6144]
    posr = _kv1f[0:1, 6144:8192]
    wb = p.sb("wb", [128, 8, 1024], BF16)
    bqb = p.sb("bqb", [128, D], F32)
    bxb = p.sb("bxb", [128, D], F32)
    lngb = p.sb("lngb", [128, D], F32); lnbb = p.sb("lnbb", [128, D], F32)
    wrb = p.sb("wrb", [128, 8, 32], BF16)
    iotE = p.sb("iotE", [128, 1], F32)
    ones1g = p.sb("ones1g", [1, 32], F32)
    ones32 = p.sb("ones32", [32, 1], F32)
    idb = p.sb("ident", [128, 128], BF16)
    idf = p.sb("identf_s", [128, 128], F32)
    nrm = p.sb("nrm", [128, 16, 1], F32)
    ttq = p.sb("ttq", [128, 16, 64], F32)
    stk = p.sb("stk", [128, 2, 8, 64], F32)
    stbd = p.sb("stbd", [128, 8, 128], BF16)
    otile = p.sb("otile", [128, 1024], F32)
    lnob = _kv0b[:, 8192:9216]
    lnoT = _kv0b[:, 9216:10240].rearrange("p (c q) -> p c q", q=128)
    stg3 = _kv0f[:, 0:2048].rearrange("p (s c) -> p s c", c=1024)
    mx8 = p.sb("mx8", [128, 8], F32)
    idxu = p.sb("idxu", [128, 16], U32)
    idx8 = p.sb("idx8", [128, 8], U32)
    t1c = p.sb("t1c", [128, 16], F32)
    gatec = p.sb("gatec", [128, 16], F32)
    negm2 = p.sb("negm2", [128, 1], F32)
    lsum = p.sb("lsum", [128, 1], F32)
    oh = p.sb("oh", [128, 32], F32)
    cnts = p.sb("cnts", [32, 1], F32)
    kv1f = None
    pq = p.ps("pq", [128, 512], F32)
    pv = p.ps("pv", [128, 512], F32)
    pst = p.ps("pst", [128, 512], F32)
    ptb = p.ps("ptb", [128, 1024], BF16)
    ptf = p.ps("ptf", [128, 512], F32)

    ld(p, "sp", idb[:], identb[:], w=("ident",))
    ld(p, "sp", idf[:], identf[:], w=("identf",))
    ld(p, "sp", iotE[:], iotaE[:], w=("iotE",))
    memset(p, ones1g[0:1, 0:32], 1.0, w=("ones1g",))
    memset(p, ones32[0:32, 0:1], 1.0, w=("ones32",))
    ld(p, "sp", lngb[:], lng_bc[:], w=("lng",))
    ld(p, "sp", lnbb[:], lnb_bc[:], w=("lnb",))
    ld(p, "pool", wrb[:, :, :], wrout.rearrange("(c p) o -> p c o", p=128), w=("wr",))

    x2v = x2_i.rearrange("(tt q) c -> tt q c", q=128)
    a2v = at2_i.rearrange("(tt q) c -> tt q c", q=128)
    f2v = ff2_i.rearrange("(tt q) c -> tt q c", q=128)
    x3v = x3_o.rearrange("(tt q) c -> tt q c", q=128)
    xn3v = xn3_o.rearrange("(tt q) c -> tt q c", q=128)
    at3v = at3_o.rearrange("(tt q) c -> tt q c", q=128)
    bnl = p.t["l.bn"]; mvl = p.t["l.mv"]; sdl = p.t["l.sd"]
    rstdl = p.t["l.rstd"]; ngml = p.t["l.ngm"]; xnfl = p.t["l.xnf"]
    for t_ in range(NT):
        sl = t_ % 2
        ld(p, "sp", stg3[:, sl, :], x2v[t_], w=(f"stg3{sl}", "kv0"))
        ld(p, "sp", xnfl[:, sl, :], a2v[t_], w=(f"l.xnf{sl}",))
        tt(p, stg3[:, sl, :], stg3[:, sl, :], xnfl[:, sl, :], OP.add,
           r=(f"stg3{sl}", f"l.xnf{sl}", "kv0"), w=(f"stg3{sl}", "kv0"))
        ld(p, "sp", xnfl[:, sl, :], f2v[t_], w=(f"l.xnf{sl}",))
        tt(p, stg3[:, sl, :], stg3[:, sl, :], xnfl[:, sl, :], OP.add,
           r=(f"stg3{sl}", f"l.xnf{sl}", "kv0"), w=(f"stg3{sl}", "kv0"))
        st(p, "act", x3v[t_], stg3[:, sl, :], r=(f"stg3{sl}", "kv0"), w=("x3_o",))
        p.op("dve", lambda e, s=stg3[:, sl, 0:512], d=bnl[:, 0:6]: e.bn_stats(d, s),
             r=(f"stg3{sl}", "kv0"), w=("l.bn",))
        p.op("dve", lambda e, s=stg3[:, sl, 512:1024], d=bnl[:, 6:12]: e.bn_stats(d, s),
             r=(f"stg3{sl}", "kv0"), w=("l.bn",))
        p.op("dve", lambda e, s=bnl[:, 0:12], d=mvl[:, 0:2]: e.bn_aggr(d, s),
             r=("l.bn",), w=("l.mv",))
        act(p, sdl[:, 0:1], mvl[:, 1:2], AF.Sqrt, bias=f32(1e-5),
            r=("l.mv",), w=("l.sd",))
        p.op("dve", lambda e, d=rstdl[:, 0:1], s=sdl[:, 0:1]: e.reciprocal(d, s),
             r=("l.sd",), w=("l.rstd",))
        stt(p, ngml[:, 0:1], mvl[:, 0:1], f32(-1.0), rstdl[:, 0:1],
            OP.mult, OP.mult, r=("l.mv", "l.rstd"), w=("l.ngm",))
        act(p, xnfl[:, sl, :], stg3[:, sl, :], AF.Identity,
            bias=ngml[:, 0:1], scale=rstdl[:, 0:1],
            r=(f"stg3{sl}", "l.rstd", "l.ngm", "kv0"), w=(f"l.xnf{sl}",))
        st(p, "act", xn3v[t_], xnfl[:, sl, :], r=(f"l.xnf{sl}",), w=("xn3_o",))
        cp(p, "dve", xn3b[:, t_, :], xnfl[:, sl, :], r=(f"l.xnf{sl}",), w=("xn3b",))
        for g in range(2):
            trans(p, [(ptf[:, j * 128:(j + 1) * 128],
                       xnfl[:, sl, (g * 4 + j) * 128:(g * 4 + j + 1) * 128])
                      for j in range(4)], idf,
                  r=(f"l.xnf{sl}", "identf"), w=("ptf2",))
            cp(p, "dve", xnT[:, g * 4:(g + 1) * 4, t_ * 128:(t_ + 1) * 128],
               ptf[:, 0:512].rearrange("p (c q) -> p c q", q=128),
               r=("ptf2",), w=(f"xnT{t_}",))

    memset(p, stbd[:, :, :], 0.0, w=("stbd",))
    SC = 1.0 / np.sqrt(2048.0)
    for k2 in range(2):
        stk_k = stk
        for (dstb, wsrc, bsrc, feat) in ((kv0, wkT[k2], f"bk_bc{k2}", True),
                                         (kv1, wvT[k2], f"bv_bc{k2}", False)):
            key = "kv0" if feat else "kv1"
            ld(p, "pool", wb[:, :, :], wsrc.rearrange("(c p) o -> p c o", p=128), w=("wb",))
            ld(p, "sp", bxb[:, :], p.t[bsrc][:, :], w=("bx",))
            for t_ in range(NT):
                for oc in range(2):
                    mm(p, [(pv[:, 0:512], xnT[:, c, t_ * 128:(t_ + 1) * 128],
                            wb[:, c, oc * 512:(oc + 1) * 512], c == 0, c == 7)
                           for c in range(NCH)],
                       r=XNT_KEYS + ("wb",), w=("pv",), tag=f"kv{k2}.{t_}.{oc}")
                    tt(p, otile[:, oc * 512:(oc + 1) * 512], pv[:, 0:512],
                       xn3b[:, t_, oc * 512:(oc + 1) * 512], OP.add,
                       r=("pv", "xn3b"), w=("ot", "pv"))
                    tt(p, otile[:, oc * 512:(oc + 1) * 512],
                       otile[:, oc * 512:(oc + 1) * 512],
                       bxb[:, oc * 512:(oc + 1) * 512], OP.add,
                       r=("ot", "bx"), w=("ot",))
                if feat:
                    act(p, otile[:, :], otile[:, :], AF.Silu, r=("ot",), w=("ot",))
                    ts(p, otile[:, :], otile[:, :], f32(0.5), OP.add, r=("ot",), w=("ot",))
                    ot3 = otile[:].rearrange("p (h d) -> p h d", d=64)
                    tt(p, ttq[:, :, :], ot3, ot3, OP.mult, r=("ot",), w=("ttq",))
                    red(p, nrm[:, :, :], ttq[:, :, :], OP.add, r=("ttq",), w=("nrm",))
                    act(p, nrm[:, :, :], nrm[:, :, :], AF.Sqrt, r=("nrm",), w=("nrm",))
                    ts(p, nrm[:, :, :], nrm[:, :, :], f32(1e-6), OP.add, r=("nrm",), w=("nrm",))
                    p.op("dve", lambda e, d=nrm[:, :, :], s=nrm[:, :, :]:
                         e.reciprocal(d, s), r=("nrm",), w=("nrm",))
                    tt(p, ot3, ot3, nrm[:, :, :].broadcast_to((128, 16, 64)), OP.mult,
                       r=("ot", "nrm"), w=("ot",))
                ts(p, dstb[:, t_, :], otile[:, :], f32(SC), OP.mult, r=("ot",), w=(key,))
        for h in range(16):
            hp = (h % 2) * 64
            mm(p, [(pst[hp:hp + 64, (h // 2) * 64:(h // 2) * 64 + 64],
                    kv0[:, t_, h * 64:(h + 1) * 64], kv1[:, t_, h * 64:(h + 1) * 64],
                    t_ == 0, t_ == NT - 1) for t_ in range(NT)],
               r=("kv0", "kv1"), w=("pst",), tag=f"st{k2}.{h}")
        cp(p, "dve", stk_k[:, k2, :, :],
           pst[:, 0:512].rearrange("p (c q) -> p c q", q=64), r=("pst",), w=("stk",))
    tt(p, stk[:, 0, :, :], stk[:, 0, :, :], stk[:, 1, :, :], OP.mult,
       r=("stk",), w=("stk",))
    for j in range(8):
        cp(p, "dve", stbd[0:64, j, 0:64], stk[0:64, 0, j, :], r=("stk",), w=("stbd",))
        cp(p, "dve", stbd[64:128, j, 64:128], stk[64:128, 0, j, :], r=("stk",), w=("stbd",))

    ld(p, "pool", wb[:, :, :], wqT.rearrange("(c p) o -> p c o", p=128), w=("wb",))
    ld(p, "sp", bqb[:, :], bq_bc[:, :], w=("bq",))
    ld(p, "sp", bxb[:, :], bo_bc[:, :], w=("bx",))
    for t_ in range(NT):
        for oc in range(2):
            mm(p, [(pv[:, 0:512], xnT[:, c, t_ * 128:(t_ + 1) * 128],
                    wb[:, c, oc * 512:(oc + 1) * 512], c == 0, c == 7)
               for c in range(NCH)],
               r=XNT_KEYS + ("wb",), w=("pv",), tag=f"q{t_}.{oc}")
            tt(p, otile[:, oc * 512:(oc + 1) * 512], pv[:, 0:512],
               xn3b[:, t_, oc * 512:(oc + 1) * 512], OP.add,
               r=("pv", "xn3b"), w=("ot", "pv"))
            tt(p, otile[:, oc * 512:(oc + 1) * 512], otile[:, oc * 512:(oc + 1) * 512],
               bqb[:, oc * 512:(oc + 1) * 512], OP.add, r=("ot", "bq"), w=("ot",))
        act(p, otile[:, :], otile[:, :], AF.Silu, r=("ot",), w=("ot",))
        ts(p, otile[:, :], otile[:, :], f32(0.5), OP.add, r=("ot",), w=("ot",))
        ot3 = otile[:].rearrange("p (h d) -> p h d", d=64)
        tt(p, ttq[:, :, :], ot3, ot3, OP.mult, r=("ot",), w=("ttq",))
        red(p, nrm[:, :, :], ttq[:, :, :], OP.add, r=("ttq",), w=("nrm",))
        act(p, nrm[:, :, :], nrm[:, :, :], AF.Sqrt, r=("nrm",), w=("nrm",))
        ts(p, nrm[:, :, :], nrm[:, :, :], f32(1e-6), OP.add, r=("nrm",), w=("nrm",))
        p.op("dve", lambda e, d=nrm[:, :, :], s=nrm[:, :, :]: e.reciprocal(d, s),
             r=("nrm",), w=("nrm",))
        tt(p, ot3, ot3, nrm[:, :, :].broadcast_to((128, 16, 64)), OP.mult,
           r=("ot", "nrm"), w=("ot",))
        cp(p, "dve", lnob[:, :], otile[:, :], r=("ot",), w=("lnob", "kv0"))
        trans(p, [(ptb[:, c * 128:(c + 1) * 128], lnob[:, c * 128:(c + 1) * 128])
                  for c in range(NCH)], idb, r=("lnob", "kv0"), w=("ptb",))
        cp(p, "dve", lnoT[:, :, :], ptb[:, 0:1024].rearrange("p (c q) -> p c q", q=128),
           r=("ptb",), w=("lnoT", "kv0", "ptb"))
        for jg in range(2):
            mm(p, [(pq[:, (j - 4 * jg) * 128:(j - 4 * jg + 1) * 128],
                    lnoT[:, j, :], stbd[:, j, :], True, True)
                   for j in range(4 * jg, 4 * jg + 4)],
               r=("lnoT", "stbd", "kv0"), w=("pq",), tag=f"out{t_}.{jg}")
            cp(p, "dve", otile[:, jg * 512:(jg + 1) * 512], pq[:, 0:512],
               r=("pq",), w=("ot", "pq"))
        p.op("dve", lambda e, s=otile[:, 0:512], d=bnl[:, 0:6]: e.bn_stats(d, s),
             r=("ot",), w=("l.bn",))
        p.op("dve", lambda e, s=otile[:, 512:1024], d=bnl[:, 6:12]: e.bn_stats(d, s),
             r=("ot",), w=("l.bn",))
        p.op("dve", lambda e, s=bnl[:, 0:12], d=mvl[:, 0:2]: e.bn_aggr(d, s),
             r=("l.bn",), w=("l.mv",))
        act(p, sdl[:, 0:1], mvl[:, 1:2], AF.Sqrt, bias=f32(1e-5),
            r=("l.mv",), w=("l.sd",))
        p.op("dve", lambda e, d=rstdl[:, 0:1], s=sdl[:, 0:1]: e.reciprocal(d, s),
             r=("l.sd",), w=("l.rstd",))
        stt(p, ngml[:, 0:1], mvl[:, 0:1], f32(-1.0), rstdl[:, 0:1],
            OP.mult, OP.mult, r=("l.mv", "l.rstd"), w=("l.ngm",))
        act(p, otile[:, :], otile[:, :], AF.Identity,
            bias=ngml[:, 0:1], scale=rstdl[:, 0:1],
            r=("ot", "l.rstd", "l.ngm"), w=("ot",))
        tt(p, otile[:, :], otile[:, :], lngb[:, 0:D], OP.mult, r=("ot", "lng"), w=("ot",))
        tt(p, otile[:, :], otile[:, :], lnbb[:, 0:D], OP.add, r=("ot", "lnb"), w=("ot",))
        cp(p, "dve", lnob[:, :], otile[:, :], r=("ot",), w=("lnob", "kv0"))
        trans(p, [(ptb[:, c * 128:(c + 1) * 128], lnob[:, c * 128:(c + 1) * 128])
                  for c in range(NCH)], idb, r=("lnob", "kv0"), w=("ptb",))
        cp(p, "dve", lnoT[:, :, :], ptb[:, 0:1024].rearrange("p (c q) -> p c q", q=128),
           r=("ptb",), w=("lnoT", "kv0", "ptb"))
        st(p, "act", lnT_d[t_], lnoT[:, :, :].rearrange("p c q -> p (c q)"),
           r=("lnoT", "kv0"), w=("lnT_d",))
        mm(p, [(pq[:, 0:32], xnT[:, c, t_ * 128:(t_ + 1) * 128], wrb[:, c, :],
                c == 0, c == 7) for c in range(NCH)],
           r=XNT_KEYS + ("wr",), w=("pq",), tag=f"rt{t_}")
        cp(p, "dve", oh[:, :], pq[:, 0:32], r=("pq",), w=("oh", "pq"))
        p.op("dve", lambda e, d=mx8[:, 0:8], s=oh[:, :]: e.max(d, s),
             r=("oh",), w=("mx8",))
        p.op("dve", lambda e, d=idx8[:, 0:8], mxx=mx8[:, 0:8], s=oh[:, :]:
             e.max_index(d, mxx, s), r=("mx8", "oh"), w=("idx8",))
        cp(p, "dve", idxu[:, t_:t_ + 1], idx8[:, 0:1], r=("idx8",), w=("idxu",))
        ts(p, negm2[:, 0:1], mx8[:, 0:1], f32(-1.0), OP.mult, r=("mx8",), w=("negm2",))
        act(p, oh[:, :], oh[:, :], AF.Exp, bias=negm2[:, 0:1], accum=lsum[:, 0:1],
            r=("oh", "negm2"), w=("oh", "lsum"))
        p.op("dve", lambda e, d=gatec[:, t_:t_ + 1], s=lsum[:, 0:1]: e.reciprocal(d, s),
             r=("lsum",), w=("gatec",))
        cp(p, "dve", t1c[:, t_:t_ + 1], idxu[:, t_:t_ + 1], r=("idxu",), w=("t1c",))
    ld(p, "pool", wb[:, :, :], woT.rearrange("(c p) o -> p c o", p=128), w=("wb",))
    for t_ in range(NT):
        ld(p, "sp", lnoT[:, :, :], lnT_d.rearrange("t p co -> t p co")[t_].rearrange(
            "p (c q) -> p c q", q=128), w=("lnoT", "kv0"), r=("lnT_d",))
        for oc in range(2):
            mm(p, [(pv[:, 0:512], lnoT[:, ic, :], wb[:, ic, oc * 512:(oc + 1) * 512],
                    ic == 0, ic == 7) for ic in range(NCH)],
               r=("lnoT", "wb", "kv0"), w=("pv",), tag=f"wo3.{t_}.{oc}")
            tt(p, otile[:, oc * 512:(oc + 1) * 512], pv[:, 0:512],
               bxb[:, oc * 512:(oc + 1) * 512], OP.add, r=("pv", "bx"), w=("ot", "pv"))
        st(p, "act", at3v[t_], otile[:, :], r=("ot",), w=("at3_o",))
    st(p, "act", top1_o[:, :], t1c[:, :], r=("t1c",), w=("top1_o",))
    st(p, "act", gate_o[:, :], gatec[:, :], r=("gatec",), w=("gate_o",))
    trans(p, [(ptf[0:16, 0:128], t1c[:, :])], idf, r=("t1c", "identf"), w=("ptf2",))
    cp(p, "dve", ohT[0:16, 0:128], ptf[0:16, 0:128], r=("ptf2",), w=("ohT", "kv1"))
    st(p, "sp", r_d[:, :], ohT[0:16, 0:128], r=("ohT", "kv1"), w=("r_d",))
    onr_view = r_d.rearrange("a b -> (a b)").rearrange("(x n) -> x n", x=1)
    ld(p, "sp", onr[0:1, 0:2048], onr_view, w=("onr", "kv1"), r=("r_d",))
    for tc in range(4):
        mm(p, [(pq[0:32, 0:512], ones1g[0:1, 0:32], onr[0:1, tc * 512:(tc + 1) * 512],
                True, True)], r=("onr", "ones1g", "kv1"), w=("pq",), tag=f"bct{tc}")
        ts(p, ohT[0:32, tc * 512:(tc + 1) * 512], pq[0:32, 0:512], iotE[0:32, 0:1],
           OP.is_equal, r=("pq", "iotE"), w=("ohT", "kv1", "pq"))
    p.op("dve", lambda e, d=cs[0:32, 0:2048], s=ohT[0:32, 0:2048]:
         e.tensor_tensor_scan(d, s, s, 0.0, OP.add, OP.bypass),
         r=("ohT", "kv1"), w=("cs", "kv1"))
    cp(p, "dve", cnts[0:32, 0:1], cs[0:32, 2047:2048], r=("cs", "kv1"), w=("cnts",))
    trans(p, [(ptf[0:1, 0:32], cnts[0:32, 0:1])], idf[0:32, 0:32], r=("cnts", "identf"), w=("ptf2",))
    cp(p, "dve", oh[0:1, 0:32], ptf[0:1, 0:32], r=("ptf2",), w=("oh",))
    st(p, "act", cnt_o[:, :], oh[0:1, 0:32], r=("oh",), w=("cnt_o",))
    tt(p, ohT[0:32, 0:2048], ohT[0:32, 0:2048], cs[0:32, 0:2048], OP.mult,
       r=("ohT", "cs", "kv1"), w=("ohT", "kv1"))
    for tc in range(4):
        mm(p, [(ptf[0:1, 0:512], ones32[0:32, 0:1], ohT[0:32, tc * 512:(tc + 1) * 512],
                True, True)], r=("ohT", "ones32", "kv1"), w=("ptf2",), tag=f"ps{tc}")
        ts(p, posr[0:1, tc * 512:(tc + 1) * 512], ptf[0:1, 0:512], f32(-1.0), OP.add,
           r=("ptf2",), w=("posr", "kv1", "ptf2"))
    st(p, "act", pos_o[:, :], posr[0:1, 0:2048], r=("posr", "kv1"), w=("pos_o",))
    return p



EC = 640
TGLOB = 8 * T


def emit_offsets(p, cnt_ap, offs, sc32):
    ld(p, "sp", sc32[0:32, 0:8], cnt_ap, w=("sc32",))
    p.op("dve", lambda e: e.tensor_tensor_scan(offs[0:32, 0:8], sc32[0:32, 0:8],
         sc32[0:32, 0:8], 0.0, OP.add, OP.bypass), r=("sc32",), w=("offs",))
    cp(p, "dve", sc32[0:32, 1:8], offs[0:32, 0:7], r=("offs",), w=("sc32",))
    memset(p, sc32[0:32, 0:1], 0.0, w=("sc32",))
    cp(p, "dve", offs[0:32, 0:8], sc32[0:32, 0:8], r=("sc32",), w=("offs",))


def build_lh():
    p = Prog("lh")
    xnf_i = p.inp("xnf", (TGLOB + 1, D))
    cnt_i = p.inp("cnts", (32, 8))
    t1_i = p.inp("t1g", (128, 8, 16))
    pos_i = p.inp("posg", (128, 8, 16))
    wg_i = p.inp("wg", (4, D, D)); wu_i = p.inp("wu", (4, D, D)); wd_i = p.inp("wd", (4, D, D))
    cidk = p.inp("cidk", (128, 1))
    c4lo = p.inp("c4lo", (128, 1))
    c4hi = p.inp("c4hi", (128, 1))
    identb = p.inp("identb", (128, 128), BF16)
    identf = p.inp("identf", (128, 128), F32)
    y_o = p.outp("y_me", (4 * EC, D), BF16)
    buf_d = p.dscratch("buf_d", (4 * EC, D), F32)

    BIG = 1 << 22

    sc32 = p.sb("sc32", [32, 8], F32)
    offs = p.sb("offs", [32, 8], F32)
    t1c = p.sb("t1c", [128, 8, 16], F32)
    posc = p.sb("posc", [128, 8, 16], F32)
    iotE = p.sb("iotE", [128, 32], F32)
    ohb = p.sb("ohb", [128, 32], F32)
    ohbT = p.sb("ohbT", [32, 128], F32)
    offtok = p.sb("offtok", [128, 1], F32)
    m1 = p.sb("m1", [128, 1], F32)
    m2 = p.sb("m2", [128, 1], F32)
    slotc = p.sb("slotc", [128, 1], F32)
    idxi = p.sb("idxi", [128, 1], I32)
    xstg = p.sb("xstg", [128, 2, D], F32)
    zstg = p.sb("zstg", [128, D], F32)
    idb = p.sb("ident", [128, 128], BF16)
    idf = p.sb("identf_s", [128, 128], F32)
    bufb = p.sb("bufb", [128, D], BF16)
    bufT = p.sb("bufT", [128, NCH, EC], BF16)
    wgb = p.sb("wgb", [128, 8, D], BF16)
    wub = p.sb("wub", [128, 8, D], BF16)
    wdb = p.sb("wdb", [128, 8, D], BF16)
    gs = p.sb("gs", [128, 512], F32)
    us = p.sb("us", [128, 512], F32)
    hT = p.sb("hT", [128, 8, EC], BF16)
    ystg = p.sb("ystg", [128, D], BF16)
    pg = p.ps("pg", [128, 512], F32)
    pu = p.ps("pu", [128, 512], F32)
    ptf = p.ps("ptf", [128, 512], F32)
    ptb = p.ps("ptb", [128, 1024], BF16)

    ld(p, "sp", idb[:], identb[:], w=("ident",))
    ld(p, "sp", idf[:], identf[:], w=("identf",))
    ld(p, "sp", t1c[:, :, :], t1_i[:, :, :], w=("t1c",))
    ld(p, "sp", posc[:, :, :], pos_i[:, :, :], w=("posc",))
    cidk_s = p.sb("cidk_s", [128, 1], F32)
    c4lo_s = p.sb("c4lo_s", [128, 1], F32)
    c4hi_s = p.sb("c4hi_s", [128, 1], F32)
    ld(p, "sp", cidk_s[:, :], cidk[:, :], w=("cidk",))
    ld(p, "sp", c4lo_s[:, :], c4lo[:, :], w=("c4lo",))
    ld(p, "sp", c4hi_s[:, :], c4hi[:, :], w=("c4hi",))
    p.op("pool", lambda e: e.iota(iotE[:, :], [[1, 32]], base=0, channel_multiplier=0,
                                  allow_small_or_imprecise_dtypes=True),
         r=(), w=("iotE",))
    emit_offsets(p, cnt_i[:, :], offs, sc32)
    memset(p, zstg[:, :], 0.0, w=("zstg",))
    for q in range(4 * EC // 128):
        st(p, "sp", buf_d[q * 128:(q + 1) * 128, :], zstg[:, :], r=("zstg",), w=("buf_d",))
    xv = xnf_i[0:TGLOB, :].rearrange("(gt q) c -> gt q c", q=128)
    for g in range(8):
        for t_ in range(NT):
            gi = g * NT + t_
            sl = gi % 2
            tt(p, ohb[:, :], t1c[:, g, t_:t_ + 1].broadcast_to((128, 32)), iotE[:, :],
               OP.is_equal, r=("t1c", "iotE"), w=("ohb",))
            trans(p, [(ptf[0:32, 0:128], ohb[:, :])], idf,
                  r=("ohb", "identf"), w=("ptf2",))
            cp(p, "dve", ohbT[0:32, 0:128], ptf[0:32, 0:128], r=("ptf2",), w=("ohbT",))
            mm(p, [(ptf[0:128, 0:1], ohbT[0:32, 0:128], offs[0:32, g:g + 1], True, True)],
               r=("ohbT", "offs"), w=("ptf2",), tag=f"of{gi}")
            cp(p, "dve", offtok[:, 0:1], ptf[0:128, 0:1], r=("ptf2",), w=("offtok", "ptf2"))
            tt(p, offtok[:, 0:1], offtok[:, 0:1], posc[:, g, t_:t_ + 1], OP.add,
               r=("offtok", "posc"), w=("offtok",))
            ts(p, m1[:, 0:1], offtok[:, 0:1], f32(float(EC)), OP.is_lt,
               r=("offtok",), w=("m1",))
            ts(p, m2[:, 0:1], t1c[:, g, t_:t_ + 1], c4lo_s[:, 0:1], OP.is_gt,
               r=("t1c", "c4lo"), w=("m2",))
            tt(p, m1[:, 0:1], m1[:, 0:1], m2[:, 0:1], OP.mult, r=("m1", "m2"), w=("m1",))
            ts(p, m2[:, 0:1], t1c[:, g, t_:t_ + 1], c4hi_s[:, 0:1], OP.is_lt,
               r=("t1c", "c4hi"), w=("m2",))
            tt(p, m1[:, 0:1], m1[:, 0:1], m2[:, 0:1], OP.mult, r=("m1", "m2"), w=("m1",))
            stt(p, slotc[:, 0:1], t1c[:, g, t_:t_ + 1], f32(float(EC)), offtok[:, 0:1],
                OP.mult, OP.add, r=("t1c", "offtok"), w=("slotc",))
            ts(p, slotc[:, 0:1], slotc[:, 0:1], cidk_s[:, 0:1], OP.add,
               r=("slotc", "cidk"), w=("slotc",))
            tt(p, slotc[:, 0:1], slotc[:, 0:1], m1[:, 0:1], OP.mult,
               r=("slotc", "m1"), w=("slotc",))
            ts(p, slotc[:, 0:1], slotc[:, 0:1], f32(float(BIG)), OP.add,
               r=("slotc",), w=("slotc",))
            act(p, idxi[:, 0:1], slotc[:, 0:1], AF.Identity, r=("slotc",), w=("idxi",))
            ld(p, "sp", xstg[:, sl, :], xv[gi], w=(f"xstg{sl}",))
            p.dma("pool", lambda e, ii=idxi[:, 0:1], s=xstg[:, sl, :]:
                  e.indirect_dma_start(out=buf_d[:, :],
                                       out_offset=bass.IndirectOffsetOnAxis(ap=ii, axis=0),
                                       in_=s, in_offset=None, oob_is_err=False),
                  r=("idxi", f"xstg{sl}"), w=("buf_d",), tag=f"sc{gi}")
    for el in range(4):
        ld(p, "pool", wgb[:, :, :], wg_i[el].rearrange("(c p) f -> p c f", p=128), w=("wgb",))
        ld(p, "pool", wub[:, :, :], wu_i[el].rearrange("(c p) f -> p c f", p=128), w=("wub",))
        ld(p, "pool", wdb[:, :, :], wd_i[el].rearrange("(f p) o -> p f o", p=128), w=("wdb",))
        for rt in range(5):
            sl = rt % 2
            ld(p, "sp", xstg[:, sl, :], buf_d[el * EC + rt * 128: el * EC + (rt + 1) * 128, :],
               w=(f"xstg{sl}",), r=("buf_d",))
            cp(p, "dve", bufb[:, :], xstg[:, sl, :], r=(f"xstg{sl}",), w=("bufb",))
            trans(p, [(ptb[:, c * 128:(c + 1) * 128], bufb[:, c * 128:(c + 1) * 128])
                      for c in range(NCH)], idb, r=("bufb",), w=("ptb",))
            cp(p, "dve", bufT[:, :, rt * 128:(rt + 1) * 128],
               ptb[:, 0:1024].rearrange("p (c q) -> p c q", q=128),
               r=("ptb",), w=("bufT", "ptb"))
        for fc in range(NCH):
            for rc in range(2):
                w_ = 512 if rc == 0 else 128
                mm(p, [(pg[0:128, 0:w_], wgb[:, c, fc * 128:(fc + 1) * 128],
                        bufT[:, c, rc * 512:rc * 512 + w_], c == 0, c == 7)
                       for c in range(NCH)],
                   r=("wgb", "bufT"), w=("pg",), tag=f"g{el}.{fc}.{rc}")
                mm(p, [(pu[0:128, 0:w_], wub[:, c, fc * 128:(fc + 1) * 128],
                        bufT[:, c, rc * 512:rc * 512 + w_], c == 0, c == 7)
                       for c in range(NCH)],
                   r=("wub", "bufT"), w=("pu",), tag=f"u{el}.{fc}.{rc}")
                act(p, gs[:, 0:w_], pg[0:128, 0:w_], AF.Silu, r=("pg",), w=("gs", "pg"))
                cp(p, "dve", us[:, 0:w_], pu[0:128, 0:w_], r=("pu",), w=("us", "pu"))
                tt(p, us[:, 0:w_], gs[:, 0:w_], us[:, 0:w_], OP.mult,
                   r=("gs", "us"), w=("us",))
                cp(p, "dve", hT[:, fc, rc * 512:rc * 512 + w_], us[:, 0:w_],
                   r=("us",), w=("hT",))
        for rt in range(5):
            for oc in range(2):
                mm(p, [(pg[:, 0:512], hT[:, fc, rt * 128:(rt + 1) * 128],
                        wdb[:, fc, oc * 512:(oc + 1) * 512], fc == 0, fc == 7)
                       for fc in range(NCH)],
                   r=("hT", "wdb"), w=("pg",), tag=f"y{el}.{rt}.{oc}")
                cp(p, "dve", ystg[:, oc * 512:(oc + 1) * 512], pg[:, 0:512],
                   r=("pg",), w=("ystg", "pg"))
            st(p, "act", y_o[el * EC + rt * 128: el * EC + (rt + 1) * 128, :],
               ystg[:, :], r=("ystg",), w=("y_o",))
    return p


def build_li():
    p = Prog("li")
    y_i = p.inp("y_full", (32 * EC + 1, D), BF16)
    cnt_i = p.inp("cnts", (32, 8))
    t1_i = p.inp("t1", (128, 16))
    pos_i = p.inp("posn", (128, 16))
    gate_i = p.inp("gaten", (128, 16))
    x3_i = p.inp("x3", (T, D)); at3_i = p.inp("at3", (T, D)); xn3_i = p.inp("xn3", (T, D))
    cido = p.inp("cido", (32, 8))
    identf = p.inp("identf", (128, 128), F32)
    out_o = p.outp("out", (T, D), F32)

    BIGY = 32 * EC
    sc32 = p.sb("sc32", [32, 8], F32)
    offs = p.sb("offs", [32, 8], F32)
    offc = p.sb("offc", [32, 1], F32)
    cidob = p.sb("cidob", [32, 8], F32)
    t1c = p.sb("t1c", [128, 16], F32)
    posc = p.sb("posc", [128, 16], F32)
    gatec = p.sb("gatec", [128, 16], F32)
    iotE = p.sb("iotE", [128, 32], F32)
    ohb = p.sb("ohb", [128, 32], F32)
    ohbT = p.sb("ohbT", [32, 128], F32)
    offtok = p.sb("offtok", [128, 1], F32)
    m1 = p.sb("m1", [128, 1], F32)
    slotc = p.sb("slotc", [128, 1], F32)
    idxi = p.sb("idxi", [128, 1], I32)
    ygat = p.sb("ygat", [128, D], BF16)
    s1 = p.sb("s1", [128, 2, D], F32)
    s2 = p.sb("s2", [128, 2, D], F32)
    idf = p.sb("identf_s", [128, 128], F32)
    ptf = p.ps("ptf", [128, 512], F32)

    ld(p, "sp", idf[:], identf[:], w=("identf",))
    ld(p, "sp", t1c[:, :], t1_i[:, :], w=("t1c",))
    ld(p, "sp", posc[:, :], pos_i[:, :], w=("posc",))
    ld(p, "sp", gatec[:, :], gate_i[:, :], w=("gatec",))
    ld(p, "sp", cidob[:, :], cido[:, :], w=("cido",))
    p.op("pool", lambda e: e.iota(iotE[:, :], [[1, 32]], base=0, channel_multiplier=0,
                                  allow_small_or_imprecise_dtypes=True),
         r=(), w=("iotE",))
    emit_offsets(p, cnt_i[:, :], offs, sc32)
    tt(p, sc32[0:32, 0:8], offs[0:32, 0:8], cidob[0:32, 0:8], OP.mult,
       r=("offs", "cido"), w=("sc32",))
    red(p, offc[0:32, 0:1], sc32[0:32, 0:8], OP.add, r=("sc32",), w=("offc",))
    x3v = x3_i.rearrange("(tt q) c -> tt q c", q=128)
    a3v = at3_i.rearrange("(tt q) c -> tt q c", q=128)
    n3v = xn3_i.rearrange("(tt q) c -> tt q c", q=128)
    ov = out_o.rearrange("(tt q) c -> tt q c", q=128)
    for t_ in range(NT):
        sl = t_ % 2
        tt(p, ohb[:, :], t1c[:, t_:t_ + 1].broadcast_to((128, 32)), iotE[:, :],
           OP.is_equal, r=("t1c", "iotE"), w=("ohb",))
        trans(p, [(ptf[0:32, 0:128], ohb[:, :])], idf, r=("ohb", "identf"), w=("ptf2",))
        cp(p, "dve", ohbT[0:32, 0:128], ptf[0:32, 0:128], r=("ptf2",), w=("ohbT",))
        mm(p, [(ptf[0:128, 0:1], ohbT[0:32, 0:128], offc[0:32, 0:1], True, True)],
           r=("ohbT", "offc"), w=("ptf2",), tag=f"of{t_}")
        cp(p, "dve", offtok[:, 0:1], ptf[0:128, 0:1], r=("ptf2",), w=("offtok", "ptf2"))
        tt(p, offtok[:, 0:1], offtok[:, 0:1], posc[:, t_:t_ + 1], OP.add,
           r=("offtok", "posc"), w=("offtok",))
        ts(p, m1[:, 0:1], offtok[:, 0:1], f32(float(EC)), OP.is_lt,
           r=("offtok",), w=("m1",))
        stt(p, slotc[:, 0:1], t1c[:, t_:t_ + 1], f32(float(EC)), offtok[:, 0:1],
            OP.mult, OP.add, r=("t1c", "offtok"), w=("slotc",))
        ts(p, slotc[:, 0:1], slotc[:, 0:1], f32(float(-BIGY)), OP.add,
           r=("slotc",), w=("slotc",))
        tt(p, slotc[:, 0:1], slotc[:, 0:1], m1[:, 0:1], OP.mult,
           r=("slotc", "m1"), w=("slotc",))
        ts(p, slotc[:, 0:1], slotc[:, 0:1], f32(float(BIGY)), OP.add,
           r=("slotc",), w=("slotc",))
        act(p, idxi[:, 0:1], slotc[:, 0:1], AF.Identity, r=("slotc",), w=("idxi",))
        p.dma("pool", lambda e, ii=idxi[:, 0:1], d=ygat[:, :]:
              e.indirect_dma_start(out=d, out_offset=None, in_=y_i[:, :],
                                   in_offset=bass.IndirectOffsetOnAxis(ap=ii, axis=0),
                                   oob_is_err=False),
              r=("idxi",), w=("ygat",), tag=f"gat{t_}")
        ld(p, "sp", s1[:, sl, :], n3v[t_], w=(f"s1{sl}",))
        ts(p, s2[:, sl, :], ygat[:, :], gatec[:, t_:t_ + 1], OP.mult,
           r=("ygat", "gatec"), w=(f"s2{sl}",))
        tt(p, s1[:, sl, :], s1[:, sl, :], s2[:, sl, :], OP.add,
           r=(f"s1{sl}", f"s2{sl}"), w=(f"s1{sl}",))
        ld(p, "sp", s2[:, sl, :], x3v[t_], w=(f"s2{sl}",))
        tt(p, s1[:, sl, :], s1[:, sl, :], s2[:, sl, :], OP.add,
           r=(f"s1{sl}", f"s2{sl}"), w=(f"s1{sl}",))
        ld(p, "sp", s2[:, sl, :], a3v[t_], w=(f"s2{sl}",))
        tt(p, s1[:, sl, :], s1[:, sl, :], s2[:, sl, :], OP.add,
           r=(f"s1{sl}", f"s2{sl}"), w=(f"s1{sl}",))
        st(p, "act", ov[t_], s1[:, sl, :], r=(f"s1{sl}",), w=("out_o",))
    return p


def stage_lg(x2, at2, ff2, L3a, L3f):
    bc = lambda v: np.ascontiguousarray(np.broadcast_to(v, (128, D)), np.float32)
    return {
        "x2": np.ascontiguousarray(x2, np.float32),
        "at2": np.ascontiguousarray(at2, np.float32),
        "ff2": np.ascontiguousarray(ff2, np.float32),
        "wqT": np.ascontiguousarray(L3a["Wq"].T, np.float32),
        "wkT0": np.ascontiguousarray(L3a["Wk"][0].T, np.float32),
        "wkT1": np.ascontiguousarray(L3a["Wk"][1].T, np.float32),
        "wvT0": np.ascontiguousarray(L3a["Wv"][0].T, np.float32),
        "wvT1": np.ascontiguousarray(L3a["Wv"][1].T, np.float32),
        "woT": np.ascontiguousarray(L3a["Wo"].T, np.float32),
        "bq_bc": bc(L3a["bq"]),
        "bk_bc0": bc(L3a["bk"][0]), "bk_bc1": bc(L3a["bk"][1]),
        "bv_bc0": bc(L3a["bv"][0]), "bv_bc1": bc(L3a["bv"][1]),
        "bo_bc": bc(L3a["bo"]),
        "lng_bc": bc(L3a["ln_g"]), "lnb_bc": bc(L3a["ln_b"]),
        "wrout": np.ascontiguousarray(L3f["w_router"], np.float32),
        "iotaE": np.arange(128, dtype=np.float32).reshape(128, 1),
        "identb": np.eye(128, dtype=ml_dtypes.bfloat16),
        "identf": np.eye(128, dtype=np.float32),
    }


def stage_lh(c, xn3_all, cnts, t1_all, pos_all, L3f):
    BIG = 1 << 22
    xnf = np.zeros((TGLOB + 1, D), np.float32)
    xnf[0:TGLOB] = xn3_all
    return {
        "xnf": xnf, "cnts": np.ascontiguousarray(cnts, np.float32),
        "t1g": np.ascontiguousarray(t1_all.transpose(1, 0, 2), np.float32),
        "posg": np.ascontiguousarray(pos_all.transpose(1, 0, 2), np.float32),
        "wg": np.ascontiguousarray(L3f["w_gate"][4 * c:4 * c + 4], np.float32),
        "wu": np.ascontiguousarray(L3f["w_up"][4 * c:4 * c + 4], np.float32),
        "wd": np.ascontiguousarray(L3f["w_down"][4 * c:4 * c + 4], np.float32),
        "cidk": np.full((128, 1), -2560.0 * c - BIG, np.float32),
        "c4lo": np.full((128, 1), 4 * c - 0.5, np.float32),
        "c4hi": np.full((128, 1), 4 * c + 3.5, np.float32),
        "identb": np.eye(128, dtype=ml_dtypes.bfloat16),
        "identf": np.eye(128, dtype=np.float32),
    }


def stage_li(c, y_all, cnts, t1, posn, gaten, x3, at3, xn3):
    yf = np.zeros((32 * EC + 1, D), ml_dtypes.bfloat16)
    yf[0:32 * EC] = y_all
    oh = np.zeros((1, 8), np.float32); oh[0, c] = 1.0
    return {
        "y_full": yf, "cnts": np.ascontiguousarray(cnts, np.float32),
        "t1": np.ascontiguousarray(t1, np.float32),
        "posn": np.ascontiguousarray(posn, np.float32),
        "gaten": np.ascontiguousarray(gaten, np.float32),
        "x3": np.ascontiguousarray(x3, np.float32),
        "at3": np.ascontiguousarray(at3, np.float32),
        "xn3": np.ascontiguousarray(xn3, np.float32),
        "cido": np.ascontiguousarray(np.broadcast_to(oh, (32, 8)), np.float32),
        "identf": np.eye(128, dtype=np.float32),
    }


def _pos_row_to_tok(row):
    return np.ascontiguousarray(row.reshape(16, 128).T, np.float32)


def build_ln2():
    p = Prog("ln2")
    x_i = p.inp("x", (T, D))
    identf = p.inp("identf", (128, 128), F32)
    xn_o = p.outp("xn_o", (NT, 128, D), F32)
    xnT_o = p.outp("xnT_o", (NCH, 128, T), BF16)
    alloc_ln(p, "l.")
    idf = p.sb("identf_s", [128, 128], F32)
    xnT = p.sb("xnT", [128, NCH, T], BF16)
    pq = p.ps("pq", [128, 512], F32)
    ld(p, "sp", idf[:], identf[:], w=("identf",))
    emit_ln_from_hbm_rot(p, x_i, "l.", xnT, pq, None, xn_store_f32=xn_o)
    for c in range(NCH):
        st(p, "act", xnT_o[c], xnT[:, c, :], r=XNT_KEYS, w=("xnT_o",))
    return p



def _r8(build, stagers):
    p = build()
    res = p.run(stagers)
    return res.results


def kernel(**inp):
    X = np.ascontiguousarray(np.asarray(inp["L0_attn_x"]), np.float32)
    L0a = {k.replace("L0_attn_", ""): np.asarray(v) for k, v in inp.items()
           if k.startswith("L0_attn_") and k != "L0_attn_x"}
    L0f = {k.replace("L0_ffn_", ""): np.asarray(v) for k, v in inp.items() if k.startswith("L0_ffn_")}
    L1 = {k.replace("L1_attn_", ""): np.asarray(v) for k, v in inp.items() if k.startswith("L1_attn_")}
    L1f = {k.replace("L1_ffn_", ""): np.asarray(v) for k, v in inp.items() if k.startswith("L1_ffn_")}
    L2a = {k.replace("L2_attn_", ""): np.asarray(v) for k, v in inp.items() if k.startswith("L2_attn_")}
    L2f = {k.replace("L2_ffn_", ""): np.asarray(v) for k, v in inp.items() if k.startswith("L2_ffn_")}
    L3a = {k.replace("L3_attn_", ""): np.asarray(v) for k, v in inp.items() if k.startswith("L3_attn_")}
    L3f = {k.replace("L3_ffn_", ""): np.asarray(v) for k, v in inp.items() if k.startswith("L3_ffn_")}

    ra = _r8(build_la, [stage_la(X[c], L0a) for c in range(NC)])
    m8 = np.array([float(ra[c]["mx"][0, 0]) for c in range(NC)], np.float32)
    rb = _r8(build_lb, [stage_lb(np.asarray(ra[c]["xnT_o"]), m8, L0f) for c in range(NC)])
    mup8 = np.array([float(rb[c]["mup"][0, 0]) for c in range(NC)], np.float32)
    mgt8 = np.array([float(rb[c]["mgt"][0, 0]) for c in range(NC)], np.float32)
    rc = _r8(build_lc, [stage_lc(np.asarray(rb[c]["up_o"]), np.asarray(rb[c]["gt_o"]),
                                 mup8, mgt8) for c in range(NC)])
    mg28 = np.array([float(rc[c]["mg2"][0, 0]) for c in range(NC)], np.float32)
    rd = _r8(build_ld, [stage_ld(np.asarray(rc[c]["g2_o"]), np.asarray(rc[c]["upq_o"]),
                                 mg28) for c in range(NC)])
    mo8 = np.array([float(rd[c]["mo"][0, 0]) for c in range(NC)], np.float32)
    re_ = _r8(build_le, [stage_le(np.asarray(rd[c]["or_o"]), mo8, L0f, X[c],
                                  np.asarray(ra[c]["a0"])) for c in range(NC)])
    X1 = [np.asarray(re_[c]["x1"]) for c in range(NC)]

    r1a = _r8(build_l1a, [stage_l1a(X1[c], L1) for c in range(NC)])
    r1f = _r8(build_l1f, [stage_l1f(X1[c], np.asarray(r1a[c]["a1"]),
                                    np.asarray(r1a[c]["xn_o"]), L1f) for c in range(NC)])
    X2 = [np.asarray(r1f[c]["x2"]) for c in range(NC)]

    rl = _r8(build_ln2, [{"x": X2[c], "identf": np.eye(128, dtype=np.float32)}
                         for c in range(NC)])
    xn2 = np.stack([np.asarray(rl[c]["xn_o"]).reshape(T, D)
                    for c in range(NC)], 0)
    xn2T = [np.asarray(rl[c]["xnT_o"]) for c in range(NC)]

    lf1_maps = []
    for j in range(NC):
        h_bsz = np.ascontiguousarray(
            xn2[:, j * 256:(j + 1) * 256, :].reshape(T, D), np.float32)
        hT = np.empty((NCH, 128, T), ml_dtypes.bfloat16)
        for s in range(8):
            hT[:, :, s * 256:(s + 1) * 256] = xn2T[s][:, :, j * 256:(j + 1) * 256]
        lf1_maps.append(stage_lf1(np.ascontiguousarray(hT), h_bsz, L2a))
    rf1 = _r8(build_lf1, lf1_maps)
    at2g = np.empty((8, 2048, D), np.float32)
    for j in range(NC):
        at2g[:, j * 256:(j + 1) * 256, :] = np.asarray(rf1[j]["at2"]).reshape(8, 256, D)

    rf2 = _r8(build_lf2, [stage_lf2(xn2[c], L2f) for c in range(NC)])

    rg = _r8(build_lg, [stage_lg(X2[c], at2g[c], np.asarray(rf2[c]["ff2"]), L3a, L3f)
                        for c in range(NC)])
    xn3_all = np.concatenate([np.asarray(rg[c]["xn3"]) for c in range(NC)], 0)
    cnts = np.stack([np.asarray(rg[c]["cnt"])[0] for c in range(NC)], 1)
    t1_all = np.stack([np.asarray(rg[c]["top1"]) for c in range(NC)], 0)
    pos_all = np.stack([_pos_row_to_tok(np.asarray(rg[c]["pos"])) for c in range(NC)], 0)

    rh = _r8(build_lh, [stage_lh(c, xn3_all, cnts, t1_all, pos_all, L3f)
                        for c in range(NC)])
    y_all = np.concatenate([np.asarray(rh[c]["y_me"]) for c in range(NC)], 0)
    ri = _r8(build_li, [stage_li(c, y_all, cnts, t1_all[c], pos_all[c],
                                 np.asarray(rg[c]["gate"]), np.asarray(rg[c]["x3"]),
                                 np.asarray(rg[c]["at3"]), np.asarray(rg[c]["xn3"]))
                        for c in range(NC)])
    out = np.stack([np.asarray(ri[c]["out"], np.float32) for c in range(NC)], 0)
    return out
